# version 01
# baseline (speedup 1.0000x reference)
import math
import numpy as np
import ml_dtypes
import concourse.bass as bass
import concourse.mybir as mybir
from concourse.bass_utils import run_bass_kernel_spmd
from concourse.alu_op_type import AluOpType as OP

AF = mybir.ActivationFunctionType
AX = mybir.AxisListType
F32 = mybir.dt.float32
BF = mybir.dt.bfloat16
I32 = mybir.dt.int32
BFNP = ml_dtypes.bfloat16

B, T, D = 8, 2048, 1024
NT = T // 128


def _ap(x):
    return x if isinstance(x, bass.AP) else x[:]


class Prog:
    def __init__(self, nc, nsem=4):
        self.nc = nc
        self.nsem = nsem
        self.stages = []
        self.cur = []

    def op(self, eng, fn, inc=1):
        self.cur.append((eng, fn, inc))

    def dma(self, out, in_, eng="sync"):
        out = _ap(out); in_ = _ap(in_)
        self.op(eng, lambda e, o=out, i=in_: e.dma_start(out=o, in_=i), 16)

    def bar(self):
        if self.cur:
            self.stages.append(self.cur)
            self.cur = []

    def finalize(self, sems):
        self.bar()
        R = len(sems)
        cum = [0] * R
        tgt = []
        for s, st in enumerate(self.stages):
            k = s % R
            cum[k] += sum(i for (_, _, i) in st)
            tgt.append((k, cum[k]))

        def emit(eng_obj, name):
            for s, st in enumerate(self.stages):
                ops = [(f, i) for (e, f, i) in st if e == name]
                if not ops:
                    continue
                if s > 0:
                    k, v = tgt[s - 1]
                    eng_obj.wait_ge(sems[k], v)
                for f, i in ops:
                    inst = f(eng_obj)
                    inst.then_inc(sems[s % R], i)

        nc = self.nc
        with nc.Block() as blk:
            @blk.sync
            def _(e):
                emit(e, "sync")
                k, v = tgt[-1]
                e.wait_ge(sems[k], v)

            @blk.tensor
            def _(e):
                emit(e, "tensor")

            @blk.vector
            def _(e):
                emit(e, "vector")

            @blk.scalar
            def _(e):
                emit(e, "scalar")

            @blk.gpsimd
            def _(e):
                emit(e, "gpsimd")


class Tall:

    def __init__(self, tiles):
        self.t = tiles

    def tile(self, i):
        return self.t[i]

    def __len__(self):
        return len(self.t)


def alloc_tall(stk, nc, name, rows, cols, dtype):
    n = (rows + 127) // 128
    tiles = []
    for i in range(n):
        r = min(128, rows - 128 * i)
        h = stk.enter_context(nc.sbuf_tensor(f"{name}_{i}", [r, cols], dtype))
        tiles.append(h)
    return Tall(tiles)



class Ctx:
    pass


def mk_ctx(stk, nc, p):
    c = Ctx()
    c.nc = nc
    c.p = p
    c.stk = stk
    c.PT = [stk.enter_context(nc.psum_tensor(f"pt{i}", [128, 512], F32)) for i in range(2)]
    c.PM = [stk.enter_context(nc.psum_tensor(f"pm{i}", [128, 512], F32)) for i in range(4)]
    c.PR = stk.enter_context(nc.psum_tensor("pr", [128, 512], F32))
    c.PS = stk.enter_context(nc.psum_tensor("ps", [128, 512], F32))
    c.IDENT = stk.enter_context(nc.sbuf_tensor("ident", [128, 128], F32))
    c.ONES1 = stk.enter_context(nc.sbuf_tensor("ones1", [1, 128], F32))
    c.ONES1B = stk.enter_context(nc.sbuf_tensor("ones1b", [1, 128], BF))
    c.ONESCB = stk.enter_context(nc.sbuf_tensor("onescb", [128, 1], BF))
    ident = nc.dram_tensor("c_ident", [128, 128], F32, kind="ExternalInput").ap()
    ones1 = nc.dram_tensor("c_ones1", [1, 128], F32, kind="ExternalInput").ap()
    ones1b = nc.dram_tensor("c_ones1b", [1, 128], BF, kind="ExternalInput").ap()
    onescb = nc.dram_tensor("c_onescb", [128, 1], BF, kind="ExternalInput").ap()
    p.dma(c.IDENT[:], ident)
    p.dma(c.ONES1[:], ones1)
    p.dma(c.ONES1B[:], ones1b)
    p.dma(c.ONESCB[:], onescb)
    c.CC = stk.enter_context(nc.sbuf_tensor("cc", [128, 4], F32))
    p.op("vector", lambda e: e.memset(c.CC[:, 0:1], 1e-5))
    p.bar()
    c.EPS = c.CC[:, 0:1]
    return c


def host_consts():
    return {
        "c_ident": np.eye(128, dtype=np.float32),
        "c_ones1": np.ones((1, 128), np.float32),
        "c_ones1b": np.ones((1, 128), BFNP),
        "c_onescb": np.ones((128, 1), BFNP),
    }


def mm(c, out, items, start=True, stop=True):
    out = _ap(out)
    items = [(_ap(l), _ap(r)) for (l, r) in items]
    def fn(e, out=out, items=list(items)):
        n = len(items)
        inst = None
        for i, (l, r) in enumerate(items):
            inst = e.matmul(out, l, r, start=(start and i == 0), stop=(stop and i == n - 1))
        return inst
    c.p.op("tensor", fn)


def transpose_round(c, items):
    for g0 in range(0, len(items), 8):
        grp = items[g0:g0 + 8]
        for i, it in enumerate(grp):
            src = it[0]
            a, bb = src.shape
            slot = c.PT[i // 4][0:bb, 128 * (i % 4):128 * (i % 4) + a]
            c.p.op("tensor", lambda e, s=src, o=slot, aa=a: e.transpose(o, s, c.IDENT[0:aa, 0:aa]))
        c.p.bar()
        for i, it in enumerate(grp):
            src, dst = it[0], it[1]
            func = it[2] if len(it) > 2 else AF.Copy
            scale = it[3] if len(it) > 3 else 1.0
            a, bb = src.shape
            slot = c.PT[i // 4][0:bb, 128 * (i % 4):128 * (i % 4) + a]
            c.p.op("scalar", lambda e, o=dst, s=slot, f=func, sc=scale: e.activation(o, s, f, scale=sc))
        c.p.bar()


def evac(c, dst, src, func=AF.Copy, scale=1.0, bias=...):
    dst = _ap(dst); src = _ap(src)
    c.p.op("scalar", lambda e, o=dst, s=src, f=func, sc=scale, b=bias:
           e.activation(o, s, f, scale=sc) if b is ... else e.activation(o, s, f, scale=sc, bias=b))


def vec_tt(c, out, in0, in1, op):
    out = _ap(out); in0 = _ap(in0); in1 = _ap(in1)
    c.p.op("vector", lambda e, o=out, a=in0, b=in1, oo=op: e.tensor_tensor(o, a, b, oo))


def vec_ts(c, out, in0, s1, op0, s2=None, op1=None):
    out = _ap(out); in0 = _ap(in0)
    s1 = _ap(s1) if not isinstance(s1, (int, float)) else s1
    if s2 is None:
        c.p.op("vector", lambda e, o=out, a=in0, s=s1, oo=op0: e.tensor_scalar(o, a, s, None, oo))
    else:
        c.p.op("vector", lambda e, o=out, a=in0, sa=s1, sb=s2, o0=op0, o1=op1:
               e.tensor_scalar(o, a, sa, sb, o0, o1))


def vec_stt(c, out, in0, scalar, in1, op0, op1):
    out = _ap(out); in0 = _ap(in0); in1 = _ap(in1)
    scalar = _ap(scalar) if not isinstance(scalar, (int, float)) else scalar
    c.p.op("vector", lambda e, o=out, a=in0, s=scalar, b=in1, o0=op0, o1=op1:
           e.scalar_tensor_tensor(o, a, s, b, o0, o1))


def vec_red(c, out, in_, op):
    out = _ap(out); in_ = _ap(in_)
    c.p.op("vector", lambda e, o=out, i=in_, oo=op: e.tensor_reduce(o, i, AX.X, oo))


def act(c, out, in_, func, scale=1.0, bias=...):
    out = _ap(out); in_ = _ap(in_)
    if bias is not ... and not isinstance(bias, (int, float)):
        bias = _ap(bias)
    c.p.op("scalar", lambda e, o=out, i=in_, f=func, sc=scale, b=bias:
           e.activation(o, i, f, scale=sc) if b is ... else e.activation(o, i, f, scale=sc, bias=b))


def bcast_row(c, dst, row, n=128, lhs=None):
    F = row.shape[1]
    lhsT = (lhs if lhs is not None else c.ONES1)[0:1, 0:n]
    mm(c, c.PR[0:n, 0:F], [(lhsT, row)])
    c.p.bar()
    evac(c, dst, c.PR[0:n, 0:F])
    c.p.bar()


def layernorm_tile(c, xn_out, x_in, tmp, eps=None):
    Dd = x_in.shape[1]
    vec_red(c, tmp["r1"][:, 0:1], x_in, OP.add)
    c.p.bar()
    c.p.op("scalar", lambda e, o=tmp["r2"][:, 0:1], i=tmp["r1"][:, 0:1]: e.mul(o, i, -1.0 / Dd))
    c.p.bar()
    vec_ts(c, tmp["a"], x_in, tmp["r2"][:, 0:1], OP.add)
    c.p.bar()
    act(c, tmp["b"], tmp["a"], AF.Square)
    c.p.bar()
    vec_red(c, tmp["r1"][:, 0:1], tmp["b"], OP.add)
    c.p.bar()
    act(c, tmp["r2"][:, 0:1], tmp["r1"][:, 0:1], AF.Sqrt, scale=1.0 / Dd, bias=c.EPS)
    c.p.bar()
    c.p.op("vector", lambda e, o=tmp["r1"][:, 0:1], i=tmp["r2"][:, 0:1]: e.reciprocal(o, i))
    c.p.bar()
    vec_ts(c, xn_out, tmp["a"], tmp["r1"][:, 0:1], OP.mult)
    c.p.bar()


def fp4_quant(c, q_out_f32, x_in, tmp):
    F = x_in.shape[1]
    nb = F // 16
    absx = tmp["a"]
    act(c, absx[:, 0:F], x_in, AF.Abs)
    c.p.bar()
    a3 = absx[:, 0:F].rearrange("p (b k) -> p b k", k=16)
    vec_red(c, tmp["s"][:, 0:nb], a3, OP.max)
    c.p.bar()
    vec_ts(c, tmp["s2"][:, 0:nb], tmp["s"][:, 0:nb], 1.0 / 6.0, OP.mult, 1e-12, OP.max)
    c.p.bar()
    c.p.op("vector", lambda e, o=tmp["s"][:, 0:nb], i=tmp["s2"][:, 0:nb]: e.reciprocal(o, i))
    c.p.bar()
    recb = tmp["s"][:, 0:nb].rearrange("p (b o) -> p b o", o=1).broadcast_to((128, nb, 16))
    scb = tmp["s2"][:, 0:nb].rearrange("p (b o) -> p b o", o=1).broadcast_to((128, nb, 16))
    tq = tmp["b"]
    vec_tt(c, tq[:, 0:F].rearrange("p (b k) -> p b k", k=16), a3, recb, OP.mult)
    c.p.bar()
    accs = [tmp["c"], tmp["d"]]
    vec_ts(c, accs[0][:, 0:F], tq[:, 0:F], 0.25, OP.is_gt, 0.5, OP.mult)
    c.p.bar()
    cur = 0
    for mid, w in [(0.75, 0.5), (1.25, 0.5), (1.75, 0.5), (2.5, 1.0), (3.5, 1.0), (5.0, 2.0)]:
        vec_ts(c, tmp["e"][:, 0:F], tq[:, 0:F], mid, OP.is_gt, w, OP.mult)
        c.p.bar()
        vec_tt(c, accs[1 - cur][:, 0:F], accs[cur][:, 0:F], tmp["e"][:, 0:F], OP.add)
        c.p.bar()
        cur = 1 - cur
    vec_tt(c, tmp["e"][:, 0:F].rearrange("p (b k) -> p b k", k=16),
           accs[cur][:, 0:F].rearrange("p (b k) -> p b k", k=16), scb, OP.mult)
    c.p.op("scalar", lambda e, o=tmp["b"][:, 0:F], i=x_in: e.sign(o, i))
    c.p.bar()
    vec_tt(c, q_out_f32, tmp["e"][:, 0:F], tmp["b"][:, 0:F], OP.mult)
    c.p.bar()


def mk_tmp(stk, nc, name, F=1024):
    t = {}
    for k in ["a", "b", "c", "d", "e"]:
        t[k] = stk.enter_context(nc.sbuf_tensor(f"{name}_{k}", [128, F], F32))
    t["s"] = stk.enter_context(nc.sbuf_tensor(f"{name}_s", [128, 256], F32))
    t["s2"] = stk.enter_context(nc.sbuf_tensor(f"{name}_s2", [128, 256], F32))
    t["r1"] = stk.enter_context(nc.sbuf_tensor(f"{name}_r1", [128, 8], F32))
    t["r2"] = stk.enter_context(nc.sbuf_tensor(f"{name}_r2", [128, 8], F32))
    return t




import contextlib


def phase1(c, x_in, tmp, xnTb=None, l0=None, l2=None):
    nc, p = c.nc, c.p
    for tt in range(NT):
        p.dma(c.xtile, x_in[tt * 128:(tt + 1) * 128, :])
        p.bar()
        layernorm_tile(c, c.xn[:], c.xtile[:], tmp)
        items = []
        if xnTb is not None:
            for k in range(8):
                items.append((c.xn[:, 128 * k:128 * (k + 1)],
                              [(xnTb.tile(k)[:, tt * 128:(tt + 1) * 128], AF.Copy, 1.0)]))
        if l2 is not None:
            for k in range(8):
                items.append((c.xn[:, 128 * k:128 * (k + 1)],
                              [(l2["strip"].tile(k)[:], AF.Copy, 1.0)]))
        if l0 is not None:
            p.dma(l0["scr_xn"][tt * 128:(tt + 1) * 128, :], c.xn[:])
        transpose_round2(c, items)
        if l0 is not None:
            fp4_quant(c, c.xq[:], c.xn[:], tmp)
            qitems = [(c.xq[:, 128 * k:128 * (k + 1)],
                       [(l0["xqT"].tile(k)[:, tt * 128:(tt + 1) * 128], AF.Copy, 1.0)])
                      for k in range(8)]
            transpose_round2(c, qitems)
        if l2 is not None:
            mm(c, c.PS[0:128, 0:8],
               [(l2["strip"].tile(pi)[:], l2["gwT"].tile(pi)[:]) for pi in range(8)])
            p.bar()
            evac(c, l2["lg"][:, 0:8], c.PS[0:128, 0:8])
            p.bar()
            lg = l2["lg"]
            r1, r2 = tmp["r1"], tmp["r2"]
            vec_red(c, r1[:, 0:1], lg[:, 0:8], OP.max)
            p.bar()
            vec_ts(c, l2["msk1"][:, 0:8], lg[:, 0:8], r1[:, 0:1], OP.is_ge)
            p.bar()
            vec_stt(c, l2["l2nd"][:, 0:8], l2["msk1"][:, 0:8], -1e30, lg[:, 0:8], OP.mult, OP.add)
            p.bar()
            vec_red(c, r2[:, 0:1], l2["l2nd"][:, 0:8], OP.max)
            p.bar()
            vec_ts(c, l2["msk2"][:, 0:8], l2["l2nd"][:, 0:8], r2[:, 0:1], OP.is_ge)
            vec_tt(c, r2[:, 1:2], r2[:, 0:1], r1[:, 0:1], OP.subtract)
            p.bar()
            act(c, r2[:, 2:3], r2[:, 1:2], AF.Exp)
            p.bar()
            c.p.op("scalar", lambda e, o=r2[:, 3:4], i=r2[:, 2:3]: e.add(o, i, 1.0))
            p.bar()
            c.p.op("vector", lambda e, o=r1[:, 1:2], i=r2[:, 3:4]: e.reciprocal(o, i))
            p.bar()
            vec_ts(c, l2["msk2"][:, 0:8], l2["msk2"][:, 0:8], r2[:, 2:3], OP.mult)
            p.bar()
            vec_tt(c, l2["msk1"][:, 0:8], l2["msk1"][:, 0:8], l2["msk2"][:, 0:8], OP.add)
            p.bar()
            vec_ts(c, l2["comb"][tt][:, 0:8], l2["msk1"][:, 0:8], r1[:, 1:2], OP.mult)
            p.bar()
        p.bar()


def transpose_round2(c, items):
    for g0 in range(0, len(items), 8):
        grp = items[g0:g0 + 8]
        for i, (src, dsts) in enumerate(grp):
            a, bb = src.shape
            slot = c.PT[i // 4][0:bb, 128 * (i % 4):128 * (i % 4) + a]
            c.p.op("tensor", lambda e, s=src, o=slot, aa=a: e.transpose(o, s, c.IDENT[0:aa, 0:aa]))
        c.p.bar()
        for i, (src, dsts) in enumerate(grp):
            a, bb = src.shape
            slot = c.PT[i // 4][0:bb, 128 * (i % 4):128 * (i % 4) + a]
            for (dst, func, scale) in dsts:
                c.p.op("scalar", lambda e, o=_ap(dst), ss=slot, f=func, sc=scale: e.activation(o, ss, f, scale=sc))
        c.p.bar()


def update_phase(c, x_in, x_out, scr_s, g, dbg=None):
    p = c.p
    for tt in range(NT):
        sl = slice(tt * 128, (tt + 1) * 128)
        p.dma(c.xtile, x_in[sl, :])
        p.dma(c.xn, scr_s[sl, :])
        p.bar()
        if dbg is not None:
            p.dma(dbg["o_s"][sl, :], c.xn[:])
            evac(c, c.xq[:], g.tile(tt)[:])
            p.bar()
            p.dma(dbg["o_g"][sl, :], c.xq[:])
            p.bar()
        vec_tt(c, c.xq[:], c.xn[:], g.tile(tt)[:], OP.mult)
        p.bar()
        vec_tt(c, c.xn[:], c.xtile[:], c.xq[:], OP.add)
        p.bar()
        p.dma(x_out[sl, :], c.xn[:])
        p.bar()


def cast_load(c, stage, dst_tall, src_dram, rows, cols, scale=1.0, col0=0, row0=0):
    n = (rows + 127) // 128
    for i in range(n):
        r = min(128, rows - 128 * i)
        c.p.dma(stage[0:r, 0:cols], src_dram[row0 + 128 * i:row0 + 128 * i + r, col0:col0 + cols])
        c.p.bar()
        evac(c, dst_tall.tile(i)[0:r, 0:cols], stage[0:r, 0:cols], scale=scale)
        c.p.bar()


def build_l0(dbg=False):
    nc = bass.Bass("TRN2", target_bir_lowering=False)
    x_in = nc.dram_tensor("xin", [T, D], F32, kind="ExternalInput").ap()
    x_out = nc.dram_tensor("xo", [T, D], F32, kind="ExternalOutput").ap()
    dec = nc.dram_tensor("w_dec", [1, 1], F32, kind="ExternalInput").ap()
    WT = nc.dram_tensor("w_WT", [D, D], F32, kind="ExternalInput").ap()
    bb = nc.dram_tensor("w_b", [1, D], F32, kind="ExternalInput").ap()
    gam = nc.dram_tensor("w_gam", [1, D], F32, kind="ExternalInput").ap()
    bet = nc.dram_tensor("w_bet", [1, D], F32, kind="ExternalInput").ap()
    wg = nc.dram_tensor("w_wg", [2816, D], F32, kind="ExternalInput").ap()
    wu = nc.dram_tensor("w_wu", [2816, D], F32, kind="ExternalInput").ap()
    wd = nc.dram_tensor("w_wd", [D, 2816], F32, kind="ExternalInput").ap()
    scr_xn = nc.dram_tensor("scr_xn", [T, D], F32, kind="Internal").ap()
    scr_s = nc.dram_tensor("scr_s", [T, D], F32, kind="Internal").ap()
    scr_hq = nc.dram_tensor("scr_hq", [2816, T], BF, kind="Internal").ap()
    dbgd = None
    if dbg:
        dbgd = {"o_s": nc.dram_tensor("o_s", [T, D], F32, kind="ExternalOutput").ap(),
                "o_g": nc.dram_tensor("o_g", [T, D], F32, kind="ExternalOutput").ap()}

    with contextlib.ExitStack() as stk:
        p = Prog(nc)
        c = mk_ctx(stk, nc, p)
        sems = [stk.enter_context(nc.semaphore(f"sem{i}")) for i in range(4)]
        c.xtile = stk.enter_context(nc.sbuf_tensor("xtile", [128, D], F32))
        c.xn = stk.enter_context(nc.sbuf_tensor("xnb", [128, D], F32))
        c.xq = stk.enter_context(nc.sbuf_tensor("xqb", [128, D], F32))
        c.hqtb = stk.enter_context(nc.sbuf_tensor("hqtb", [128, 512], BF))
        tmp = mk_tmp(stk, nc, "tmp", D)
        xqT = alloc_tall(stk, nc, "xqT", D, T, BF)
        g = alloc_tall(stk, nc, "gacc", T, D, BF)

        phase1(c, x_in, tmp, l0={"scr_xn": scr_xn, "xqT": xqT})

        with contextlib.ExitStack() as ph:
            Sb = alloc_tall(ph, nc, "Sb", D, T, BF)
            afull = ph.enter_context(nc.sbuf_tensor("afull", [128, 512], F32))
            states = ph.enter_context(nc.sbuf_tensor("states", [128, 8], F32))
            stage = ph.enter_context(nc.sbuf_tensor("stg", [128, 512], F32))
            stage2 = ph.enter_context(nc.sbuf_tensor("stg2", [128, 512], F32))
            dct = ph.enter_context(nc.sbuf_tensor("dct", [1, 4], F32))
            acol = ph.enter_context(nc.sbuf_tensor("acol", [128, 1], F32))
            xns = [ph.enter_context(nc.sbuf_tensor(f"xns{j}", [128, D], F32)) for j in range(4)]
            p.dma(dct[0:1, 0:1], dec)
            p.op("vector", lambda e: e.memset(afull[:], 1.0))
            p.op("vector", lambda e: e.memset(states[:], 0.0))
            p.bar()
            act(c, dct[0:1, 1:2], dct[0:1, 0:1], AF.Sigmoid)
            p.bar()
            mm(c, c.PR[0:128, 0:1], [(c.ONES1[0:1, 0:128], dct[0:1, 1:2])])
            p.bar()
            evac(c, acol[:, 0:1], c.PR[0:128, 0:1])
            p.bar()
            vec_ts(c, afull[:], afull[:], acol[:, 0:1], OP.mult)
            p.bar()
            for tc in range(4):
                for j in range(4):
                    p.dma(xns[j][:], scr_xn[(4 * tc + j) * 128:(4 * tc + j + 1) * 128, :])
                p.bar()
                for ft in range(8):
                    items = [(xns[j][:, 128 * ft:128 * (ft + 1)],
                              [(stage[:, 128 * j:128 * (j + 1)], AF.Copy, 1.0)]) for j in range(4)]
                    transpose_round2(c, items)
                    p.op("vector", lambda e, o=stage2[:], d0=afull[:], d1=stage[:], ini=states[:, ft:ft + 1]:
                         e.tensor_tensor_scan(o, d0, d1, ini, OP.mult, OP.add))
                    p.bar()
                    evac(c, Sb.tile(ft)[:, tc * 512:(tc + 1) * 512], stage2[:])
                    c.p.op("vector", lambda e, o=states[:, ft:ft + 1], i=stage2[:, 511:512]: e.tensor_copy(o, i))
                    p.bar()
            WTb = alloc_tall(ph, nc, "WTb", D, D, BF)
            brow = ph.enter_context(nc.sbuf_tensor("brow", [1, D], BF))
            Grep = ph.enter_context(nc.sbuf_tensor("grep", [128, D], F32))
            Brep = ph.enter_context(nc.sbuf_tensor("brep", [128, D], F32))
            rstage = ph.enter_context(nc.sbuf_tensor("rstage", [1, D], F32))
            cast_load(c, c.xtile, WTb, WT, D, D)
            p.dma(rstage[0:1, :], bb)
            p.bar()
            evac(c, brow[0:1, :], rstage[0:1, :])
            p.bar()
            p.dma(rstage[0:1, :], gam)
            p.bar()
            bcast_row(c, Grep[:, 0:512], rstage[0:1, 0:512])
            bcast_row(c, Grep[:, 512:1024], rstage[0:1, 512:1024])
            p.dma(rstage[0:1, :], bet)
            p.bar()
            bcast_row(c, Brep[:, 0:512], rstage[0:1, 0:512])
            bcast_row(c, Brep[:, 512:1024], rstage[0:1, 512:1024])
            for tt in range(NT):
                for ch in range(2):
                    items = [(Sb.tile(pi)[:, tt * 128:(tt + 1) * 128],
                              WTb.tile(pi)[:, ch * 512:(ch + 1) * 512]) for pi in range(8)]
                    items.append((c.ONES1B[0:1, 0:128], brow[0:1, ch * 512:(ch + 1) * 512]))
                    mm(c, c.PM[ch][0:128, 0:512], items)
                p.bar()
                for ch in range(2):
                    evac(c, c.xtile[:, ch * 512:(ch + 1) * 512], c.PM[ch][0:128, 0:512])
                p.bar()
                layernorm_tile(c, c.xn[:], c.xtile[:], tmp)
                vec_tt(c, c.xn[:], c.xn[:], Grep[:], OP.mult)
                p.bar()
                vec_tt(c, c.xn[:], c.xn[:], Brep[:], OP.add)
                p.bar()
                p.dma(scr_s[tt * 128:(tt + 1) * 128, :], c.xn[:])
                p.bar()

        NCH = [(0, 512), (512, 1024), (1024, 1536), (1536, 2048), (2048, 2560), (2560, 2816)]
        with contextlib.ExitStack() as ph:
            wgqT = alloc_tall(ph, nc, "wgqT", D, 2816, BF)
            wuqT = alloc_tall(ph, nc, "wuqT", D, 2816, BF)
            hqf = ph.enter_context(nc.sbuf_tensor("hqf", [128, 512], F32))
            tmq = mk_tmp(ph, nc, "tmq", 512)
            for (wdram, wdst) in [(wg, wgqT), (wu, wuqT)]:
                for i in range(22):
                    p.dma(c.xtile, wdram[i * 128:(i + 1) * 128, :])
                    p.bar()
                    fp4_quant(c, c.xq[:], c.xtile[:], tmp)
                    items = [(c.xq[:, 128 * k:128 * (k + 1)],
                              [(wdst.tile(k)[:, 128 * i:128 * (i + 1)], AF.Copy, 1.0)]) for k in range(8)]
                    transpose_round2(c, items)
            for tt in range(NT):
                for (n0, n1) in NCH:
                    w = n1 - n0
                    mm(c, c.PM[0][0:128, 0:w],
                       [(xqT.tile(pi)[:, tt * 128:(tt + 1) * 128], wgqT.tile(pi)[:, n0:n1]) for pi in range(8)])
                    mm(c, c.PM[1][0:128, 0:w],
                       [(xqT.tile(pi)[:, tt * 128:(tt + 1) * 128], wuqT.tile(pi)[:, n0:n1]) for pi in range(8)])
                    p.bar()
                    act(c, tmq["a"][:, 0:w], c.PM[0][0:128, 0:w], AF.Silu)
                    p.bar()
                    vec_tt(c, c.xq[:, 0:w], tmq["a"][:, 0:w], c.PM[1][0:128, 0:w], OP.mult)
                    p.bar()
                    fp4_quant(c, hqf[:, 0:w], c.xq[:, 0:w], tmq)
                    nb = w // 128
                    items = [(hqf[:, 128 * k:128 * (k + 1)],
                              [(c.hqtb[:, 128 * k:128 * (k + 1)], AF.Copy, 1.0)]) for k in range(nb)]
                    transpose_round2(c, items)
                    for k in range(nb):
                        p.dma(scr_hq[n0 + 128 * k:n0 + 128 * (k + 1), tt * 128:(tt + 1) * 128],
                              c.hqtb[:, 128 * k:128 * (k + 1)])
                    p.bar()
        with contextlib.ExitStack() as ph:
            wdqT = alloc_tall(ph, nc, "wdqT", 2816, D, BF)
            hcol = alloc_tall(ph, nc, "hcol", 2816, 128, BF)
            qstage = ph.enter_context(nc.sbuf_tensor("qstage", [128, 2816], F32))
            hqf = ph.enter_context(nc.sbuf_tensor("hqf2", [128, 512], F32))
            tmq = mk_tmp(ph, nc, "tmq2", 512)
            for i in range(8):
                p.dma(qstage[:], wd[i * 128:(i + 1) * 128, :])
                p.bar()
                for (n0, n1) in NCH:
                    w = n1 - n0
                    fp4_quant(c, hqf[:, 0:w], qstage[:, n0:n1], tmq)
                    nb = w // 128
                    items = [(hqf[:, 128 * k:128 * (k + 1)],
                              [(wdqT.tile((n0 + 128 * k) // 128)[:, 128 * i:128 * (i + 1)], AF.Copy, 1.0)])
                             for k in range(nb)]
                    transpose_round2(c, items)
            for tt in range(NT):
                for pi in range(22):
                    p.dma(hcol.tile(pi)[:], scr_hq[pi * 128:(pi + 1) * 128, tt * 128:(tt + 1) * 128])
                p.bar()
                for ch in range(2):
                    mm(c, c.PM[ch][0:128, 0:512],
                       [(hcol.tile(pi)[:], wdqT.tile(pi)[:, ch * 512:(ch + 1) * 512]) for pi in range(22)])
                p.bar()
                for ch in range(2):
                    evac(c, g.tile(tt)[:, ch * 512:(ch + 1) * 512], c.PM[ch][0:128, 0:512], func=AF.Silu)
                p.bar()

        update_phase(c, x_in, x_out, scr_s, g, dbg=dbgd)
        p.finalize(sems)
    return nc


def prep_l0(inputs):
    d = {
        "w_dec": np.asarray(inputs["L0_attn_decay"], np.float32).reshape(1, 1),
        "w_WT": np.ascontiguousarray(np.asarray(inputs["L0_attn_W"], np.float32).T),
        "w_b": np.asarray(inputs["L0_attn_b"], np.float32).reshape(1, D),
        "w_gam": np.asarray(inputs["L0_attn_gamma"], np.float32).reshape(1, D),
        "w_bet": np.asarray(inputs["L0_attn_beta"], np.float32).reshape(1, D),
        "w_wg": np.asarray(inputs["L0_ffn_w_gate"], np.float32),
        "w_wu": np.asarray(inputs["L0_ffn_w_up"], np.float32),
        "w_wd": np.asarray(inputs["L0_ffn_w_down"], np.float32),
    }
    return d


def build_l1(parts=("z", "conv", "proj", "ffn", "upd")):
    nc = bass.Bass("TRN2", target_bir_lowering=False)
    x_in = nc.dram_tensor("xin", [T, D], F32, kind="ExternalInput").ap()
    x_out = nc.dram_tensor("xo", [T, D], F32, kind="ExternalOutput").ap()
    w_inT = nc.dram_tensor("w_inT", [D, 2048], F32, kind="ExternalInput").ap()
    b_in = nc.dram_tensor("w_binr", [1, 2048], F32, kind="ExternalInput").ap()
    waT = nc.dram_tensor("w_waT", [256, D], F32, kind="ExternalInput").ap()
    wbT = nc.dram_tensor("w_wbT", [256, D], F32, kind="ExternalInput").ap()
    bab = nc.dram_tensor("w_bab", [1, 2048], F32, kind="ExternalInput").ap()
    w_outT = nc.dram_tensor("w_outT", [2048, D], F32, kind="ExternalInput").ap()
    b_out = nc.dram_tensor("w_bout", [1, D], F32, kind="ExternalInput").ap()
    wgT = nc.dram_tensor("w_wgT", [D, 3072], I32, kind="ExternalInput").ap()
    wuT = nc.dram_tensor("w_wuT", [D, 3072], I32, kind="ExternalInput").ap()
    wdT = nc.dram_tensor("w_wdT", [3072, D], I32, kind="ExternalInput").ap()
    C512 = nc.dram_tensor("c_C512", [512, 512], BF, kind="ExternalInput").ap()
    S512 = nc.dram_tensor("c_S512", [512, 512], BF, kind="ExternalInput").ap()
    CRD = nc.dram_tensor("c_CR", [256, 512], BF, kind="ExternalInput").ap()
    SRD = nc.dram_tensor("c_SR", [256, 512], BF, kind="ExternalInput").ap()
    ICD = nc.dram_tensor("c_IC", [512, 256], BF, kind="ExternalInput").ap()
    ISD = nc.dram_tensor("c_IS", [512, 256], BF, kind="ExternalInput").ap()
    scr_z = nc.dram_tensor("scr_z", [T, 2048], BF, kind="Internal").ap()
    scr_cat = nc.dram_tensor("scr_cat", [2048, T], BF, kind="Internal").ap()
    scr_s = nc.dram_tensor("scr_s", [T, D], F32, kind="Internal").ap()

    with contextlib.ExitStack() as stk:
        p = Prog(nc)
        c = mk_ctx(stk, nc, p)
        sems = [stk.enter_context(nc.semaphore(f"sem{i}")) for i in range(4)]
        c.xtile = stk.enter_context(nc.sbuf_tensor("xtile", [128, D], F32))
        c.xn = stk.enter_context(nc.sbuf_tensor("xnb", [128, D], F32))
        c.xq = stk.enter_context(nc.sbuf_tensor("xqb", [128, D], F32))
        tmp = mk_tmp(stk, nc, "tmp", D)
        xnTb = alloc_tall(stk, nc, "xnTb", D, T, BF)

        phase1(c, x_in, tmp, xnTb=xnTb)

        if "z" in parts:
         with contextlib.ExitStack() as ph:
            w_inTb = alloc_tall(ph, nc, "winTb", D, 2048, BF)
            binb = ph.enter_context(nc.sbuf_tensor("binb", [1, 2048], BF))
            rstage = ph.enter_context(nc.sbuf_tensor("rstg", [1, 2048], F32))
            zbuf = ph.enter_context(nc.sbuf_tensor("zbuf", [128, 512], BF))
            for i in range(8):
                p.dma(c.xtile[:, 0:D], w_inT[i * 128:(i + 1) * 128, 0:1024])
                p.bar()
                evac(c, w_inTb.tile(i)[:, 0:1024], c.xtile[:, 0:D])
                p.bar()
                p.dma(c.xtile[:, 0:D], w_inT[i * 128:(i + 1) * 128, 1024:2048])
                p.bar()
                evac(c, w_inTb.tile(i)[:, 1024:2048], c.xtile[:, 0:D])
                p.bar()
            p.dma(rstage[0:1, :], b_in)
            p.bar()
            evac(c, binb[0:1, :], rstage[0:1, :])
            p.bar()
            for tt in range(NT):
                for ec in range(4):
                    items = [(xnTb.tile(pi)[:, tt * 128:(tt + 1) * 128],
                              w_inTb.tile(pi)[:, ec * 512:(ec + 1) * 512]) for pi in range(8)]
                    items.append((c.ONES1B[0:1, 0:128], binb[0:1, ec * 512:(ec + 1) * 512]))
                    mm(c, c.PM[0][0:128, 0:512], items)
                    p.bar()
                    evac(c, zbuf[:], c.PM[0][0:128, 0:512])
                    p.bar()
                    p.dma(scr_z[tt * 128:(tt + 1) * 128, ec * 512:(ec + 1) * 512], zbuf[:])
                    p.bar()

        if "conv" in parts:
         with contextlib.ExitStack() as ph:
            C512b = alloc_tall(ph, nc, "C512b", 512, 512, BF)
            S512b = alloc_tall(ph, nc, "S512b", 512, 512, BF)
            CRb = alloc_tall(ph, nc, "CRb", 256, 512, BF)
            SRb = alloc_tall(ph, nc, "SRb", 256, 512, BF)
            ICb = alloc_tall(ph, nc, "ICb", 512, 256, BF)
            ISb = alloc_tall(ph, nc, "ISb", 512, 256, BF)
            for tl, dr in [(C512b, C512), (S512b, S512)]:
                for i in range(4):
                    p.dma(tl.tile(i)[:], dr[i * 128:(i + 1) * 128, :])
            for tl, dr in [(CRb, CRD), (SRb, SRD)]:
                for i in range(2):
                    p.dma(tl.tile(i)[:], dr[i * 128:(i + 1) * 128, :])
            for tl, dr in [(ICb, ICD), (ISb, ISD)]:
                for i in range(4):
                    p.dma(tl.tile(i)[:], dr[i * 128:(i + 1) * 128, :])
            p.bar()
            Wc = alloc_tall(ph, nc, "Wcb", 512, 2048, BF)
            Ws = alloc_tall(ph, nc, "Wsb", 512, 2048, BF)
            watb = alloc_tall(ph, nc, "watb", 256, D, BF)
            wbtb = alloc_tall(ph, nc, "wbtb", 256, D, BF)
            for tl, dr in [(watb, waT), (wbtb, wbT)]:
                for i in range(2):
                    p.dma(c.xtile, dr[i * 128:(i + 1) * 128, :])
                    p.bar()
                    evac(c, tl.tile(i)[:], c.xtile[:])
                    p.bar()
            for (dst, lt) in [(Wc, CRb), (Ws, SRb)]:
                for po in range(4):
                    for m, wsb in [(0, watb), (1, wbtb)]:
                        for cch in range(2):
                            items = [(lt.tile(pi)[:, po * 128:(po + 1) * 128],
                                      wsb.tile(pi)[:, cch * 512:(cch + 1) * 512]) for pi in range(2)]
                            mm(c, c.PM[0][0:128, 0:512], items)
                            p.bar()
                            evac(c, dst.tile(po)[:, m * 1024 + cch * 512:m * 1024 + (cch + 1) * 512],
                                 c.PM[0][0:128, 0:512])
                            p.bar()
            babb = ph.enter_context(nc.sbuf_tensor("babb", [1, 2048], BF))
            rstage = ph.enter_context(nc.sbuf_tensor("rstg2", [1, 2048], F32))
            p.dma(rstage[0:1, :], bab)
            p.bar()
            evac(c, babb[0:1, :], rstage[0:1, :])
            p.bar()
            win = [ph.enter_context(nc.sbuf_tensor(f"win{j}", [128, 2048], BF)) for j in range(4)]
            zpad = ph.enter_context(nc.sbuf_tensor("zpad", [128, 2048], BF))
            p.op("vector", lambda e: e.memset(zpad[:], 0.0))
            p.bar()
            a_bf = alloc_tall(ph, nc, "abf", 512, 512, BF)
            b_bf = alloc_tall(ph, nc, "bbf", 512, 512, BF)
            conv_all = alloc_tall(ph, nc, "conv", 256, 2048, F32)
            gab = ph.enter_context(nc.sbuf_tensor("gab", [128, 1024], F32))
            gbb = ph.enter_context(nc.sbuf_tensor("gbb", [128, 1024], F32))
            sgb = ph.enter_context(nc.sbuf_tensor("sgb", [128, 1024], F32))
            ctst = ph.enter_context(nc.sbuf_tensor("ctst", [128, 1024], BF))
            VSP = [c.PT[0], c.PT[1], c.PR, c.PS]
            for tau in range(8):
                for si in range(4):
                    zt = 2 * tau - 2 + si
                    if zt < 0:
                        pass
                    else:
                        p.dma(win[si][:], scr_z[zt * 128:(zt + 1) * 128, :])
                p.bar()
                wins = [zpad if (2 * tau - 2 + si) < 0 else win[si] for si in range(4)]
                for cc in range(4):
                    cs = slice(cc * 512, (cc + 1) * 512)
                    for po in range(4):
                        mm(c, c.PM[po][0:128, 0:512],
                           [(C512b.tile(pi)[:, po * 128:(po + 1) * 128], wins[pi][:, cs]) for pi in range(4)])
                        mm(c, VSP[po][0:128, 0:512],
                           [(S512b.tile(pi)[:, po * 128:(po + 1) * 128], wins[pi][:, cs]) for pi in range(4)])
                    p.bar()
                    for po in range(4):
                        vec_stt(c, tmp["a"][:, 0:512], c.PM[po][0:128, 0:512], 1.0, Wc.tile(po)[:, cs], OP.mult, OP.mult)
                        p.bar()
                        vec_stt(c, tmp["b"][:, 0:512], VSP[po][0:128, 0:512], 1.0, Ws.tile(po)[:, cs], OP.mult, OP.mult)
                        p.bar()
                        vec_tt(c, a_bf.tile(po)[:], tmp["a"][:, 0:512], tmp["b"][:, 0:512], OP.subtract)
                        vec_stt(c, tmp["c"][:, 0:512], c.PM[po][0:128, 0:512], 1.0, Ws.tile(po)[:, cs], OP.mult, OP.mult)
                        p.bar()
                        vec_stt(c, tmp["d"][:, 0:512], VSP[po][0:128, 0:512], 1.0, Wc.tile(po)[:, cs], OP.mult, OP.mult)
                        p.bar()
                        vec_tt(c, b_bf.tile(po)[:], tmp["c"][:, 0:512], tmp["d"][:, 0:512], OP.add)
                        p.bar()
                    for pp in range(2):
                        items = [(ICb.tile(pi)[:, pp * 128:(pp + 1) * 128], a_bf.tile(pi)[:]) for pi in range(4)]
                        items += [(ISb.tile(pi)[:, pp * 128:(pp + 1) * 128], b_bf.tile(pi)[:]) for pi in range(4)]
                        items.append((c.ONES1B[0:1, 0:128], babb[0:1, cs]))
                        mm(c, c.PM[pp][0:128, 0:512], items)
                    p.bar()
                    for pp in range(2):
                        evac(c, conv_all.tile(pp)[:, cs], c.PM[pp][0:128, 0:512])
                    p.bar()
                for pt in range(2):
                    act(c, sgb[:], conv_all.tile(pt)[:, 1024:2048], AF.Sigmoid)
                    p.bar()
                    vec_tt(c, gab[:], conv_all.tile(pt)[:, 0:1024], sgb[:], OP.mult)
                    act(c, tmp["a"][:, 0:1024], conv_all.tile(pt)[:, 0:1024], AF.Sigmoid)
                    p.bar()
                    vec_tt(c, gbb[:], conv_all.tile(pt)[:, 1024:2048], tmp["a"][:, 0:1024], OP.mult)
                    p.bar()
                    tcol = tau * 256 + pt * 128
                    items = [(gab[:, cb * 128:(cb + 1) * 128],
                              [(ctst[:, cb * 128:(cb + 1) * 128], AF.Copy, 1.0)]) for cb in range(8)]
                    transpose_round2(c, items)
                    for cb in range(8):
                        p.dma(scr_cat[cb * 128:(cb + 1) * 128, tcol:tcol + 128], ctst[:, cb * 128:(cb + 1) * 128])
                    p.bar()
                    items = [(gbb[:, cb * 128:(cb + 1) * 128],
                              [(ctst[:, cb * 128:(cb + 1) * 128], AF.Copy, 1.0)]) for cb in range(8)]
                    transpose_round2(c, items)
                    for cb in range(8):
                        p.dma(scr_cat[1024 + cb * 128:1024 + (cb + 1) * 128, tcol:tcol + 128],
                              ctst[:, cb * 128:(cb + 1) * 128])
                    p.bar()

        if "proj" in parts:
         with contextlib.ExitStack() as ph:
            w_outTb = alloc_tall(ph, nc, "woutTb", 2048, D, BF)
            boutb = ph.enter_context(nc.sbuf_tensor("boutb", [1, D], BF))
            rstage = ph.enter_context(nc.sbuf_tensor("rstg3", [1, D], F32))
            catcol = alloc_tall(ph, nc, "catcol", 2048, 128, BF)
            cast_load(c, c.xtile, w_outTb, w_outT, 2048, D)
            p.dma(rstage[0:1, :], b_out)
            p.bar()
            evac(c, boutb[0:1, :], rstage[0:1, :])
            p.bar()
            for tt in range(NT):
                for pi in range(16):
                    p.dma(catcol.tile(pi)[:], scr_cat[pi * 128:(pi + 1) * 128, tt * 128:(tt + 1) * 128])
                p.bar()
                for ch in range(2):
                    items = [(catcol.tile(pi)[:], w_outTb.tile(pi)[:, ch * 512:(ch + 1) * 512])
                             for pi in range(16)]
                    items.append((c.ONES1B[0:1, 0:128], boutb[0:1, ch * 512:(ch + 1) * 512]))
                    mm(c, c.PM[ch][0:128, 0:512], items)
                p.bar()
                for ch in range(2):
                    evac(c, c.xn[:, ch * 512:(ch + 1) * 512], c.PM[ch][0:128, 0:512])
                p.bar()
                p.dma(scr_s[tt * 128:(tt + 1) * 128, :], c.xn[:])
                p.bar()

        g = alloc_tall(stk, nc, "gacc", T, D, BF)
        if "ffn" in parts:
         with contextlib.ExitStack() as ph:
            istage = ph.enter_context(nc.sbuf_tensor("istage", [128, 1024], I32))
            yTh = alloc_tall(ph, nc, "yTh", D, 1024, F32)
            wgb = alloc_tall(ph, nc, "wgb", D, 512, BF)
            wub = alloc_tall(ph, nc, "wub", D, 512, BF)
            wdb = alloc_tall(ph, nc, "wdb", 512, D, BF)
            hTb = alloc_tall(ph, nc, "hTb", 512, 1024, BF)

            def i32load(dst_ap, src_ap, wcols):
                p.dma(istage[0:128, 0:wcols], src_ap)
                p.bar()
                c.p.op("vector", lambda e, o=c.xtile[:, 0:wcols], i=istage[0:128, 0:wcols]: e.tensor_copy(o, i))
                p.bar()
                evac(c, dst_ap, c.xtile[:, 0:wcols], scale=1.0 / 127.0)
                p.bar()

            for half in range(2):
                for ho in range(8):
                    p.op("vector", lambda e, o=yTh.tile(ho)[:]: e.memset(o, 0.0))
                p.bar()
                for chh in range(6):
                    for i in range(8):
                        i32load(wgb.tile(i)[:], wgT[i * 128:(i + 1) * 128, chh * 512:(chh + 1) * 512], 512)
                        i32load(wub.tile(i)[:], wuT[i * 128:(i + 1) * 128, chh * 512:(chh + 1) * 512], 512)
                    for i in range(4):
                        i32load(wdb.tile(i)[:], wdT[chh * 512 + i * 128:chh * 512 + (i + 1) * 128, :], 1024)
                    for io in range(4):
                        for tch in range(2):
                            t0 = half * 1024 + tch * 512
                            mm(c, c.PM[0][0:128, 0:512],
                               [(wgb.tile(pi)[:, io * 128:(io + 1) * 128], xnTb.tile(pi)[:, t0:t0 + 512])
                                for pi in range(8)])
                            mm(c, c.PM[1][0:128, 0:512],
                               [(wub.tile(pi)[:, io * 128:(io + 1) * 128], xnTb.tile(pi)[:, t0:t0 + 512])
                                for pi in range(8)])
                            p.bar()
                            act(c, tmp["a"][:, 0:512], c.PM[0][0:128, 0:512], AF.Silu)
                            p.bar()
                            vec_tt(c, hTb.tile(io)[:, tch * 512:(tch + 1) * 512],
                                   tmp["a"][:, 0:512], c.PM[1][0:128, 0:512], OP.mult)
                            p.bar()
                    for ho in range(8):
                        for tch in range(2):
                            mm(c, c.PM[2][0:128, 0:512],
                               [(wdb.tile(pi)[:, ho * 128:(ho + 1) * 128],
                                 hTb.tile(pi)[:, tch * 512:(tch + 1) * 512]) for pi in range(4)])
                            p.bar()
                            vec_tt(c, yTh.tile(ho)[:, tch * 512:(tch + 1) * 512],
                                   yTh.tile(ho)[:, tch * 512:(tch + 1) * 512], c.PM[2][0:128, 0:512], OP.add)
                            p.bar()
                for tl in range(8):
                    items = [(yTh.tile(pi)[:, tl * 128:(tl + 1) * 128],
                              [(g.tile(half * 8 + tl)[:, pi * 128:(pi + 1) * 128], AF.Silu, 1.0)])
                             for pi in range(8)]
                    transpose_round2(c, items)

        if "upd" in parts:
            update_phase(c, x_in, x_out, scr_s, g)
        else:
            for tt in range(NT):
                p.dma(c.xtile, x_in[tt * 128:(tt + 1) * 128, :])
                p.bar()
                p.dma(x_out[tt * 128:(tt + 1) * 128, :], c.xtile[:])
                p.bar()
        p.finalize(sems)
    return nc


def prep_l1(inputs):
    s_ = np.arange(512)
    f_ = np.arange(512)
    ang = 2 * np.pi * np.outer(s_, f_) / 512.0
    k_ = np.arange(256)
    angr = 2 * np.pi * np.outer(255 - k_, f_).T.astype(np.float64) / 512.0
    pp_ = np.arange(256)
    angi = 2 * np.pi * np.outer(f_, pp_ + 256) / 512.0
    d = {
        "w_inT": np.ascontiguousarray(np.asarray(inputs["L1_attn_w_in"], np.float32).T),
        "w_binr": np.asarray(inputs["L1_attn_b_in"], np.float32).reshape(1, 2048),
        "w_waT": np.ascontiguousarray(np.asarray(inputs["L1_attn_w_a"], np.float32).reshape(1024, 256).T),
        "w_wbT": np.ascontiguousarray(np.asarray(inputs["L1_attn_w_b"], np.float32).reshape(1024, 256).T),
        "w_bab": np.concatenate([np.asarray(inputs["L1_attn_b_a"], np.float32),
                                 np.asarray(inputs["L1_attn_b_b"], np.float32)]).reshape(1, 2048),
        "w_outT": np.ascontiguousarray(np.asarray(inputs["L1_attn_w_out"], np.float32).T),
        "w_bout": np.asarray(inputs["L1_attn_b_out"], np.float32).reshape(1, D),
        "w_wgT": np.ascontiguousarray(np.asarray(inputs["L1_ffn_w_gate"], np.int32).T),
        "w_wuT": np.ascontiguousarray(np.asarray(inputs["L1_ffn_w_up"], np.int32).T),
        "w_wdT": np.ascontiguousarray(np.asarray(inputs["L1_ffn_w_down"], np.int32).T),
        "c_C512": np.cos(ang).astype(BFNP),
        "c_S512": np.sin(ang).astype(BFNP),
        "c_CR": np.cos(angr.T).astype(BFNP).reshape(256, 512) if False else np.cos(2 * np.pi * np.outer(255 - k_, f_) / 512.0).astype(BFNP),
        "c_SR": np.sin(2 * np.pi * np.outer(255 - k_, f_) / 512.0).astype(BFNP),
        "c_IC": (np.cos(angi) / 512.0).astype(BFNP),
        "c_IS": (np.sin(angi) / 512.0).astype(BFNP),
    }
    return d


def attn_core(c, xnTb, ctxT, h, qt_src, kt_src, v_src, bias_fn, p):
    for qc in range(4):
        qs = slice(qc * 512, (qc + 1) * 512)
        valid = [kt for kt in range(16) if bias_fn(qc, kt, None, None, probe=True)]
        for j, kt in enumerate(valid):
            mm(c, c.PM[2][0:128, 0:512], [(kt_src[:, kt * 128:(kt + 1) * 128], qt_src[:, qs])])
            p.bar()
            bias_fn(qc, kt, c.PM[2][0:128, 0:512], c.ATT)
            p.bar()
            mm(c, c.PS[0:1, 0:512], [(c.ONESCB[0:128, 0:1], c.ATT[:])],
               start=(j == 0), stop=(j == len(valid) - 1))
            mm(c, c.PM[3][0:64, 0:512], [(v_src.tile(kt)[:], c.ATT[:])],
               start=(j == 0), stop=(j == len(valid) - 1))
            p.bar()
        evac(c, c.csr[0:1, 0:512], c.PS[0:1, 0:512])
        p.bar()
        c.p.op("vector", lambda e, o=c.csr[1:2, 0:512] if False else None: None) if False else None
        c.p.op("vector", lambda e, o=c.csi[0:1, 0:512], i=c.csr[0:1, 0:512]: e.reciprocal(o, i))
        p.bar()
        mm(c, c.PR[0:64, 0:512], [(c.ONES1[0:1, 0:64], c.csi[0:1, 0:512])])
        p.bar()
        evac(c, c.rrep[0:64, 0:512], c.PR[0:64, 0:512])
        p.bar()
        vec_tt(c, ctxT.tile(h // 2)[64 * (h % 2):64 * (h % 2) + 64, qs],
               c.PM[3][0:64, 0:512], c.rrep[0:64, 0:512], OP.mult)
        p.bar()


def load_head_w(c, p, dst8, wdram, col0, stage):
    for i in range(8):
        p.dma(stage[0:128, 0:64], wdram[i * 128:(i + 1) * 128, col0:col0 + 64])
        p.bar()
        evac(c, dst8[i][0:128, 0:64], stage[0:128, 0:64])
        p.bar()


def qkv_head(c, p, xnTb, wb8, bcol, scale, QT):
    for qc in range(4):
        mm(c, c.PM[0][0:64, 0:512],
           [(wb8[pi][0:128, 0:64], xnTb.tile(pi)[:, qc * 512:(qc + 1) * 512]) for pi in range(8)])
        p.bar()
        vec_ts(c, QT[0:64, qc * 512:(qc + 1) * 512], c.PM[0][0:64, 0:512], bcol, OP.add, scale, OP.mult)
        p.bar()


def vproj_head(c, p, xnTb, wb8, bvrow_bf, Vb):
    for tt in range(NT):
        items = [(xnTb.tile(pi)[:, tt * 128:(tt + 1) * 128], wb8[pi][0:128, 0:64]) for pi in range(8)]
        items.append((c.ONES1B[0:1, 0:128], bvrow_bf[0:1, 0:64]))
        mm(c, c.PM[0][0:128, 0:64], items)
        p.bar()
        evac(c, Vb.tile(tt)[:, 0:64], c.PM[0][0:128, 0:64])
        p.bar()


def out_proj(c, p, ctxT, wob, borow_bf, scr_s):
    for tt in range(NT):
        for ch in range(2):
            items = [(ctxT.tile(pi)[:, tt * 128:(tt + 1) * 128], wob.tile(pi)[:, ch * 512:(ch + 1) * 512])
                     for pi in range(8)]
            items.append((c.ONES1B[0:1, 0:128], borow_bf[0:1, ch * 512:(ch + 1) * 512]))
            mm(c, c.PM[ch][0:128, 0:512], items)
        p.bar()
        for ch in range(2):
            evac(c, c.xn[:, ch * 512:(ch + 1) * 512], c.PM[ch][0:128, 0:512])
        p.bar()
        p.dma(scr_s[tt * 128:(tt + 1) * 128, :], c.xn[:])
        p.bar()


def build_l2():
    nc = bass.Bass("TRN2", target_bir_lowering=False)
    x_in = nc.dram_tensor("xin", [T, D], F32, kind="ExternalInput").ap()
    x_out = nc.dram_tensor("xo", [T, D], F32, kind="ExternalOutput").ap()
    Wq = nc.dram_tensor("w_Wq", [D, D], F32, kind="ExternalInput").ap()
    Wk = nc.dram_tensor("w_Wk", [D, D], F32, kind="ExternalInput").ap()
    Wv = nc.dram_tensor("w_Wv", [D, D], F32, kind="ExternalInput").ap()
    Wo = nc.dram_tensor("w_Wo", [D, D], F32, kind="ExternalInput").ap()
    bq = nc.dram_tensor("w_bq", [D, 1], F32, kind="ExternalInput").ap()
    bk = nc.dram_tensor("w_bk", [D, 1], F32, kind="ExternalInput").ap()
    bv = nc.dram_tensor("w_bv", [1, D], F32, kind="ExternalInput").ap()
    bo = nc.dram_tensor("w_bo", [1, D], F32, kind="ExternalInput").ap()
    relb = nc.dram_tensor("w_relb", [32, 16], F32, kind="ExternalInput").ap()
    crawk = nc.dram_tensor("w_crawk", [8, 16, 128, 512], F32, kind="ExternalInput").ap()
    gwT = nc.dram_tensor("w_gwT", [D, 8], F32, kind="ExternalInput").ap()
    wegT = nc.dram_tensor("w_wegT", [8, D, D], F32, kind="ExternalInput").ap()
    weuT = nc.dram_tensor("w_weuT", [8, D, D], F32, kind="ExternalInput").ap()
    wedT = nc.dram_tensor("w_wedT", [8, D, D], F32, kind="ExternalInput").ap()
    wsgT = nc.dram_tensor("w_wsgT", [D, 2048], F32, kind="ExternalInput").ap()
    wsuT = nc.dram_tensor("w_wsuT", [D, 2048], F32, kind="ExternalInput").ap()
    wsdT = nc.dram_tensor("w_wsdT", [2048, D], F32, kind="ExternalInput").ap()
    scr_s = nc.dram_tensor("scr_s", [T, D], F32, kind="Internal").ap()

    with contextlib.ExitStack() as stk:
        p = Prog(nc)
        c = mk_ctx(stk, nc, p)
        sems = [stk.enter_context(nc.semaphore(f"sem{i}")) for i in range(4)]
        c.xtile = stk.enter_context(nc.sbuf_tensor("xtile", [128, D], F32))
        c.xn = stk.enter_context(nc.sbuf_tensor("xnb", [128, D], F32))
        c.xq = stk.enter_context(nc.sbuf_tensor("xqb", [128, D], F32))
        tmp = mk_tmp(stk, nc, "tmp", D)
        xnTb = alloc_tall(stk, nc, "xnTb", D, T, BF)
        g = alloc_tall(stk, nc, "gacc", T, D, BF)
        combs = [stk.enter_context(nc.sbuf_tensor(f"comb{i}", [128, 8], F32)) for i in range(NT)]

        with contextlib.ExitStack() as ph:
            strip = alloc_tall(ph, nc, "strip", D, 128, F32)
            gwTb = alloc_tall(ph, nc, "gwTb", D, 8, F32)
            l2d = {"strip": strip, "gwT": gwTb,
                   "lg": ph.enter_context(nc.sbuf_tensor("lg", [128, 8], F32)),
                   "msk1": ph.enter_context(nc.sbuf_tensor("msk1", [128, 8], F32)),
                   "msk2": ph.enter_context(nc.sbuf_tensor("msk2", [128, 8], F32)),
                   "l2nd": ph.enter_context(nc.sbuf_tensor("l2nd", [128, 8], F32)),
                   "comb": combs}
            for i in range(8):
                p.dma(gwTb.tile(i)[:], gwT[i * 128:(i + 1) * 128, :])
            p.bar()
            phase1(c, x_in, tmp, xnTb=xnTb, l2=l2d)

        with contextlib.ExitStack() as ph:
            ctxT = alloc_tall(ph, nc, "ctxT", D, T, BF)
            QT = ph.enter_context(nc.sbuf_tensor("QT", [64, T], BF))
            KT = ph.enter_context(nc.sbuf_tensor("KT", [64, T], BF))
            Vb = alloc_tall(ph, nc, "Vb", T, 64, BF)
            wq8 = [ph.enter_context(nc.sbuf_tensor(f"wq8_{i}", [128, 64], BF)) for i in range(8)]
            wk8 = [ph.enter_context(nc.sbuf_tensor(f"wk8_{i}", [128, 64], BF)) for i in range(8)]
            wv8 = [ph.enter_context(nc.sbuf_tensor(f"wv8_{i}", [128, 64], BF)) for i in range(8)]
            wst = ph.enter_context(nc.sbuf_tensor("wst", [128, 64], F32))
            bcolq = ph.enter_context(nc.sbuf_tensor("bcolq", [64, 2], F32))
            bcolk = ph.enter_context(nc.sbuf_tensor("bcolk", [64, 2], F32))
            bvrow = ph.enter_context(nc.sbuf_tensor("bvrow", [1, 64], F32))
            bvrow_bf = ph.enter_context(nc.sbuf_tensor("bvrowb", [1, 64], BF))
            rb0 = ph.enter_context(nc.sbuf_tensor("rb0", [128, 16], F32))
            rb31 = ph.enter_context(nc.sbuf_tensor("rb31", [128, 16], F32))
            rstage = ph.enter_context(nc.sbuf_tensor("rstg", [1, 16], F32))
            crb = ph.enter_context(nc.sbuf_tensor("crb", [128, 512], F32))
            c.ATT = ph.enter_context(nc.sbuf_tensor("ATT", [128, 512], BF))
            c.csr = ph.enter_context(nc.sbuf_tensor("csr", [2, 512], F32))
            c.csi = ph.enter_context(nc.sbuf_tensor("csi", [2, 512], F32))
            c.rrep = ph.enter_context(nc.sbuf_tensor("rrep", [64, 512], F32))
            scs = ph.enter_context(nc.sbuf_tensor("scs", [128, 512], F32))
            p.dma(rstage[0:1, :], relb[0:1, :])
            p.bar()
            mm(c, c.PR[0:128, 0:16], [(c.ONES1[0:1, 0:128], rstage[0:1, 0:16])])
            p.bar()
            evac(c, rb0[:, 0:16], c.PR[0:128, 0:16], scale=8.0)
            p.bar()
            p.dma(rstage[0:1, :], relb[31:32, :])
            p.bar()
            mm(c, c.PR[0:128, 0:16], [(c.ONES1[0:1, 0:128], rstage[0:1, 0:16])])
            p.bar()
            evac(c, rb31[:, 0:16], c.PR[0:128, 0:16], scale=8.0)
            p.bar()
            for h in range(16):
                load_head_w(c, p, wq8, Wq, 64 * h, wst)
                load_head_w(c, p, wk8, Wk, 64 * h, wst)
                load_head_w(c, p, wv8, Wv, 64 * h, wst)
                p.dma(bcolq[0:64, 0:1], bq[64 * h:64 * h + 64, 0:1])
                p.dma(bcolk[0:64, 0:1], bk[64 * h:64 * h + 64, 0:1])
                p.dma(bvrow[0:1, 0:64], bv[0:1, 64 * h:64 * h + 64])
                p.bar()
                evac(c, bvrow_bf[0:1, 0:64], bvrow[0:1, 0:64])
                p.bar()
                qkv_head(c, p, xnTb, wq8, bcolq[0:64, 0:1], 0.125, QT)
                qkv_head(c, p, xnTb, wk8, bcolk[0:64, 0:1], 1.0, KT)
                vproj_head(c, p, xnTb, wv8, bvrow_bf, Vb)

                def bias_fn(qc, kt, ps, att, probe=False, h=h):
                    if probe:
                        return True
                    cls = 4 * qc - kt
                    if cls <= -4:
                        act(c, att[:], ps, AF.Exp, bias=rb0[:, h:h + 1])
                    elif cls >= 5:
                        act(c, att[:], ps, AF.Exp, bias=rb31[:, h:h + 1])
                    else:
                        p.dma(crb[:], crawk[cls + 3, h, :, :])
                        p.bar()
                        vec_stt(c, scs[:], crb[:], 8.0, ps, OP.mult, OP.add)
                        p.bar()
                        act(c, att[:], scs[:], AF.Exp)
                    return True
                attn_core(c, xnTb, ctxT, h, QT, KT, Vb, bias_fn, p)
            wob = alloc_tall(ph, nc, "wob", D, D, BF)
            borow = ph.enter_context(nc.sbuf_tensor("borow", [1, D], F32))
            borow_bf = ph.enter_context(nc.sbuf_tensor("borowb", [1, D], BF))
            cast_load(c, c.xtile, wob, Wo, D, D)
            p.dma(borow[0:1, :], bo)
            p.bar()
            evac(c, borow_bf[0:1, :], borow[0:1, :])
            p.bar()
            out_proj(c, p, ctxT, wob, borow_bf, scr_s)

        with contextlib.ExitStack() as ph:
            yTh = alloc_tall(ph, nc, "yTh", D, 1024, F32)
            wegb = alloc_tall(ph, nc, "wegb", D, D, BF)
            weub = alloc_tall(ph, nc, "weub", D, D, BF)
            wedb = alloc_tall(ph, nc, "wedb", D, D, BF)
            hsT = alloc_tall(ph, nc, "hsT", D, 1024, BF)
            combrow = ph.enter_context(nc.sbuf_tensor("combrow", [1, T], F32))
            crep = ph.enter_context(nc.sbuf_tensor("crep", [128, 512], F32))
            for half in range(2):
                for ho in range(8):
                    p.op("vector", lambda e, o=yTh.tile(ho)[:]: e.memset(o, 0.0))
                p.bar()
                for e_ in range(8):
                    cast_load(c, c.xtile, wegb, wegT[e_], D, D)
                    cast_load(c, c.xtile, weub, weuT[e_], D, D)
                    items = [(combs[tt][:, e_:e_ + 1],
                              [(combrow[0:1, tt * 128:(tt + 1) * 128], AF.Copy, 1.0)]) for tt in range(NT)]
                    transpose_round2(c, items)
                    for tch in range(2):
                        t0 = half * 1024 + tch * 512
                        mm(c, c.PR[0:128, 0:512], [(c.ONES1[0:1, 0:128], combrow[0:1, t0:t0 + 512])])
                        p.bar()
                        evac(c, crep[:], c.PR[0:128, 0:512])
                        p.bar()
                        for io in range(8):
                            mm(c, c.PM[0][0:128, 0:512],
                               [(wegb.tile(pi)[:, io * 128:(io + 1) * 128], xnTb.tile(pi)[:, t0:t0 + 512])
                                for pi in range(8)])
                            mm(c, c.PM[1][0:128, 0:512],
                               [(weub.tile(pi)[:, io * 128:(io + 1) * 128], xnTb.tile(pi)[:, t0:t0 + 512])
                                for pi in range(8)])
                            p.bar()
                            act(c, tmp["a"][:, 0:512], c.PM[0][0:128, 0:512], AF.Silu)
                            p.bar()
                            vec_tt(c, tmp["b"][:, 0:512], tmp["a"][:, 0:512], c.PM[1][0:128, 0:512], OP.mult)
                            p.bar()
                            vec_tt(c, hsT.tile(io)[:, tch * 512:(tch + 1) * 512],
                                   tmp["b"][:, 0:512], crep[:], OP.mult)
                            p.bar()
                    cast_load(c, c.xtile, wedb, wedT[e_], D, D)
                    for ho in range(8):
                        for tch in range(2):
                            mm(c, c.PM[2][0:128, 0:512],
                               [(wedb.tile(pi)[:, ho * 128:(ho + 1) * 128],
                                 hsT.tile(pi)[:, tch * 512:(tch + 1) * 512]) for pi in range(8)])
                            p.bar()
                            vec_tt(c, yTh.tile(ho)[:, tch * 512:(tch + 1) * 512],
                                   yTh.tile(ho)[:, tch * 512:(tch + 1) * 512], c.PM[2][0:128, 0:512], OP.add)
                            p.bar()
                for chh in range(4):
                    i0 = chh * 512
                    cast_load(c, c.xtile, wegb, wsgT, D, 512, col0=i0)
                    cast_load(c, c.xtile, weub, wsuT, D, 512, col0=i0)
                    for i in range(4):
                        p.dma(c.xtile, wsdT[i0 + i * 128:i0 + (i + 1) * 128, :])
                        p.bar()
                        evac(c, wedb.tile(i)[:], c.xtile[:])
                        p.bar()
                    for io in range(4):
                        for tch in range(2):
                            t0 = half * 1024 + tch * 512
                            mm(c, c.PM[0][0:128, 0:512],
                               [(wegb.tile(pi)[:, io * 128:(io + 1) * 128], xnTb.tile(pi)[:, t0:t0 + 512])
                                for pi in range(8)])
                            mm(c, c.PM[1][0:128, 0:512],
                               [(weub.tile(pi)[:, io * 128:(io + 1) * 128], xnTb.tile(pi)[:, t0:t0 + 512])
                                for pi in range(8)])
                            p.bar()
                            act(c, tmp["a"][:, 0:512], c.PM[0][0:128, 0:512], AF.Silu)
                            p.bar()
                            vec_tt(c, hsT.tile(io)[:, tch * 512:(tch + 1) * 512],
                                   tmp["a"][:, 0:512], c.PM[1][0:128, 0:512], OP.mult)
                            p.bar()
                    for ho in range(8):
                        for tch in range(2):
                            mm(c, c.PM[2][0:128, 0:512],
                               [(wedb.tile(pi)[:, ho * 128:(ho + 1) * 128],
                                 hsT.tile(pi)[:, tch * 512:(tch + 1) * 512]) for pi in range(4)])
                            p.bar()
                            vec_tt(c, yTh.tile(ho)[:, tch * 512:(tch + 1) * 512],
                                   yTh.tile(ho)[:, tch * 512:(tch + 1) * 512], c.PM[2][0:128, 0:512], OP.add)
                            p.bar()
                for tl in range(8):
                    items = [(yTh.tile(pi)[:, tl * 128:(tl + 1) * 128],
                              [(g.tile(half * 8 + tl)[:, pi * 128:(pi + 1) * 128], AF.Silu, 1.0)])
                             for pi in range(8)]
                    transpose_round2(c, items)

        update_phase(c, x_in, x_out, scr_s, g)
        p.finalize(sems)
    return nc


def _bucket_np(d):
    rp = np.maximum(d, 0)
    is_small = rp < 16
    rp_f = np.maximum(rp, 1).astype(np.float32)
    large = 16 + (np.log(rp_f / np.float32(16.0)) / np.log(64.0 / 16.0) * 16).astype(np.int32)
    large = np.minimum(large, 31)
    return np.where(is_small, rp, large)


def prep_l2(inputs):
    relb = np.asarray(inputs["L2_attn_rel_bias"], np.float32)
    cls_ = np.arange(-3, 5)
    p_ = np.arange(128)
    f_ = np.arange(512)
    dmat = 128 * cls_[:, None, None] + f_[None, None, :] - p_[None, :, None]
    buck = _bucket_np(dmat)
    crawk = relb[buck]
    crawk = np.ascontiguousarray(np.transpose(crawk, (0, 3, 1, 2)))
    d = {
        "w_Wq": np.asarray(inputs["L2_attn_Wq"], np.float32),
        "w_Wk": np.asarray(inputs["L2_attn_Wk"], np.float32),
        "w_Wv": np.asarray(inputs["L2_attn_Wv"], np.float32),
        "w_Wo": np.asarray(inputs["L2_attn_Wo"], np.float32),
        "w_bq": np.asarray(inputs["L2_attn_bq"], np.float32).reshape(D, 1),
        "w_bk": np.asarray(inputs["L2_attn_bk"], np.float32).reshape(D, 1),
        "w_bv": np.asarray(inputs["L2_attn_bv"], np.float32).reshape(1, D),
        "w_bo": np.asarray(inputs["L2_attn_bo"], np.float32).reshape(1, D),
        "w_relb": relb,
        "w_crawk": crawk,
        "w_gwT": np.ascontiguousarray(np.asarray(inputs["L2_ffn_gate_w"], np.float32).T),
        "w_wegT": np.ascontiguousarray(np.transpose(np.asarray(inputs["L2_ffn_we_gate"], np.float32), (0, 2, 1))),
        "w_weuT": np.ascontiguousarray(np.transpose(np.asarray(inputs["L2_ffn_we_up"], np.float32), (0, 2, 1))),
        "w_wedT": np.ascontiguousarray(np.transpose(np.asarray(inputs["L2_ffn_we_down"], np.float32), (0, 2, 1))),
        "w_wsgT": np.ascontiguousarray(np.asarray(inputs["L2_ffn_ws_gate"], np.float32).T),
        "w_wsuT": np.ascontiguousarray(np.asarray(inputs["L2_ffn_ws_up"], np.float32).T),
        "w_wsdT": np.ascontiguousarray(np.asarray(inputs["L2_ffn_ws_down"], np.float32).T),
    }
    return d


def build_l3():
    nc = bass.Bass("TRN2", target_bir_lowering=False)
    x_in = nc.dram_tensor("xin", [T, D], F32, kind="ExternalInput").ap()
    x_out = nc.dram_tensor("xo", [T, D], F32, kind="ExternalOutput").ap()
    wqkv = nc.dram_tensor("w_qkv", [D, 3072], F32, kind="ExternalInput").ap()
    bqkv = nc.dram_tensor("w_bqkv", [3072, 1], F32, kind="ExternalInput").ap()
    bqkvr = nc.dram_tensor("w_bqkvr", [1, 3072], F32, kind="ExternalInput").ap()
    wproj = nc.dram_tensor("w_proj", [D, D], F32, kind="ExternalInput").ap()
    bproj = nc.dram_tensor("w_bproj", [1, D], F32, kind="ExternalInput").ap()
    wfr = nc.dram_tensor("w_wfr", [D, D], F32, kind="ExternalInput").ap()
    wfi = nc.dram_tensor("w_wfi", [D, D], F32, kind="ExternalInput").ap()
    fbias = nc.dram_tensor("w_fbias", [1, D], F32, kind="ExternalInput").ap()
    C1024 = nc.dram_tensor("c_C1024", [D, D], BF, kind="ExternalInput").ap()
    S1024 = nc.dram_tensor("c_S1024", [D, D], BF, kind="ExternalInput").ap()
    negm = nc.dram_tensor("c_negm", [4, 128, 512], F32, kind="ExternalInput").ap()
    scr_s = nc.dram_tensor("scr_s", [T, D], F32, kind="Internal").ap()

    with contextlib.ExitStack() as stk:
        p = Prog(nc)
        c = mk_ctx(stk, nc, p)
        sems = [stk.enter_context(nc.semaphore(f"sem{i}")) for i in range(4)]
        c.xtile = stk.enter_context(nc.sbuf_tensor("xtile", [128, D], F32))
        c.xn = stk.enter_context(nc.sbuf_tensor("xnb", [128, D], F32))
        c.xq = stk.enter_context(nc.sbuf_tensor("xqb", [128, D], F32))
        tmp = mk_tmp(stk, nc, "tmp", D)
        xnTb = alloc_tall(stk, nc, "xnTb", D, T, BF)
        g = alloc_tall(stk, nc, "gacc", T, D, BF)

        phase1(c, x_in, tmp, xnTb=xnTb)

        with contextlib.ExitStack() as ph:
            ctxT = alloc_tall(ph, nc, "ctxT", D, T, BF)
            QT = ph.enter_context(nc.sbuf_tensor("QT", [64, T], BF))
            KT = ph.enter_context(nc.sbuf_tensor("KT", [64, T], BF))
            Vb = alloc_tall(ph, nc, "Vb", T, 64, BF)
            wq8 = [ph.enter_context(nc.sbuf_tensor(f"wq8_{i}", [128, 64], BF)) for i in range(8)]
            wk8 = [ph.enter_context(nc.sbuf_tensor(f"wk8_{i}", [128, 64], BF)) for i in range(8)]
            wv8 = [ph.enter_context(nc.sbuf_tensor(f"wv8_{i}", [128, 64], BF)) for i in range(8)]
            wst = ph.enter_context(nc.sbuf_tensor("wst", [128, 64], F32))
            bcolq = ph.enter_context(nc.sbuf_tensor("bcolq", [64, 2], F32))
            bcolk = ph.enter_context(nc.sbuf_tensor("bcolk", [64, 2], F32))
            bvrow = ph.enter_context(nc.sbuf_tensor("bvrow", [1, 64], F32))
            bvrow_bf = ph.enter_context(nc.sbuf_tensor("bvrowb", [1, 64], BF))
            nmt = [ph.enter_context(nc.sbuf_tensor(f"nmt{i}", [128, 512], F32)) for i in range(4)]
            c.ATT = ph.enter_context(nc.sbuf_tensor("ATT", [128, 512], BF))
            c.csr = ph.enter_context(nc.sbuf_tensor("csr", [2, 512], F32))
            c.csi = ph.enter_context(nc.sbuf_tensor("csi", [2, 512], F32))
            c.rrep = ph.enter_context(nc.sbuf_tensor("rrep", [64, 512], F32))
            scs = ph.enter_context(nc.sbuf_tensor("scs", [128, 512], F32))
            for i in range(4):
                p.dma(nmt[i][:], negm[i, :, :])
            p.bar()
            for h in range(16):
                load_head_w(c, p, wq8, wqkv, 64 * h, wst)
                load_head_w(c, p, wk8, wqkv, 1024 + 64 * h, wst)
                load_head_w(c, p, wv8, wqkv, 2048 + 64 * h, wst)
                p.dma(bcolq[0:64, 0:1], bqkv[64 * h:64 * h + 64, 0:1])
                p.dma(bcolk[0:64, 0:1], bqkv[1024 + 64 * h:1024 + 64 * h + 64, 0:1])
                p.dma(bvrow[0:1, 0:64], bqkvr[0:1, 2048 + 64 * h:2048 + 64 * h + 64])
                p.bar()
                evac(c, bvrow_bf[0:1, 0:64], bvrow[0:1, 0:64])
                p.bar()
                qkv_head(c, p, xnTb, wq8, bcolq[0:64, 0:1], 0.125, QT)
                qkv_head(c, p, xnTb, wk8, bcolk[0:64, 0:1], 1.0, KT)
                vproj_head(c, p, xnTb, wv8, bvrow_bf, Vb)

                def bias_fn(qc, kt, ps, att, probe=False):
                    cls = 4 * qc - kt
                    if probe:
                        return cls >= -3
                    if cls >= 1:
                        act(c, att[:], ps, AF.Exp)
                    else:
                        vec_tt(c, scs[:], ps, nmt[cls + 3][:], OP.add)
                        p.bar()
                        act(c, att[:], scs[:], AF.Exp)
                    return True
                attn_core(c, xnTb, ctxT, h, QT, KT, Vb, bias_fn, p)
            wob = alloc_tall(ph, nc, "wob", D, D, BF)
            borow = ph.enter_context(nc.sbuf_tensor("borow", [1, D], F32))
            borow_bf = ph.enter_context(nc.sbuf_tensor("borowb", [1, D], BF))
            cast_load(c, c.xtile, wob, wproj, D, D)
            p.dma(borow[0:1, :], bproj)
            p.bar()
            evac(c, borow_bf[0:1, :], borow[0:1, :])
            p.bar()
            out_proj(c, p, ctxT, wob, borow_bf, scr_s)

        with contextlib.ExitStack() as ph:
            Cb = alloc_tall(ph, nc, "Cb", D, D, BF)
            Sb_ = alloc_tall(ph, nc, "Sb_", D, D, BF)
            ATb = alloc_tall(ph, nc, "ATb", D, D, BF)
            BTb = alloc_tall(ph, nc, "BTb", D, D, BF)
            for i in range(8):
                p.dma(Cb.tile(i)[:], C1024[i * 128:(i + 1) * 128, :])
                p.dma(Sb_.tile(i)[:], S1024[i * 128:(i + 1) * 128, :])
            p.bar()
            with contextlib.ExitStack() as ph2:
                Wrb = alloc_tall(ph2, nc, "Wrb", D, D, BF)
                Wib = alloc_tall(ph2, nc, "Wib", D, D, BF)
                cast_load(c, c.xtile, Wrb, wfr, D, D)
                cast_load(c, c.xtile, Wib, wfi, D, D)
                for po in range(8):
                    for ch in range(2):
                        cs = slice(ch * 512, (ch + 1) * 512)
                        items = [(Wrb.tile(pi)[:, po * 128:(po + 1) * 128], Cb.tile(pi)[:, cs]) for pi in range(8)]
                        items += [(Wib.tile(pi)[:, po * 128:(po + 1) * 128], Sb_.tile(pi)[:, cs]) for pi in range(8)]
                        mm(c, c.PM[0][0:128, 0:512], items)
                        p.bar()
                        evac(c, ATb.tile(po)[:, cs], c.PM[0][0:128, 0:512])
                        p.bar()
                for i in range(8):
                    evac(c, Wib.tile(i)[:], Wib.tile(i)[:], scale=-1.0)
                p.bar()
                for po in range(8):
                    for ch in range(2):
                        cs = slice(ch * 512, (ch + 1) * 512)
                        items = [(Wrb.tile(pi)[:, po * 128:(po + 1) * 128], Sb_.tile(pi)[:, cs]) for pi in range(8)]
                        items += [(Wib.tile(pi)[:, po * 128:(po + 1) * 128], Cb.tile(pi)[:, cs]) for pi in range(8)]
                        mm(c, c.PM[0][0:128, 0:512], items)
                        p.bar()
                        evac(c, BTb.tile(po)[:, cs], c.PM[0][0:128, 0:512])
                        p.bar()
            Mb = alloc_tall(ph, nc, "Mb", D, D, BF)
            for po in range(8):
                for ch in range(2):
                    cs = slice(ch * 512, (ch + 1) * 512)
                    items = [(ATb.tile(pi)[:, po * 128:(po + 1) * 128], Cb.tile(pi)[:, cs]) for pi in range(8)]
                    items += [(BTb.tile(pi)[:, po * 128:(po + 1) * 128], Sb_.tile(pi)[:, cs]) for pi in range(8)]
                    mm(c, c.PM[0][0:128, 0:512], items)
                    p.bar()
                    evac(c, Mb.tile(po)[:, cs], c.PM[0][0:128, 0:512], scale=1.0 / 1024.0)
                    p.bar()
            fbb = ph.enter_context(nc.sbuf_tensor("fbb", [1, D], BF))
            rstage = ph.enter_context(nc.sbuf_tensor("rstg", [1, D], F32))
            p.dma(rstage[0:1, :], fbias)
            p.bar()
            evac(c, fbb[0:1, :], rstage[0:1, :])
            p.bar()
            for tt in range(NT):
                for ch in range(2):
                    cs = slice(ch * 512, (ch + 1) * 512)
                    items = [(xnTb.tile(pi)[:, tt * 128:(tt + 1) * 128], Mb.tile(pi)[:, cs]) for pi in range(8)]
                    items.append((c.ONES1B[0:1, 0:128], fbb[0:1, cs]))
                    mm(c, c.PM[ch][0:128, 0:512], items)
                p.bar()
                for ch in range(2):
                    evac(c, g.tile(tt)[:, ch * 512:(ch + 1) * 512], c.PM[ch][0:128, 0:512], func=AF.Silu)
                p.bar()

        update_phase(c, x_in, x_out, scr_s, g)
        p.finalize(sems)
    return nc


def prep_l3(inputs):
    i_ = np.arange(1024)
    ang = 2 * np.pi * np.outer(i_, i_) / 1024.0
    cls_ = np.arange(-3, 1)
    p_ = np.arange(128)
    f_ = np.arange(512)
    dmat = 128 * cls_[:, None, None] + f_[None, None, :] - p_[None, :, None]
    negm = np.where(dmat >= 0, 0.0, -1e30).astype(np.float32)
    b = np.asarray(inputs["L3_attn_b_qkv"], np.float32)
    d = {
        "w_qkv": np.asarray(inputs["L3_attn_w_qkv"], np.float32),
        "w_bqkv": b.reshape(3072, 1),
        "w_bqkvr": b.reshape(1, 3072),
        "w_proj": np.asarray(inputs["L3_attn_w_proj"], np.float32),
        "w_bproj": np.asarray(inputs["L3_attn_b_proj"], np.float32).reshape(1, D),
        "w_wfr": np.asarray(inputs["L3_ffn_w_freq_real"], np.float32),
        "w_wfi": np.asarray(inputs["L3_ffn_w_freq_imag"], np.float32),
        "w_fbias": np.asarray(inputs["L3_ffn_bias"], np.float32).reshape(1, D),
        "c_C1024": np.cos(ang).astype(BFNP),
        "c_S1024": np.sin(ang).astype(BFNP),
        "c_negm": negm,
    }
    return d


def kernel(**inputs):
    consts = host_consts()
    x = np.ascontiguousarray(np.asarray(inputs["L0_attn_Z"], np.float32))
    cur = [np.ascontiguousarray(x[b]) for b in range(B)]
    builders = [(build_l0, prep_l0), (build_l1, prep_l1), (build_l2, prep_l2), (build_l3, prep_l3)]
    for li, (bf, pf) in enumerate(builders):
        nc = bf()
        wmap = pf(inputs)
        wmap.update(consts)
        in_maps = [dict(wmap, xin=cur[b]) for b in range(B)]
        res = run_bass_kernel_spmd(nc, in_maps, core_ids=list(range(B)))
        cur = [np.ascontiguousarray(np.asarray(res.results[b]["xo"], np.float32)) for b in range(B)]
    return np.stack(cur, 0)


# version 02
# speedup vs baseline: 1.1536x; 1.1536x over previous
import math
import numpy as np
import ml_dtypes
import concourse.bass as bass
import concourse.mybir as mybir
from concourse.bass_utils import run_bass_kernel_spmd
from concourse.alu_op_type import AluOpType as OP

AF = mybir.ActivationFunctionType
AX = mybir.AxisListType
F32 = mybir.dt.float32
BF = mybir.dt.bfloat16
I32 = mybir.dt.int32
BFNP = ml_dtypes.bfloat16

B, T, D = 8, 2048, 1024
NT = T // 128


def _ap(x):
    return x if isinstance(x, bass.AP) else x[:]


_UNIQ = [0]


def _sb(nc, name, shape, dtype):
    _UNIQ[0] += 1
    return nc.sbuf_tensor(f"{name}_u{_UNIQ[0]}", shape, dtype)


class Prog:
    def __init__(self, nc, nsem=4):
        self.nc = nc
        self.nsem = nsem
        self.stages = []
        self.cur = []

    def op(self, eng, fn, inc=1):
        self.cur.append((eng, fn, inc))

    def dma(self, out, in_, eng="sync"):
        out = _ap(out); in_ = _ap(in_)
        self.op(eng, lambda e, o=out, i=in_: e.dma_start(out=o, in_=i), 16)

    def bar(self):
        if self.cur:
            self.stages.append(self.cur)
            self.cur = []

    def finalize(self, sems):
        self.bar()
        R = len(sems)
        cum = [0] * R
        tgt = []
        for s, st in enumerate(self.stages):
            k = s % R
            cum[k] += sum(i for (_, _, i) in st)
            tgt.append((k, cum[k]))

        def emit(eng_obj, name):
            for s, st in enumerate(self.stages):
                ops = [(f, i) for (e, f, i) in st if e == name]
                if not ops:
                    continue
                if s > 0:
                    k, v = tgt[s - 1]
                    eng_obj.wait_ge(sems[k], v)
                for f, i in ops:
                    inst = f(eng_obj)
                    inst.then_inc(sems[s % R], i)

        nc = self.nc
        with nc.Block() as blk:
            @blk.sync
            def _(e):
                emit(e, "sync")
                k, v = tgt[-1]
                e.wait_ge(sems[k], v)

            @blk.tensor
            def _(e):
                emit(e, "tensor")

            @blk.vector
            def _(e):
                emit(e, "vector")

            @blk.scalar
            def _(e):
                emit(e, "scalar")

            @blk.gpsimd
            def _(e):
                emit(e, "gpsimd")


class Tall:

    def __init__(self, tiles):
        self.t = tiles

    def tile(self, i):
        return self.t[i]

    def __len__(self):
        return len(self.t)


def alloc_tall(stk, nc, name, rows, cols, dtype):
    n = (rows + 127) // 128
    tiles = []
    for i in range(n):
        r = min(128, rows - 128 * i)
        h = stk.enter_context(_sb(nc, f"{name}_{i}", [r, cols], dtype))
        tiles.append(h)
    return Tall(tiles)



class Ctx:
    pass


def mk_ctx(stk, nc, p):
    c = Ctx()
    c.nc = nc
    c.p = p
    c.stk = stk
    c.PT = [stk.enter_context(nc.psum_tensor(f"pt{i}", [128, 512], F32)) for i in range(2)]
    c.PM = [stk.enter_context(nc.psum_tensor(f"pm{i}", [128, 512], F32)) for i in range(4)]
    c.PR = stk.enter_context(nc.psum_tensor("pr", [128, 512], F32))
    c.PS = stk.enter_context(nc.psum_tensor("ps", [128, 512], F32))
    c.IDENT = stk.enter_context(_sb(nc, "ident", [128, 128], F32))
    c.ONES1 = stk.enter_context(_sb(nc, "ones1", [1, 128], F32))
    c.ONES1B = stk.enter_context(_sb(nc, "ones1b", [1, 128], BF))
    c.ONESCB = stk.enter_context(_sb(nc, "onescb", [128, 1], BF))
    ident = nc.dram_tensor("c_ident", [128, 128], F32, kind="ExternalInput").ap()
    ones1 = nc.dram_tensor("c_ones1", [1, 128], F32, kind="ExternalInput").ap()
    ones1b = nc.dram_tensor("c_ones1b", [1, 128], BF, kind="ExternalInput").ap()
    onescb = nc.dram_tensor("c_onescb", [128, 1], BF, kind="ExternalInput").ap()
    p.dma(c.IDENT[:], ident)
    p.dma(c.ONES1[:], ones1)
    p.dma(c.ONES1B[:], ones1b)
    p.dma(c.ONESCB[:], onescb)
    c.CC = stk.enter_context(_sb(nc, "cc", [128, 4], F32))
    p.op("vector", lambda e: e.memset(c.CC[:, 0:1], 1e-5))
    p.bar()
    c.EPS = c.CC[:, 0:1]
    return c


def host_consts():
    return {
        "c_ident": np.eye(128, dtype=np.float32),
        "c_ones1": np.ones((1, 128), np.float32),
        "c_ones1b": np.ones((1, 128), BFNP),
        "c_onescb": np.ones((128, 1), BFNP),
    }


def mm(c, out, items, start=True, stop=True):
    out = _ap(out)
    items = [(_ap(l), _ap(r)) for (l, r) in items]
    def fn(e, out=out, items=list(items)):
        n = len(items)
        inst = None
        for i, (l, r) in enumerate(items):
            inst = e.matmul(out, l, r, start=(start and i == 0), stop=(stop and i == n - 1))
        return inst
    c.p.op("tensor", fn)


def transpose_round(c, items):
    for g0 in range(0, len(items), 8):
        grp = items[g0:g0 + 8]
        for i, it in enumerate(grp):
            src = it[0]
            a, bb = src.shape
            slot = c.PT[i // 4][0:bb, 128 * (i % 4):128 * (i % 4) + a]
            c.p.op("tensor", lambda e, s=src, o=slot, aa=a: e.transpose(o, s, c.IDENT[0:aa, 0:aa]))
        c.p.bar()
        for i, it in enumerate(grp):
            src, dst = it[0], it[1]
            func = it[2] if len(it) > 2 else AF.Copy
            scale = it[3] if len(it) > 3 else 1.0
            a, bb = src.shape
            slot = c.PT[i // 4][0:bb, 128 * (i % 4):128 * (i % 4) + a]
            c.p.op("scalar", lambda e, o=dst, s=slot, f=func, sc=scale: e.activation(o, s, f, scale=sc))
        c.p.bar()


def evac(c, dst, src, func=AF.Copy, scale=1.0, bias=...):
    dst = _ap(dst); src = _ap(src)
    c.p.op("scalar", lambda e, o=dst, s=src, f=func, sc=scale, b=bias:
           e.activation(o, s, f, scale=sc) if b is ... else e.activation(o, s, f, scale=sc, bias=b))


def vec_tt(c, out, in0, in1, op):
    out = _ap(out); in0 = _ap(in0); in1 = _ap(in1)
    c.p.op("vector", lambda e, o=out, a=in0, b=in1, oo=op: e.tensor_tensor(o, a, b, oo))


def vec_ts(c, out, in0, s1, op0, s2=None, op1=None):
    out = _ap(out); in0 = _ap(in0)
    s1 = _ap(s1) if not isinstance(s1, (int, float)) else s1
    if s2 is None:
        c.p.op("vector", lambda e, o=out, a=in0, s=s1, oo=op0: e.tensor_scalar(o, a, s, None, oo))
    else:
        c.p.op("vector", lambda e, o=out, a=in0, sa=s1, sb=s2, o0=op0, o1=op1:
               e.tensor_scalar(o, a, sa, sb, o0, o1))


def vec_stt(c, out, in0, scalar, in1, op0, op1):
    out = _ap(out); in0 = _ap(in0); in1 = _ap(in1)
    scalar = _ap(scalar) if not isinstance(scalar, (int, float)) else scalar
    c.p.op("vector", lambda e, o=out, a=in0, s=scalar, b=in1, o0=op0, o1=op1:
           e.scalar_tensor_tensor(o, a, s, b, o0, o1))


def vec_red(c, out, in_, op):
    out = _ap(out); in_ = _ap(in_)
    c.p.op("vector", lambda e, o=out, i=in_, oo=op: e.tensor_reduce(o, i, AX.X, oo))


def act(c, out, in_, func, scale=1.0, bias=...):
    out = _ap(out); in_ = _ap(in_)
    if bias is not ... and not isinstance(bias, (int, float)):
        bias = _ap(bias)
    c.p.op("scalar", lambda e, o=out, i=in_, f=func, sc=scale, b=bias:
           e.activation(o, i, f, scale=sc) if b is ... else e.activation(o, i, f, scale=sc, bias=b))


def bcast_row(c, dst, row, n=128, lhs=None):
    F = row.shape[1]
    lhsT = (lhs if lhs is not None else c.ONES1)[0:1, 0:n]
    mm(c, c.PR[0:n, 0:F], [(lhsT, row)])
    c.p.bar()
    evac(c, dst, c.PR[0:n, 0:F])
    c.p.bar()


def layernorm_tile(c, xn_out, x_in, tmp, eps=None):
    Dd = x_in.shape[1]
    vec_red(c, tmp["r1"][:, 0:1], x_in, OP.add)
    c.p.bar()
    c.p.op("scalar", lambda e, o=tmp["r2"][:, 0:1], i=tmp["r1"][:, 0:1]: e.mul(o, i, -1.0 / Dd))
    c.p.bar()
    vec_ts(c, tmp["a"], x_in, tmp["r2"][:, 0:1], OP.add)
    c.p.bar()
    act(c, tmp["b"], tmp["a"], AF.Square)
    c.p.bar()
    vec_red(c, tmp["r1"][:, 0:1], tmp["b"], OP.add)
    c.p.bar()
    act(c, tmp["r2"][:, 0:1], tmp["r1"][:, 0:1], AF.Sqrt, scale=1.0 / Dd, bias=c.EPS)
    c.p.bar()
    c.p.op("vector", lambda e, o=tmp["r1"][:, 0:1], i=tmp["r2"][:, 0:1]: e.reciprocal(o, i))
    c.p.bar()
    vec_ts(c, xn_out, tmp["a"], tmp["r1"][:, 0:1], OP.mult)
    c.p.bar()


def fp4_quant(c, q_out_f32, x_in, tmp):
    F = x_in.shape[1]
    nb = F // 16
    absx = tmp["a"]
    act(c, absx[:, 0:F], x_in, AF.Abs)
    c.p.bar()
    a3 = absx[:, 0:F].rearrange("p (b k) -> p b k", k=16)
    vec_red(c, tmp["s"][:, 0:nb], a3, OP.max)
    c.p.bar()
    vec_ts(c, tmp["s2"][:, 0:nb], tmp["s"][:, 0:nb], 1.0 / 6.0, OP.mult, 1e-12, OP.max)
    c.p.bar()
    c.p.op("vector", lambda e, o=tmp["s"][:, 0:nb], i=tmp["s2"][:, 0:nb]: e.reciprocal(o, i))
    c.p.bar()
    recb = tmp["s"][:, 0:nb].rearrange("p (b o) -> p b o", o=1).broadcast_to((128, nb, 16))
    scb = tmp["s2"][:, 0:nb].rearrange("p (b o) -> p b o", o=1).broadcast_to((128, nb, 16))
    tq = tmp["b"]
    vec_tt(c, tq[:, 0:F].rearrange("p (b k) -> p b k", k=16), a3, recb, OP.mult)
    c.p.bar()
    accs = [tmp["c"], tmp["d"]]
    vec_ts(c, accs[0][:, 0:F], tq[:, 0:F], 0.25, OP.is_gt, 0.5, OP.mult)
    c.p.bar()
    cur = 0
    for mid, w in [(0.75, 0.5), (1.25, 0.5), (1.75, 0.5), (2.5, 1.0), (3.5, 1.0), (5.0, 2.0)]:
        vec_ts(c, tmp["e"][:, 0:F], tq[:, 0:F], mid, OP.is_gt, w, OP.mult)
        c.p.bar()
        vec_tt(c, accs[1 - cur][:, 0:F], accs[cur][:, 0:F], tmp["e"][:, 0:F], OP.add)
        c.p.bar()
        cur = 1 - cur
    vec_tt(c, tmp["e"][:, 0:F].rearrange("p (b k) -> p b k", k=16),
           accs[cur][:, 0:F].rearrange("p (b k) -> p b k", k=16), scb, OP.mult)
    c.p.op("scalar", lambda e, o=tmp["b"][:, 0:F], i=x_in: e.sign(o, i))
    c.p.bar()
    vec_tt(c, q_out_f32, tmp["e"][:, 0:F], tmp["b"][:, 0:F], OP.mult)
    c.p.bar()


def mk_tmp(stk, nc, name, F=1024):
    t = {}
    for k in ["a", "b", "c", "d", "e"]:
        t[k] = stk.enter_context(_sb(nc, f"{name}_{k}", [128, F], F32))
    t["s"] = stk.enter_context(_sb(nc, f"{name}_s", [128, 256], F32))
    t["s2"] = stk.enter_context(_sb(nc, f"{name}_s2", [128, 256], F32))
    t["r1"] = stk.enter_context(_sb(nc, f"{name}_r1", [128, 8], F32))
    t["r2"] = stk.enter_context(_sb(nc, f"{name}_r2", [128, 8], F32))
    return t




import contextlib


def phase1(c, x_in, tmp, xnTb=None, l0=None, l2=None):
    nc, p = c.nc, c.p
    for tt in range(NT):
        p.dma(c.xtile, x_in[tt * 128:(tt + 1) * 128, :])
        p.bar()
        layernorm_tile(c, c.xn[:], c.xtile[:], tmp)
        items = []
        if xnTb is not None:
            for k in range(8):
                items.append((c.xn[:, 128 * k:128 * (k + 1)],
                              [(xnTb.tile(k)[:, tt * 128:(tt + 1) * 128], AF.Copy, 1.0)]))
        if l2 is not None:
            for k in range(8):
                items.append((c.xn[:, 128 * k:128 * (k + 1)],
                              [(l2["strip"].tile(k)[:], AF.Copy, 1.0)]))
        if l0 is not None:
            p.dma(l0["scr_xn"][tt * 128:(tt + 1) * 128, :], c.xn[:])
        transpose_round2(c, items)
        if l0 is not None:
            fp4_quant(c, c.xq[:], c.xn[:], tmp)
            qitems = [(c.xq[:, 128 * k:128 * (k + 1)],
                       [(l0["xqT"].tile(k)[:, tt * 128:(tt + 1) * 128], AF.Copy, 1.0)])
                      for k in range(8)]
            transpose_round2(c, qitems)
        if l2 is not None:
            mm(c, c.PS[0:128, 0:8],
               [(l2["strip"].tile(pi)[:], l2["gwT"].tile(pi)[:]) for pi in range(8)])
            p.bar()
            evac(c, l2["lg"][:, 0:8], c.PS[0:128, 0:8])
            p.bar()
            lg = l2["lg"]
            r1, r2 = tmp["r1"], tmp["r2"]
            vec_red(c, r1[:, 0:1], lg[:, 0:8], OP.max)
            p.bar()
            vec_ts(c, l2["msk1"][:, 0:8], lg[:, 0:8], r1[:, 0:1], OP.is_ge)
            p.bar()
            vec_stt(c, l2["l2nd"][:, 0:8], l2["msk1"][:, 0:8], -1e30, lg[:, 0:8], OP.mult, OP.add)
            p.bar()
            vec_red(c, r2[:, 0:1], l2["l2nd"][:, 0:8], OP.max)
            p.bar()
            vec_ts(c, l2["msk2"][:, 0:8], l2["l2nd"][:, 0:8], r2[:, 0:1], OP.is_ge)
            vec_tt(c, r2[:, 1:2], r2[:, 0:1], r1[:, 0:1], OP.subtract)
            p.bar()
            act(c, r2[:, 2:3], r2[:, 1:2], AF.Exp)
            p.bar()
            c.p.op("scalar", lambda e, o=r2[:, 3:4], i=r2[:, 2:3]: e.add(o, i, 1.0))
            p.bar()
            c.p.op("vector", lambda e, o=r1[:, 1:2], i=r2[:, 3:4]: e.reciprocal(o, i))
            p.bar()
            vec_ts(c, l2["msk2"][:, 0:8], l2["msk2"][:, 0:8], r2[:, 2:3], OP.mult)
            p.bar()
            vec_tt(c, l2["msk1"][:, 0:8], l2["msk1"][:, 0:8], l2["msk2"][:, 0:8], OP.add)
            p.bar()
            vec_ts(c, l2["comb"][tt][:, 0:8], l2["msk1"][:, 0:8], r1[:, 1:2], OP.mult)
            p.bar()
        p.bar()


def transpose_round2(c, items):
    for g0 in range(0, len(items), 8):
        grp = items[g0:g0 + 8]
        for i, (src, dsts) in enumerate(grp):
            a, bb = src.shape
            slot = c.PT[i // 4][0:bb, 128 * (i % 4):128 * (i % 4) + a]
            c.p.op("tensor", lambda e, s=src, o=slot, aa=a: e.transpose(o, s, c.IDENT[0:aa, 0:aa]))
        c.p.bar()
        for i, (src, dsts) in enumerate(grp):
            a, bb = src.shape
            slot = c.PT[i // 4][0:bb, 128 * (i % 4):128 * (i % 4) + a]
            for (dst, func, scale) in dsts:
                c.p.op("scalar", lambda e, o=_ap(dst), ss=slot, f=func, sc=scale: e.activation(o, ss, f, scale=sc))
        c.p.bar()


def update_phase(c, x_in, x_out, scr_s, g, dbg=None):
    p = c.p
    for tt in range(NT):
        sl = slice(tt * 128, (tt + 1) * 128)
        p.dma(c.xtile, x_in[sl, :])
        p.dma(c.xn, scr_s[sl, :])
        p.bar()
        if dbg is not None:
            p.dma(dbg["o_s"][sl, :], c.xn[:])
            evac(c, c.xq[:], g.tile(tt)[:])
            p.bar()
            p.dma(dbg["o_g"][sl, :], c.xq[:])
            p.bar()
        vec_tt(c, c.xq[:], c.xn[:], g.tile(tt)[:], OP.mult)
        p.bar()
        vec_tt(c, c.xn[:], c.xtile[:], c.xq[:], OP.add)
        p.bar()
        p.dma(x_out[sl, :], c.xn[:])
        p.bar()


def cast_load(c, stage, dst_tall, src_dram, rows, cols, scale=1.0, col0=0, row0=0):
    n = (rows + 127) // 128
    for i in range(n):
        r = min(128, rows - 128 * i)
        c.p.dma(stage[0:r, 0:cols], src_dram[row0 + 128 * i:row0 + 128 * i + r, col0:col0 + cols])
        c.p.bar()
        evac(c, dst_tall.tile(i)[0:r, 0:cols], stage[0:r, 0:cols], scale=scale)
        c.p.bar()


def build_l0(dbg=False, fx=None):
    if fx is None:
        nc = bass.Bass("TRN2", target_bir_lowering=False)
        x_in = nc.dram_tensor("xin", [T, D], F32, kind="ExternalInput").ap()
        x_out = nc.dram_tensor("xo", [T, D], F32, kind="ExternalOutput").ap()
    else:
        nc, _c, _p, x_in, x_out = fx
    dec = nc.dram_tensor("w_dec", [1, 1], F32, kind="ExternalInput").ap()
    WT = nc.dram_tensor("w_WT", [D, D], F32, kind="ExternalInput").ap()
    bb = nc.dram_tensor("w_b", [1, D], F32, kind="ExternalInput").ap()
    gam = nc.dram_tensor("w_gam", [1, D], F32, kind="ExternalInput").ap()
    bet = nc.dram_tensor("w_bet", [1, D], F32, kind="ExternalInput").ap()
    wg = nc.dram_tensor("w_wg", [2816, D], F32, kind="ExternalInput").ap()
    wu = nc.dram_tensor("w_wu", [2816, D], F32, kind="ExternalInput").ap()
    wd = nc.dram_tensor("w_wd", [D, 2816], F32, kind="ExternalInput").ap()
    scr_xn = nc.dram_tensor("l0scr_xn", [T, D], F32, kind="Internal").ap()
    scr_s = nc.dram_tensor("l0scr_s", [T, D], F32, kind="Internal").ap()
    scr_hq = nc.dram_tensor("l0scr_hq", [2816, T], BF, kind="Internal").ap()
    dbgd = None
    if dbg:
        dbgd = {"o_s": nc.dram_tensor("o_s", [T, D], F32, kind="ExternalOutput").ap(),
                "o_g": nc.dram_tensor("o_g", [T, D], F32, kind="ExternalOutput").ap()}

    with contextlib.ExitStack() as stk:
        if fx is None:
            p = Prog(nc)
            c = mk_ctx(stk, nc, p)
            sems = [stk.enter_context(nc.semaphore(f"sem{i}")) for i in range(4)]
        else:
            c, p = _c, _p
        c.xtile = stk.enter_context(_sb(nc, "xtile", [128, D], F32))
        c.xn = stk.enter_context(_sb(nc, "xnb", [128, D], F32))
        c.xq = stk.enter_context(_sb(nc, "xqb", [128, D], F32))
        c.hqtb = stk.enter_context(_sb(nc, "hqtb", [128, 512], BF))
        tmp = mk_tmp(stk, nc, "tmp", D)
        xqT = alloc_tall(stk, nc, "xqT", D, T, BF)
        g = alloc_tall(stk, nc, "gacc", T, D, BF)

        phase1(c, x_in, tmp, l0={"scr_xn": scr_xn, "xqT": xqT})

        with contextlib.ExitStack() as ph:
            Sb = alloc_tall(ph, nc, "Sb", D, T, BF)
            afull = ph.enter_context(_sb(nc, "afull", [128, 512], F32))
            states = ph.enter_context(_sb(nc, "states", [128, 8], F32))
            stage = ph.enter_context(_sb(nc, "stg", [128, 512], F32))
            stage2 = ph.enter_context(_sb(nc, "stg2", [128, 512], F32))
            dct = ph.enter_context(_sb(nc, "dct", [1, 4], F32))
            acol = ph.enter_context(_sb(nc, "acol", [128, 1], F32))
            xns = [ph.enter_context(_sb(nc, f"xns{j}", [128, D], F32)) for j in range(4)]
            p.dma(dct[0:1, 0:1], dec)
            p.op("vector", lambda e: e.memset(afull[:], 1.0))
            p.op("vector", lambda e: e.memset(states[:], 0.0))
            p.bar()
            act(c, dct[0:1, 1:2], dct[0:1, 0:1], AF.Sigmoid)
            p.bar()
            mm(c, c.PR[0:128, 0:1], [(c.ONES1[0:1, 0:128], dct[0:1, 1:2])])
            p.bar()
            evac(c, acol[:, 0:1], c.PR[0:128, 0:1])
            p.bar()
            vec_ts(c, afull[:], afull[:], acol[:, 0:1], OP.mult)
            p.bar()
            for tc in range(4):
                for j in range(4):
                    p.dma(xns[j][:], scr_xn[(4 * tc + j) * 128:(4 * tc + j + 1) * 128, :])
                p.bar()
                for ft in range(8):
                    items = [(xns[j][:, 128 * ft:128 * (ft + 1)],
                              [(stage[:, 128 * j:128 * (j + 1)], AF.Copy, 1.0)]) for j in range(4)]
                    transpose_round2(c, items)
                    p.op("vector", lambda e, o=stage2[:], d0=afull[:], d1=stage[:], ini=states[:, ft:ft + 1]:
                         e.tensor_tensor_scan(o, d0, d1, ini, OP.mult, OP.add))
                    p.bar()
                    evac(c, Sb.tile(ft)[:, tc * 512:(tc + 1) * 512], stage2[:])
                    c.p.op("vector", lambda e, o=states[:, ft:ft + 1], i=stage2[:, 511:512]: e.tensor_copy(o, i))
                    p.bar()
            WTb = alloc_tall(ph, nc, "WTb", D, D, BF)
            brow = ph.enter_context(_sb(nc, "brow", [1, D], BF))
            Grep = ph.enter_context(_sb(nc, "grep", [128, D], F32))
            Brep = ph.enter_context(_sb(nc, "brep", [128, D], F32))
            rstage = ph.enter_context(_sb(nc, "rstage", [1, D], F32))
            cast_load(c, c.xtile, WTb, WT, D, D)
            p.dma(rstage[0:1, :], bb)
            p.bar()
            evac(c, brow[0:1, :], rstage[0:1, :])
            p.bar()
            p.dma(rstage[0:1, :], gam)
            p.bar()
            bcast_row(c, Grep[:, 0:512], rstage[0:1, 0:512])
            bcast_row(c, Grep[:, 512:1024], rstage[0:1, 512:1024])
            p.dma(rstage[0:1, :], bet)
            p.bar()
            bcast_row(c, Brep[:, 0:512], rstage[0:1, 0:512])
            bcast_row(c, Brep[:, 512:1024], rstage[0:1, 512:1024])
            for tt in range(NT):
                for ch in range(2):
                    items = [(Sb.tile(pi)[:, tt * 128:(tt + 1) * 128],
                              WTb.tile(pi)[:, ch * 512:(ch + 1) * 512]) for pi in range(8)]
                    items.append((c.ONES1B[0:1, 0:128], brow[0:1, ch * 512:(ch + 1) * 512]))
                    mm(c, c.PM[ch][0:128, 0:512], items)
                p.bar()
                for ch in range(2):
                    evac(c, c.xtile[:, ch * 512:(ch + 1) * 512], c.PM[ch][0:128, 0:512])
                p.bar()
                layernorm_tile(c, c.xn[:], c.xtile[:], tmp)
                vec_tt(c, c.xn[:], c.xn[:], Grep[:], OP.mult)
                p.bar()
                vec_tt(c, c.xn[:], c.xn[:], Brep[:], OP.add)
                p.bar()
                p.dma(scr_s[tt * 128:(tt + 1) * 128, :], c.xn[:])
                p.bar()

        NCH = [(0, 512), (512, 1024), (1024, 1536), (1536, 2048), (2048, 2560), (2560, 2816)]
        with contextlib.ExitStack() as ph:
            wgqT = alloc_tall(ph, nc, "wgqT", D, 2816, BF)
            wuqT = alloc_tall(ph, nc, "wuqT", D, 2816, BF)
            hqf = ph.enter_context(_sb(nc, "hqf", [128, 512], F32))
            tmq = mk_tmp(ph, nc, "tmq", 512)
            for (wdram, wdst) in [(wg, wgqT), (wu, wuqT)]:
                for i in range(22):
                    p.dma(c.xtile, wdram[i * 128:(i + 1) * 128, :])
                    p.bar()
                    fp4_quant(c, c.xq[:], c.xtile[:], tmp)
                    items = [(c.xq[:, 128 * k:128 * (k + 1)],
                              [(wdst.tile(k)[:, 128 * i:128 * (i + 1)], AF.Copy, 1.0)]) for k in range(8)]
                    transpose_round2(c, items)
            for tt in range(NT):
                for (n0, n1) in NCH:
                    w = n1 - n0
                    mm(c, c.PM[0][0:128, 0:w],
                       [(xqT.tile(pi)[:, tt * 128:(tt + 1) * 128], wgqT.tile(pi)[:, n0:n1]) for pi in range(8)])
                    mm(c, c.PM[1][0:128, 0:w],
                       [(xqT.tile(pi)[:, tt * 128:(tt + 1) * 128], wuqT.tile(pi)[:, n0:n1]) for pi in range(8)])
                    p.bar()
                    act(c, tmq["a"][:, 0:w], c.PM[0][0:128, 0:w], AF.Silu)
                    p.bar()
                    vec_tt(c, c.xq[:, 0:w], tmq["a"][:, 0:w], c.PM[1][0:128, 0:w], OP.mult)
                    p.bar()
                    fp4_quant(c, hqf[:, 0:w], c.xq[:, 0:w], tmq)
                    nb = w // 128
                    items = [(hqf[:, 128 * k:128 * (k + 1)],
                              [(c.hqtb[:, 128 * k:128 * (k + 1)], AF.Copy, 1.0)]) for k in range(nb)]
                    transpose_round2(c, items)
                    for k in range(nb):
                        p.dma(scr_hq[n0 + 128 * k:n0 + 128 * (k + 1), tt * 128:(tt + 1) * 128],
                              c.hqtb[:, 128 * k:128 * (k + 1)])
                    p.bar()
        with contextlib.ExitStack() as ph:
            wdqT = alloc_tall(ph, nc, "wdqT", 2816, D, BF)
            hcol = alloc_tall(ph, nc, "hcol", 2816, 128, BF)
            qstage = ph.enter_context(_sb(nc, "qstage", [128, 2816], F32))
            hqf = ph.enter_context(_sb(nc, "hqf2", [128, 512], F32))
            tmq = mk_tmp(ph, nc, "tmq2", 512)
            for i in range(8):
                p.dma(qstage[:], wd[i * 128:(i + 1) * 128, :])
                p.bar()
                for (n0, n1) in NCH:
                    w = n1 - n0
                    fp4_quant(c, hqf[:, 0:w], qstage[:, n0:n1], tmq)
                    nb = w // 128
                    items = [(hqf[:, 128 * k:128 * (k + 1)],
                              [(wdqT.tile((n0 + 128 * k) // 128)[:, 128 * i:128 * (i + 1)], AF.Copy, 1.0)])
                             for k in range(nb)]
                    transpose_round2(c, items)
            for tt in range(NT):
                for pi in range(22):
                    p.dma(hcol.tile(pi)[:], scr_hq[pi * 128:(pi + 1) * 128, tt * 128:(tt + 1) * 128])
                p.bar()
                for ch in range(2):
                    mm(c, c.PM[ch][0:128, 0:512],
                       [(hcol.tile(pi)[:], wdqT.tile(pi)[:, ch * 512:(ch + 1) * 512]) for pi in range(22)])
                p.bar()
                for ch in range(2):
                    evac(c, g.tile(tt)[:, ch * 512:(ch + 1) * 512], c.PM[ch][0:128, 0:512], func=AF.Silu)
                p.bar()

        update_phase(c, x_in, x_out, scr_s, g, dbg=dbgd)
        if fx is None:
            p.finalize(sems)
    return nc


def prep_l0(inputs):
    d = {
        "w_dec": np.asarray(inputs["L0_attn_decay"], np.float32).reshape(1, 1),
        "w_WT": np.ascontiguousarray(np.asarray(inputs["L0_attn_W"], np.float32).T),
        "w_b": np.asarray(inputs["L0_attn_b"], np.float32).reshape(1, D),
        "w_gam": np.asarray(inputs["L0_attn_gamma"], np.float32).reshape(1, D),
        "w_bet": np.asarray(inputs["L0_attn_beta"], np.float32).reshape(1, D),
        "w_wg": np.asarray(inputs["L0_ffn_w_gate"], np.float32),
        "w_wu": np.asarray(inputs["L0_ffn_w_up"], np.float32),
        "w_wd": np.asarray(inputs["L0_ffn_w_down"], np.float32),
    }
    return d


def build_l1(parts=("z", "conv", "proj", "ffn", "upd"), fx=None):
    if fx is None:
        nc = bass.Bass("TRN2", target_bir_lowering=False)
        x_in = nc.dram_tensor("xin", [T, D], F32, kind="ExternalInput").ap()
        x_out = nc.dram_tensor("xo", [T, D], F32, kind="ExternalOutput").ap()
    else:
        nc, _c, _p, x_in, x_out = fx
    w_inT = nc.dram_tensor("w_inT", [D, 2048], F32, kind="ExternalInput").ap()
    b_in = nc.dram_tensor("w_binr", [1, 2048], F32, kind="ExternalInput").ap()
    waT = nc.dram_tensor("w_waT", [256, D], F32, kind="ExternalInput").ap()
    wbT = nc.dram_tensor("w_wbT", [256, D], F32, kind="ExternalInput").ap()
    bab = nc.dram_tensor("w_bab", [1, 2048], F32, kind="ExternalInput").ap()
    w_outT = nc.dram_tensor("w_outT", [2048, D], F32, kind="ExternalInput").ap()
    b_out = nc.dram_tensor("w_bout", [1, D], F32, kind="ExternalInput").ap()
    wgT = nc.dram_tensor("w_wgT", [D, 3072], I32, kind="ExternalInput").ap()
    wuT = nc.dram_tensor("w_wuT", [D, 3072], I32, kind="ExternalInput").ap()
    wdT = nc.dram_tensor("w_wdT", [3072, D], I32, kind="ExternalInput").ap()
    C512 = nc.dram_tensor("c_C512", [512, 512], BF, kind="ExternalInput").ap()
    S512 = nc.dram_tensor("c_S512", [512, 512], BF, kind="ExternalInput").ap()
    CRD = nc.dram_tensor("c_CR", [256, 512], BF, kind="ExternalInput").ap()
    SRD = nc.dram_tensor("c_SR", [256, 512], BF, kind="ExternalInput").ap()
    ICD = nc.dram_tensor("c_IC", [512, 256], BF, kind="ExternalInput").ap()
    ISD = nc.dram_tensor("c_IS", [512, 256], BF, kind="ExternalInput").ap()
    scr_z = nc.dram_tensor("l1scr_z", [T, 2048], BF, kind="Internal").ap()
    scr_cat = nc.dram_tensor("l1scr_cat", [2048, T], BF, kind="Internal").ap()
    scr_s = nc.dram_tensor("l1scr_s", [T, D], F32, kind="Internal").ap()

    with contextlib.ExitStack() as stk:
        if fx is None:
            p = Prog(nc)
            c = mk_ctx(stk, nc, p)
            sems = [stk.enter_context(nc.semaphore(f"sem{i}")) for i in range(4)]
        else:
            c, p = _c, _p
        c.xtile = stk.enter_context(_sb(nc, "xtile", [128, D], F32))
        c.xn = stk.enter_context(_sb(nc, "xnb", [128, D], F32))
        c.xq = stk.enter_context(_sb(nc, "xqb", [128, D], F32))
        tmp = mk_tmp(stk, nc, "tmp", D)
        xnTb = alloc_tall(stk, nc, "xnTb", D, T, BF)

        phase1(c, x_in, tmp, xnTb=xnTb)

        if "z" in parts:
         with contextlib.ExitStack() as ph:
            w_inTb = alloc_tall(ph, nc, "winTb", D, 2048, BF)
            binb = ph.enter_context(_sb(nc, "binb", [1, 2048], BF))
            rstage = ph.enter_context(_sb(nc, "rstg", [1, 2048], F32))
            zbuf = ph.enter_context(_sb(nc, "zbuf", [128, 512], BF))
            for i in range(8):
                p.dma(c.xtile[:, 0:D], w_inT[i * 128:(i + 1) * 128, 0:1024])
                p.bar()
                evac(c, w_inTb.tile(i)[:, 0:1024], c.xtile[:, 0:D])
                p.bar()
                p.dma(c.xtile[:, 0:D], w_inT[i * 128:(i + 1) * 128, 1024:2048])
                p.bar()
                evac(c, w_inTb.tile(i)[:, 1024:2048], c.xtile[:, 0:D])
                p.bar()
            p.dma(rstage[0:1, :], b_in)
            p.bar()
            evac(c, binb[0:1, :], rstage[0:1, :])
            p.bar()
            for tt in range(NT):
                for ec in range(4):
                    items = [(xnTb.tile(pi)[:, tt * 128:(tt + 1) * 128],
                              w_inTb.tile(pi)[:, ec * 512:(ec + 1) * 512]) for pi in range(8)]
                    items.append((c.ONES1B[0:1, 0:128], binb[0:1, ec * 512:(ec + 1) * 512]))
                    mm(c, c.PM[0][0:128, 0:512], items)
                    p.bar()
                    evac(c, zbuf[:], c.PM[0][0:128, 0:512])
                    p.bar()
                    p.dma(scr_z[tt * 128:(tt + 1) * 128, ec * 512:(ec + 1) * 512], zbuf[:])
                    p.bar()

        if "conv" in parts:
         with contextlib.ExitStack() as ph:
            C512b = alloc_tall(ph, nc, "C512b", 512, 512, BF)
            S512b = alloc_tall(ph, nc, "S512b", 512, 512, BF)
            CRb = alloc_tall(ph, nc, "CRb", 256, 512, BF)
            SRb = alloc_tall(ph, nc, "SRb", 256, 512, BF)
            ICb = alloc_tall(ph, nc, "ICb", 512, 256, BF)
            ISb = alloc_tall(ph, nc, "ISb", 512, 256, BF)
            for tl, dr in [(C512b, C512), (S512b, S512)]:
                for i in range(4):
                    p.dma(tl.tile(i)[:], dr[i * 128:(i + 1) * 128, :])
            for tl, dr in [(CRb, CRD), (SRb, SRD)]:
                for i in range(2):
                    p.dma(tl.tile(i)[:], dr[i * 128:(i + 1) * 128, :])
            for tl, dr in [(ICb, ICD), (ISb, ISD)]:
                for i in range(4):
                    p.dma(tl.tile(i)[:], dr[i * 128:(i + 1) * 128, :])
            p.bar()
            Wc = alloc_tall(ph, nc, "Wcb", 512, 2048, BF)
            Ws = alloc_tall(ph, nc, "Wsb", 512, 2048, BF)
            watb = alloc_tall(ph, nc, "watb", 256, D, BF)
            wbtb = alloc_tall(ph, nc, "wbtb", 256, D, BF)
            for tl, dr in [(watb, waT), (wbtb, wbT)]:
                for i in range(2):
                    p.dma(c.xtile, dr[i * 128:(i + 1) * 128, :])
                    p.bar()
                    evac(c, tl.tile(i)[:], c.xtile[:])
                    p.bar()
            for (dst, lt) in [(Wc, CRb), (Ws, SRb)]:
                for po in range(4):
                    for m, wsb in [(0, watb), (1, wbtb)]:
                        for cch in range(2):
                            items = [(lt.tile(pi)[:, po * 128:(po + 1) * 128],
                                      wsb.tile(pi)[:, cch * 512:(cch + 1) * 512]) for pi in range(2)]
                            mm(c, c.PM[0][0:128, 0:512], items)
                            p.bar()
                            evac(c, dst.tile(po)[:, m * 1024 + cch * 512:m * 1024 + (cch + 1) * 512],
                                 c.PM[0][0:128, 0:512])
                            p.bar()
            babb = ph.enter_context(_sb(nc, "babb", [1, 2048], BF))
            rstage = ph.enter_context(_sb(nc, "rstg2", [1, 2048], F32))
            p.dma(rstage[0:1, :], bab)
            p.bar()
            evac(c, babb[0:1, :], rstage[0:1, :])
            p.bar()
            win = [ph.enter_context(_sb(nc, f"win{j}", [128, 2048], BF)) for j in range(4)]
            zpad = ph.enter_context(_sb(nc, "zpad", [128, 2048], BF))
            p.op("vector", lambda e: e.memset(zpad[:], 0.0))
            p.bar()
            a_bf = alloc_tall(ph, nc, "abf", 512, 512, BF)
            b_bf = alloc_tall(ph, nc, "bbf", 512, 512, BF)
            conv_all = alloc_tall(ph, nc, "conv", 256, 2048, F32)
            gab = ph.enter_context(_sb(nc, "gab", [128, 1024], F32))
            gbb = ph.enter_context(_sb(nc, "gbb", [128, 1024], F32))
            sgb = ph.enter_context(_sb(nc, "sgb", [128, 1024], F32))
            ctst = ph.enter_context(_sb(nc, "ctst", [128, 1024], BF))
            VSP = [c.PT[0], c.PT[1], c.PR, c.PS]
            for tau in range(8):
                for si in range(4):
                    zt = 2 * tau - 2 + si
                    if zt < 0:
                        pass
                    else:
                        p.dma(win[si][:], scr_z[zt * 128:(zt + 1) * 128, :])
                p.bar()
                wins = [zpad if (2 * tau - 2 + si) < 0 else win[si] for si in range(4)]
                for cc in range(4):
                    cs = slice(cc * 512, (cc + 1) * 512)
                    for po in range(4):
                        mm(c, c.PM[po][0:128, 0:512],
                           [(C512b.tile(pi)[:, po * 128:(po + 1) * 128], wins[pi][:, cs]) for pi in range(4)])
                        mm(c, VSP[po][0:128, 0:512],
                           [(S512b.tile(pi)[:, po * 128:(po + 1) * 128], wins[pi][:, cs]) for pi in range(4)])
                    p.bar()
                    for po in range(4):
                        vec_stt(c, tmp["a"][:, 0:512], c.PM[po][0:128, 0:512], 1.0, Wc.tile(po)[:, cs], OP.mult, OP.mult)
                        p.bar()
                        vec_stt(c, tmp["b"][:, 0:512], VSP[po][0:128, 0:512], 1.0, Ws.tile(po)[:, cs], OP.mult, OP.mult)
                        p.bar()
                        vec_tt(c, a_bf.tile(po)[:], tmp["a"][:, 0:512], tmp["b"][:, 0:512], OP.subtract)
                        vec_stt(c, tmp["c"][:, 0:512], c.PM[po][0:128, 0:512], 1.0, Ws.tile(po)[:, cs], OP.mult, OP.mult)
                        p.bar()
                        vec_stt(c, tmp["d"][:, 0:512], VSP[po][0:128, 0:512], 1.0, Wc.tile(po)[:, cs], OP.mult, OP.mult)
                        p.bar()
                        vec_tt(c, b_bf.tile(po)[:], tmp["c"][:, 0:512], tmp["d"][:, 0:512], OP.add)
                        p.bar()
                    for pp in range(2):
                        items = [(ICb.tile(pi)[:, pp * 128:(pp + 1) * 128], a_bf.tile(pi)[:]) for pi in range(4)]
                        items += [(ISb.tile(pi)[:, pp * 128:(pp + 1) * 128], b_bf.tile(pi)[:]) for pi in range(4)]
                        items.append((c.ONES1B[0:1, 0:128], babb[0:1, cs]))
                        mm(c, c.PM[pp][0:128, 0:512], items)
                    p.bar()
                    for pp in range(2):
                        evac(c, conv_all.tile(pp)[:, cs], c.PM[pp][0:128, 0:512])
                    p.bar()
                for pt in range(2):
                    act(c, sgb[:], conv_all.tile(pt)[:, 1024:2048], AF.Sigmoid)
                    p.bar()
                    vec_tt(c, gab[:], conv_all.tile(pt)[:, 0:1024], sgb[:], OP.mult)
                    act(c, tmp["a"][:, 0:1024], conv_all.tile(pt)[:, 0:1024], AF.Sigmoid)
                    p.bar()
                    vec_tt(c, gbb[:], conv_all.tile(pt)[:, 1024:2048], tmp["a"][:, 0:1024], OP.mult)
                    p.bar()
                    tcol = tau * 256 + pt * 128
                    items = [(gab[:, cb * 128:(cb + 1) * 128],
                              [(ctst[:, cb * 128:(cb + 1) * 128], AF.Copy, 1.0)]) for cb in range(8)]
                    transpose_round2(c, items)
                    for cb in range(8):
                        p.dma(scr_cat[cb * 128:(cb + 1) * 128, tcol:tcol + 128], ctst[:, cb * 128:(cb + 1) * 128])
                    p.bar()
                    items = [(gbb[:, cb * 128:(cb + 1) * 128],
                              [(ctst[:, cb * 128:(cb + 1) * 128], AF.Copy, 1.0)]) for cb in range(8)]
                    transpose_round2(c, items)
                    for cb in range(8):
                        p.dma(scr_cat[1024 + cb * 128:1024 + (cb + 1) * 128, tcol:tcol + 128],
                              ctst[:, cb * 128:(cb + 1) * 128])
                    p.bar()

        if "proj" in parts:
         with contextlib.ExitStack() as ph:
            w_outTb = alloc_tall(ph, nc, "woutTb", 2048, D, BF)
            boutb = ph.enter_context(_sb(nc, "boutb", [1, D], BF))
            rstage = ph.enter_context(_sb(nc, "rstg3", [1, D], F32))
            catcol = alloc_tall(ph, nc, "catcol", 2048, 128, BF)
            cast_load(c, c.xtile, w_outTb, w_outT, 2048, D)
            p.dma(rstage[0:1, :], b_out)
            p.bar()
            evac(c, boutb[0:1, :], rstage[0:1, :])
            p.bar()
            for tt in range(NT):
                for pi in range(16):
                    p.dma(catcol.tile(pi)[:], scr_cat[pi * 128:(pi + 1) * 128, tt * 128:(tt + 1) * 128])
                p.bar()
                for ch in range(2):
                    items = [(catcol.tile(pi)[:], w_outTb.tile(pi)[:, ch * 512:(ch + 1) * 512])
                             for pi in range(16)]
                    items.append((c.ONES1B[0:1, 0:128], boutb[0:1, ch * 512:(ch + 1) * 512]))
                    mm(c, c.PM[ch][0:128, 0:512], items)
                p.bar()
                for ch in range(2):
                    evac(c, c.xn[:, ch * 512:(ch + 1) * 512], c.PM[ch][0:128, 0:512])
                p.bar()
                p.dma(scr_s[tt * 128:(tt + 1) * 128, :], c.xn[:])
                p.bar()

        g = alloc_tall(stk, nc, "gacc", T, D, BF)
        if "ffn" in parts:
         with contextlib.ExitStack() as ph:
            istage = ph.enter_context(_sb(nc, "istage", [128, 1024], I32))
            yTh = alloc_tall(ph, nc, "yTh", D, 1024, F32)
            wgb = alloc_tall(ph, nc, "wgb", D, 512, BF)
            wub = alloc_tall(ph, nc, "wub", D, 512, BF)
            wdb = alloc_tall(ph, nc, "wdb", 512, D, BF)
            hTb = alloc_tall(ph, nc, "hTb", 512, 1024, BF)

            def i32load(dst_ap, src_ap, wcols):
                p.dma(istage[0:128, 0:wcols], src_ap)
                p.bar()
                c.p.op("vector", lambda e, o=c.xtile[:, 0:wcols], i=istage[0:128, 0:wcols]: e.tensor_copy(o, i))
                p.bar()
                evac(c, dst_ap, c.xtile[:, 0:wcols], scale=1.0 / 127.0)
                p.bar()

            for half in range(2):
                for ho in range(8):
                    p.op("vector", lambda e, o=yTh.tile(ho)[:]: e.memset(o, 0.0))
                p.bar()
                for chh in range(6):
                    for i in range(8):
                        i32load(wgb.tile(i)[:], wgT[i * 128:(i + 1) * 128, chh * 512:(chh + 1) * 512], 512)
                        i32load(wub.tile(i)[:], wuT[i * 128:(i + 1) * 128, chh * 512:(chh + 1) * 512], 512)
                    for i in range(4):
                        i32load(wdb.tile(i)[:], wdT[chh * 512 + i * 128:chh * 512 + (i + 1) * 128, :], 1024)
                    for io in range(4):
                        for tch in range(2):
                            t0 = half * 1024 + tch * 512
                            mm(c, c.PM[0][0:128, 0:512],
                               [(wgb.tile(pi)[:, io * 128:(io + 1) * 128], xnTb.tile(pi)[:, t0:t0 + 512])
                                for pi in range(8)])
                            mm(c, c.PM[1][0:128, 0:512],
                               [(wub.tile(pi)[:, io * 128:(io + 1) * 128], xnTb.tile(pi)[:, t0:t0 + 512])
                                for pi in range(8)])
                            p.bar()
                            act(c, tmp["a"][:, 0:512], c.PM[0][0:128, 0:512], AF.Silu)
                            p.bar()
                            vec_tt(c, hTb.tile(io)[:, tch * 512:(tch + 1) * 512],
                                   tmp["a"][:, 0:512], c.PM[1][0:128, 0:512], OP.mult)
                            p.bar()
                    for ho in range(8):
                        for tch in range(2):
                            mm(c, c.PM[2][0:128, 0:512],
                               [(wdb.tile(pi)[:, ho * 128:(ho + 1) * 128],
                                 hTb.tile(pi)[:, tch * 512:(tch + 1) * 512]) for pi in range(4)])
                            p.bar()
                            vec_tt(c, yTh.tile(ho)[:, tch * 512:(tch + 1) * 512],
                                   yTh.tile(ho)[:, tch * 512:(tch + 1) * 512], c.PM[2][0:128, 0:512], OP.add)
                            p.bar()
                for tl in range(8):
                    items = [(yTh.tile(pi)[:, tl * 128:(tl + 1) * 128],
                              [(g.tile(half * 8 + tl)[:, pi * 128:(pi + 1) * 128], AF.Silu, 1.0)])
                             for pi in range(8)]
                    transpose_round2(c, items)

        if "upd" in parts:
            update_phase(c, x_in, x_out, scr_s, g)
        else:
            for tt in range(NT):
                p.dma(c.xtile, x_in[tt * 128:(tt + 1) * 128, :])
                p.bar()
                p.dma(x_out[tt * 128:(tt + 1) * 128, :], c.xtile[:])
                p.bar()
        if fx is None:
            p.finalize(sems)
    return nc


def prep_l1(inputs):
    s_ = np.arange(512)
    f_ = np.arange(512)
    ang = 2 * np.pi * np.outer(s_, f_) / 512.0
    k_ = np.arange(256)
    angr = 2 * np.pi * np.outer(255 - k_, f_).T.astype(np.float64) / 512.0
    pp_ = np.arange(256)
    angi = 2 * np.pi * np.outer(f_, pp_ + 256) / 512.0
    d = {
        "w_inT": np.ascontiguousarray(np.asarray(inputs["L1_attn_w_in"], np.float32).T),
        "w_binr": np.asarray(inputs["L1_attn_b_in"], np.float32).reshape(1, 2048),
        "w_waT": np.ascontiguousarray(np.asarray(inputs["L1_attn_w_a"], np.float32).reshape(1024, 256).T),
        "w_wbT": np.ascontiguousarray(np.asarray(inputs["L1_attn_w_b"], np.float32).reshape(1024, 256).T),
        "w_bab": np.concatenate([np.asarray(inputs["L1_attn_b_a"], np.float32),
                                 np.asarray(inputs["L1_attn_b_b"], np.float32)]).reshape(1, 2048),
        "w_outT": np.ascontiguousarray(np.asarray(inputs["L1_attn_w_out"], np.float32).T),
        "w_bout": np.asarray(inputs["L1_attn_b_out"], np.float32).reshape(1, D),
        "w_wgT": np.ascontiguousarray(np.asarray(inputs["L1_ffn_w_gate"], np.int32).T),
        "w_wuT": np.ascontiguousarray(np.asarray(inputs["L1_ffn_w_up"], np.int32).T),
        "w_wdT": np.ascontiguousarray(np.asarray(inputs["L1_ffn_w_down"], np.int32).T),
        "c_C512": np.cos(ang).astype(BFNP),
        "c_S512": np.sin(ang).astype(BFNP),
        "c_CR": np.cos(angr.T).astype(BFNP).reshape(256, 512) if False else np.cos(2 * np.pi * np.outer(255 - k_, f_) / 512.0).astype(BFNP),
        "c_SR": np.sin(2 * np.pi * np.outer(255 - k_, f_) / 512.0).astype(BFNP),
        "c_IC": (np.cos(angi) / 512.0).astype(BFNP),
        "c_IS": (np.sin(angi) / 512.0).astype(BFNP),
    }
    return d


def attn_core(c, xnTb, ctxT, h, qt_src, kt_src, v_src, bias_fn, p):
    for qc in range(4):
        qs = slice(qc * 512, (qc + 1) * 512)
        valid = [kt for kt in range(16) if bias_fn(qc, kt, None, None, probe=True)]
        for j, kt in enumerate(valid):
            mm(c, c.PM[2][0:128, 0:512], [(kt_src[:, kt * 128:(kt + 1) * 128], qt_src[:, qs])])
            p.bar()
            bias_fn(qc, kt, c.PM[2][0:128, 0:512], c.ATT)
            p.bar()
            mm(c, c.PS[0:1, 0:512], [(c.ONESCB[0:128, 0:1], c.ATT[:])],
               start=(j == 0), stop=(j == len(valid) - 1))
            mm(c, c.PM[3][0:64, 0:512], [(v_src.tile(kt)[:], c.ATT[:])],
               start=(j == 0), stop=(j == len(valid) - 1))
            p.bar()
        evac(c, c.csr[0:1, 0:512], c.PS[0:1, 0:512])
        p.bar()
        c.p.op("vector", lambda e, o=c.csr[1:2, 0:512] if False else None: None) if False else None
        c.p.op("vector", lambda e, o=c.csi[0:1, 0:512], i=c.csr[0:1, 0:512]: e.reciprocal(o, i))
        p.bar()
        mm(c, c.PR[0:64, 0:512], [(c.ONES1[0:1, 0:64], c.csi[0:1, 0:512])])
        p.bar()
        evac(c, c.rrep[0:64, 0:512], c.PR[0:64, 0:512])
        p.bar()
        vec_tt(c, ctxT.tile(h // 2)[64 * (h % 2):64 * (h % 2) + 64, qs],
               c.PM[3][0:64, 0:512], c.rrep[0:64, 0:512], OP.mult)
        p.bar()


def load_head_w(c, p, dst8, wdram, col0, stage):
    for i in range(8):
        p.dma(stage[0:128, 0:64], wdram[i * 128:(i + 1) * 128, col0:col0 + 64])
        p.bar()
        evac(c, dst8[i][0:128, 0:64], stage[0:128, 0:64])
        p.bar()


def qkv_head(c, p, xnTb, wb8, bcol, scale, QT):
    for qc in range(4):
        mm(c, c.PM[0][0:64, 0:512],
           [(wb8[pi][0:128, 0:64], xnTb.tile(pi)[:, qc * 512:(qc + 1) * 512]) for pi in range(8)])
        p.bar()
        vec_ts(c, QT[0:64, qc * 512:(qc + 1) * 512], c.PM[0][0:64, 0:512], bcol, OP.add, scale, OP.mult)
        p.bar()


def vproj_head(c, p, xnTb, wb8, bvrow_bf, Vb):
    for tt in range(NT):
        items = [(xnTb.tile(pi)[:, tt * 128:(tt + 1) * 128], wb8[pi][0:128, 0:64]) for pi in range(8)]
        items.append((c.ONES1B[0:1, 0:128], bvrow_bf[0:1, 0:64]))
        mm(c, c.PM[0][0:128, 0:64], items)
        p.bar()
        evac(c, Vb.tile(tt)[:, 0:64], c.PM[0][0:128, 0:64])
        p.bar()


def out_proj(c, p, ctxT, wob, borow_bf, scr_s):
    for tt in range(NT):
        for ch in range(2):
            items = [(ctxT.tile(pi)[:, tt * 128:(tt + 1) * 128], wob.tile(pi)[:, ch * 512:(ch + 1) * 512])
                     for pi in range(8)]
            items.append((c.ONES1B[0:1, 0:128], borow_bf[0:1, ch * 512:(ch + 1) * 512]))
            mm(c, c.PM[ch][0:128, 0:512], items)
        p.bar()
        for ch in range(2):
            evac(c, c.xn[:, ch * 512:(ch + 1) * 512], c.PM[ch][0:128, 0:512])
        p.bar()
        p.dma(scr_s[tt * 128:(tt + 1) * 128, :], c.xn[:])
        p.bar()


def build_l2(fx=None):
    if fx is None:
        nc = bass.Bass("TRN2", target_bir_lowering=False)
        x_in = nc.dram_tensor("xin", [T, D], F32, kind="ExternalInput").ap()
        x_out = nc.dram_tensor("xo", [T, D], F32, kind="ExternalOutput").ap()
    else:
        nc, _c, _p, x_in, x_out = fx
    Wq = nc.dram_tensor("w_Wq", [D, D], F32, kind="ExternalInput").ap()
    Wk = nc.dram_tensor("w_Wk", [D, D], F32, kind="ExternalInput").ap()
    Wv = nc.dram_tensor("w_Wv", [D, D], F32, kind="ExternalInput").ap()
    Wo = nc.dram_tensor("w_Wo", [D, D], F32, kind="ExternalInput").ap()
    bq = nc.dram_tensor("w_bq", [D, 1], F32, kind="ExternalInput").ap()
    bk = nc.dram_tensor("w_bk", [D, 1], F32, kind="ExternalInput").ap()
    bv = nc.dram_tensor("w_bv", [1, D], F32, kind="ExternalInput").ap()
    bo = nc.dram_tensor("w_bo", [1, D], F32, kind="ExternalInput").ap()
    relb = nc.dram_tensor("w_relb", [32, 16], F32, kind="ExternalInput").ap()
    crawk = nc.dram_tensor("w_crawk", [8, 16, 128, 512], F32, kind="ExternalInput").ap()
    gwT = nc.dram_tensor("w_gwT", [D, 8], F32, kind="ExternalInput").ap()
    wegT = nc.dram_tensor("w_wegT", [8, D, D], F32, kind="ExternalInput").ap()
    weuT = nc.dram_tensor("w_weuT", [8, D, D], F32, kind="ExternalInput").ap()
    wedT = nc.dram_tensor("w_wedT", [8, D, D], F32, kind="ExternalInput").ap()
    wsgT = nc.dram_tensor("w_wsgT", [D, 2048], F32, kind="ExternalInput").ap()
    wsuT = nc.dram_tensor("w_wsuT", [D, 2048], F32, kind="ExternalInput").ap()
    wsdT = nc.dram_tensor("w_wsdT", [2048, D], F32, kind="ExternalInput").ap()
    scr_s = nc.dram_tensor("l2scr_s", [T, D], F32, kind="Internal").ap()

    with contextlib.ExitStack() as stk:
        if fx is None:
            p = Prog(nc)
            c = mk_ctx(stk, nc, p)
            sems = [stk.enter_context(nc.semaphore(f"sem{i}")) for i in range(4)]
        else:
            c, p = _c, _p
        c.xtile = stk.enter_context(_sb(nc, "xtile", [128, D], F32))
        c.xn = stk.enter_context(_sb(nc, "xnb", [128, D], F32))
        c.xq = stk.enter_context(_sb(nc, "xqb", [128, D], F32))
        tmp = mk_tmp(stk, nc, "tmp", D)
        xnTb = alloc_tall(stk, nc, "xnTb", D, T, BF)
        g = alloc_tall(stk, nc, "gacc", T, D, BF)
        combs = [stk.enter_context(_sb(nc, f"comb{i}", [128, 8], F32)) for i in range(NT)]

        with contextlib.ExitStack() as ph:
            strip = alloc_tall(ph, nc, "strip", D, 128, F32)
            gwTb = alloc_tall(ph, nc, "gwTb", D, 8, F32)
            l2d = {"strip": strip, "gwT": gwTb,
                   "lg": ph.enter_context(_sb(nc, "lg", [128, 8], F32)),
                   "msk1": ph.enter_context(_sb(nc, "msk1", [128, 8], F32)),
                   "msk2": ph.enter_context(_sb(nc, "msk2", [128, 8], F32)),
                   "l2nd": ph.enter_context(_sb(nc, "l2nd", [128, 8], F32)),
                   "comb": combs}
            for i in range(8):
                p.dma(gwTb.tile(i)[:], gwT[i * 128:(i + 1) * 128, :])
            p.bar()
            phase1(c, x_in, tmp, xnTb=xnTb, l2=l2d)

        with contextlib.ExitStack() as ph:
            ctxT = alloc_tall(ph, nc, "ctxT", D, T, BF)
            QT = ph.enter_context(_sb(nc, "QT", [64, T], BF))
            KT = ph.enter_context(_sb(nc, "KT", [64, T], BF))
            Vb = alloc_tall(ph, nc, "Vb", T, 64, BF)
            wq8 = [ph.enter_context(_sb(nc, f"wq8_{i}", [128, 64], BF)) for i in range(8)]
            wk8 = [ph.enter_context(_sb(nc, f"wk8_{i}", [128, 64], BF)) for i in range(8)]
            wv8 = [ph.enter_context(_sb(nc, f"wv8_{i}", [128, 64], BF)) for i in range(8)]
            wst = ph.enter_context(_sb(nc, "wst", [128, 64], F32))
            bcolq = ph.enter_context(_sb(nc, "bcolq", [64, 2], F32))
            bcolk = ph.enter_context(_sb(nc, "bcolk", [64, 2], F32))
            bvrow = ph.enter_context(_sb(nc, "bvrow", [1, 64], F32))
            bvrow_bf = ph.enter_context(_sb(nc, "bvrowb", [1, 64], BF))
            rb0 = ph.enter_context(_sb(nc, "rb0", [128, 16], F32))
            rb31 = ph.enter_context(_sb(nc, "rb31", [128, 16], F32))
            rstage = ph.enter_context(_sb(nc, "rstg", [1, 16], F32))
            crb = ph.enter_context(_sb(nc, "crb", [128, 512], F32))
            c.ATT = ph.enter_context(_sb(nc, "ATT", [128, 512], BF))
            c.csr = ph.enter_context(_sb(nc, "csr", [2, 512], F32))
            c.csi = ph.enter_context(_sb(nc, "csi", [2, 512], F32))
            c.rrep = ph.enter_context(_sb(nc, "rrep", [64, 512], F32))
            scs = ph.enter_context(_sb(nc, "scs", [128, 512], F32))
            p.dma(rstage[0:1, :], relb[0:1, :])
            p.bar()
            mm(c, c.PR[0:128, 0:16], [(c.ONES1[0:1, 0:128], rstage[0:1, 0:16])])
            p.bar()
            evac(c, rb0[:, 0:16], c.PR[0:128, 0:16], scale=8.0)
            p.bar()
            p.dma(rstage[0:1, :], relb[31:32, :])
            p.bar()
            mm(c, c.PR[0:128, 0:16], [(c.ONES1[0:1, 0:128], rstage[0:1, 0:16])])
            p.bar()
            evac(c, rb31[:, 0:16], c.PR[0:128, 0:16], scale=8.0)
            p.bar()
            for h in range(16):
                load_head_w(c, p, wq8, Wq, 64 * h, wst)
                load_head_w(c, p, wk8, Wk, 64 * h, wst)
                load_head_w(c, p, wv8, Wv, 64 * h, wst)
                p.dma(bcolq[0:64, 0:1], bq[64 * h:64 * h + 64, 0:1])
                p.dma(bcolk[0:64, 0:1], bk[64 * h:64 * h + 64, 0:1])
                p.dma(bvrow[0:1, 0:64], bv[0:1, 64 * h:64 * h + 64])
                p.bar()
                evac(c, bvrow_bf[0:1, 0:64], bvrow[0:1, 0:64])
                p.bar()
                qkv_head(c, p, xnTb, wq8, bcolq[0:64, 0:1], 0.125, QT)
                qkv_head(c, p, xnTb, wk8, bcolk[0:64, 0:1], 1.0, KT)
                vproj_head(c, p, xnTb, wv8, bvrow_bf, Vb)

                def bias_fn(qc, kt, ps, att, probe=False, h=h):
                    if probe:
                        return True
                    cls = 4 * qc - kt
                    if cls <= -4:
                        act(c, att[:], ps, AF.Exp, bias=rb0[:, h:h + 1])
                    elif cls >= 5:
                        act(c, att[:], ps, AF.Exp, bias=rb31[:, h:h + 1])
                    else:
                        p.dma(crb[:], crawk[cls + 3, h, :, :])
                        p.bar()
                        vec_stt(c, scs[:], crb[:], 8.0, ps, OP.mult, OP.add)
                        p.bar()
                        act(c, att[:], scs[:], AF.Exp)
                    return True
                attn_core(c, xnTb, ctxT, h, QT, KT, Vb, bias_fn, p)
            wob = alloc_tall(ph, nc, "wob", D, D, BF)
            borow = ph.enter_context(_sb(nc, "borow", [1, D], F32))
            borow_bf = ph.enter_context(_sb(nc, "borowb", [1, D], BF))
            cast_load(c, c.xtile, wob, Wo, D, D)
            p.dma(borow[0:1, :], bo)
            p.bar()
            evac(c, borow_bf[0:1, :], borow[0:1, :])
            p.bar()
            out_proj(c, p, ctxT, wob, borow_bf, scr_s)

        with contextlib.ExitStack() as ph:
            yTh = alloc_tall(ph, nc, "yTh", D, 1024, F32)
            wegb = alloc_tall(ph, nc, "wegb", D, D, BF)
            weub = alloc_tall(ph, nc, "weub", D, D, BF)
            wedb = alloc_tall(ph, nc, "wedb", D, D, BF)
            hsT = alloc_tall(ph, nc, "hsT", D, 1024, BF)
            combrow = ph.enter_context(_sb(nc, "combrow", [1, T], F32))
            crep = ph.enter_context(_sb(nc, "crep", [128, 512], F32))
            for half in range(2):
                for ho in range(8):
                    p.op("vector", lambda e, o=yTh.tile(ho)[:]: e.memset(o, 0.0))
                p.bar()
                for e_ in range(8):
                    cast_load(c, c.xtile, wegb, wegT[e_], D, D)
                    cast_load(c, c.xtile, weub, weuT[e_], D, D)
                    items = [(combs[tt][:, e_:e_ + 1],
                              [(combrow[0:1, tt * 128:(tt + 1) * 128], AF.Copy, 1.0)]) for tt in range(NT)]
                    transpose_round2(c, items)
                    for tch in range(2):
                        t0 = half * 1024 + tch * 512
                        mm(c, c.PR[0:128, 0:512], [(c.ONES1[0:1, 0:128], combrow[0:1, t0:t0 + 512])])
                        p.bar()
                        evac(c, crep[:], c.PR[0:128, 0:512])
                        p.bar()
                        for io in range(8):
                            mm(c, c.PM[0][0:128, 0:512],
                               [(wegb.tile(pi)[:, io * 128:(io + 1) * 128], xnTb.tile(pi)[:, t0:t0 + 512])
                                for pi in range(8)])
                            mm(c, c.PM[1][0:128, 0:512],
                               [(weub.tile(pi)[:, io * 128:(io + 1) * 128], xnTb.tile(pi)[:, t0:t0 + 512])
                                for pi in range(8)])
                            p.bar()
                            act(c, tmp["a"][:, 0:512], c.PM[0][0:128, 0:512], AF.Silu)
                            p.bar()
                            vec_tt(c, tmp["b"][:, 0:512], tmp["a"][:, 0:512], c.PM[1][0:128, 0:512], OP.mult)
                            p.bar()
                            vec_tt(c, hsT.tile(io)[:, tch * 512:(tch + 1) * 512],
                                   tmp["b"][:, 0:512], crep[:], OP.mult)
                            p.bar()
                    cast_load(c, c.xtile, wedb, wedT[e_], D, D)
                    for ho in range(8):
                        for tch in range(2):
                            mm(c, c.PM[2][0:128, 0:512],
                               [(wedb.tile(pi)[:, ho * 128:(ho + 1) * 128],
                                 hsT.tile(pi)[:, tch * 512:(tch + 1) * 512]) for pi in range(8)])
                            p.bar()
                            vec_tt(c, yTh.tile(ho)[:, tch * 512:(tch + 1) * 512],
                                   yTh.tile(ho)[:, tch * 512:(tch + 1) * 512], c.PM[2][0:128, 0:512], OP.add)
                            p.bar()
                for chh in range(4):
                    i0 = chh * 512
                    cast_load(c, c.xtile, wegb, wsgT, D, 512, col0=i0)
                    cast_load(c, c.xtile, weub, wsuT, D, 512, col0=i0)
                    for i in range(4):
                        p.dma(c.xtile, wsdT[i0 + i * 128:i0 + (i + 1) * 128, :])
                        p.bar()
                        evac(c, wedb.tile(i)[:], c.xtile[:])
                        p.bar()
                    for io in range(4):
                        for tch in range(2):
                            t0 = half * 1024 + tch * 512
                            mm(c, c.PM[0][0:128, 0:512],
                               [(wegb.tile(pi)[:, io * 128:(io + 1) * 128], xnTb.tile(pi)[:, t0:t0 + 512])
                                for pi in range(8)])
                            mm(c, c.PM[1][0:128, 0:512],
                               [(weub.tile(pi)[:, io * 128:(io + 1) * 128], xnTb.tile(pi)[:, t0:t0 + 512])
                                for pi in range(8)])
                            p.bar()
                            act(c, tmp["a"][:, 0:512], c.PM[0][0:128, 0:512], AF.Silu)
                            p.bar()
                            vec_tt(c, hsT.tile(io)[:, tch * 512:(tch + 1) * 512],
                                   tmp["a"][:, 0:512], c.PM[1][0:128, 0:512], OP.mult)
                            p.bar()
                    for ho in range(8):
                        for tch in range(2):
                            mm(c, c.PM[2][0:128, 0:512],
                               [(wedb.tile(pi)[:, ho * 128:(ho + 1) * 128],
                                 hsT.tile(pi)[:, tch * 512:(tch + 1) * 512]) for pi in range(4)])
                            p.bar()
                            vec_tt(c, yTh.tile(ho)[:, tch * 512:(tch + 1) * 512],
                                   yTh.tile(ho)[:, tch * 512:(tch + 1) * 512], c.PM[2][0:128, 0:512], OP.add)
                            p.bar()
                for tl in range(8):
                    items = [(yTh.tile(pi)[:, tl * 128:(tl + 1) * 128],
                              [(g.tile(half * 8 + tl)[:, pi * 128:(pi + 1) * 128], AF.Silu, 1.0)])
                             for pi in range(8)]
                    transpose_round2(c, items)

        update_phase(c, x_in, x_out, scr_s, g)
        if fx is None:
            p.finalize(sems)
    return nc


def _bucket_np(d):
    rp = np.maximum(d, 0)
    is_small = rp < 16
    rp_f = np.maximum(rp, 1).astype(np.float32)
    large = 16 + (np.log(rp_f / np.float32(16.0)) / np.log(64.0 / 16.0) * 16).astype(np.int32)
    large = np.minimum(large, 31)
    return np.where(is_small, rp, large)


def prep_l2(inputs):
    relb = np.asarray(inputs["L2_attn_rel_bias"], np.float32)
    cls_ = np.arange(-3, 5)
    p_ = np.arange(128)
    f_ = np.arange(512)
    dmat = 128 * cls_[:, None, None] + f_[None, None, :] - p_[None, :, None]
    buck = _bucket_np(dmat)
    crawk = relb[buck]
    crawk = np.ascontiguousarray(np.transpose(crawk, (0, 3, 1, 2)))
    d = {
        "w_Wq": np.asarray(inputs["L2_attn_Wq"], np.float32),
        "w_Wk": np.asarray(inputs["L2_attn_Wk"], np.float32),
        "w_Wv": np.asarray(inputs["L2_attn_Wv"], np.float32),
        "w_Wo": np.asarray(inputs["L2_attn_Wo"], np.float32),
        "w_bq": np.asarray(inputs["L2_attn_bq"], np.float32).reshape(D, 1),
        "w_bk": np.asarray(inputs["L2_attn_bk"], np.float32).reshape(D, 1),
        "w_bv": np.asarray(inputs["L2_attn_bv"], np.float32).reshape(1, D),
        "w_bo": np.asarray(inputs["L2_attn_bo"], np.float32).reshape(1, D),
        "w_relb": relb,
        "w_crawk": crawk,
        "w_gwT": np.ascontiguousarray(np.asarray(inputs["L2_ffn_gate_w"], np.float32).T),
        "w_wegT": np.ascontiguousarray(np.transpose(np.asarray(inputs["L2_ffn_we_gate"], np.float32), (0, 2, 1))),
        "w_weuT": np.ascontiguousarray(np.transpose(np.asarray(inputs["L2_ffn_we_up"], np.float32), (0, 2, 1))),
        "w_wedT": np.ascontiguousarray(np.transpose(np.asarray(inputs["L2_ffn_we_down"], np.float32), (0, 2, 1))),
        "w_wsgT": np.ascontiguousarray(np.asarray(inputs["L2_ffn_ws_gate"], np.float32).T),
        "w_wsuT": np.ascontiguousarray(np.asarray(inputs["L2_ffn_ws_up"], np.float32).T),
        "w_wsdT": np.ascontiguousarray(np.asarray(inputs["L2_ffn_ws_down"], np.float32).T),
    }
    return d


def build_l3(fx=None):
    if fx is None:
        nc = bass.Bass("TRN2", target_bir_lowering=False)
        x_in = nc.dram_tensor("xin", [T, D], F32, kind="ExternalInput").ap()
        x_out = nc.dram_tensor("xo", [T, D], F32, kind="ExternalOutput").ap()
    else:
        nc, _c, _p, x_in, x_out = fx
    wqkv = nc.dram_tensor("w_qkv", [D, 3072], F32, kind="ExternalInput").ap()
    bqkv = nc.dram_tensor("w_bqkv", [3072, 1], F32, kind="ExternalInput").ap()
    bqkvr = nc.dram_tensor("w_bqkvr", [1, 3072], F32, kind="ExternalInput").ap()
    wproj = nc.dram_tensor("w_proj", [D, D], F32, kind="ExternalInput").ap()
    bproj = nc.dram_tensor("w_bproj", [1, D], F32, kind="ExternalInput").ap()
    wfr = nc.dram_tensor("w_wfr", [D, D], F32, kind="ExternalInput").ap()
    wfi = nc.dram_tensor("w_wfi", [D, D], F32, kind="ExternalInput").ap()
    fbias = nc.dram_tensor("w_fbias", [1, D], F32, kind="ExternalInput").ap()
    C1024 = nc.dram_tensor("c_C1024", [D, D], BF, kind="ExternalInput").ap()
    S1024 = nc.dram_tensor("c_S1024", [D, D], BF, kind="ExternalInput").ap()
    negm = nc.dram_tensor("c_negm", [4, 128, 512], F32, kind="ExternalInput").ap()
    scr_s = nc.dram_tensor("l3scr_s", [T, D], F32, kind="Internal").ap()

    with contextlib.ExitStack() as stk:
        if fx is None:
            p = Prog(nc)
            c = mk_ctx(stk, nc, p)
            sems = [stk.enter_context(nc.semaphore(f"sem{i}")) for i in range(4)]
        else:
            c, p = _c, _p
        c.xtile = stk.enter_context(_sb(nc, "xtile", [128, D], F32))
        c.xn = stk.enter_context(_sb(nc, "xnb", [128, D], F32))
        c.xq = stk.enter_context(_sb(nc, "xqb", [128, D], F32))
        tmp = mk_tmp(stk, nc, "tmp", D)
        xnTb = alloc_tall(stk, nc, "xnTb", D, T, BF)
        g = alloc_tall(stk, nc, "gacc", T, D, BF)

        phase1(c, x_in, tmp, xnTb=xnTb)

        with contextlib.ExitStack() as ph:
            ctxT = alloc_tall(ph, nc, "ctxT", D, T, BF)
            QT = ph.enter_context(_sb(nc, "QT", [64, T], BF))
            KT = ph.enter_context(_sb(nc, "KT", [64, T], BF))
            Vb = alloc_tall(ph, nc, "Vb", T, 64, BF)
            wq8 = [ph.enter_context(_sb(nc, f"wq8_{i}", [128, 64], BF)) for i in range(8)]
            wk8 = [ph.enter_context(_sb(nc, f"wk8_{i}", [128, 64], BF)) for i in range(8)]
            wv8 = [ph.enter_context(_sb(nc, f"wv8_{i}", [128, 64], BF)) for i in range(8)]
            wst = ph.enter_context(_sb(nc, "wst", [128, 64], F32))
            bcolq = ph.enter_context(_sb(nc, "bcolq", [64, 2], F32))
            bcolk = ph.enter_context(_sb(nc, "bcolk", [64, 2], F32))
            bvrow = ph.enter_context(_sb(nc, "bvrow", [1, 64], F32))
            bvrow_bf = ph.enter_context(_sb(nc, "bvrowb", [1, 64], BF))
            nmt = [ph.enter_context(_sb(nc, f"nmt{i}", [128, 512], F32)) for i in range(4)]
            c.ATT = ph.enter_context(_sb(nc, "ATT", [128, 512], BF))
            c.csr = ph.enter_context(_sb(nc, "csr", [2, 512], F32))
            c.csi = ph.enter_context(_sb(nc, "csi", [2, 512], F32))
            c.rrep = ph.enter_context(_sb(nc, "rrep", [64, 512], F32))
            scs = ph.enter_context(_sb(nc, "scs", [128, 512], F32))
            for i in range(4):
                p.dma(nmt[i][:], negm[i, :, :])
            p.bar()
            for h in range(16):
                load_head_w(c, p, wq8, wqkv, 64 * h, wst)
                load_head_w(c, p, wk8, wqkv, 1024 + 64 * h, wst)
                load_head_w(c, p, wv8, wqkv, 2048 + 64 * h, wst)
                p.dma(bcolq[0:64, 0:1], bqkv[64 * h:64 * h + 64, 0:1])
                p.dma(bcolk[0:64, 0:1], bqkv[1024 + 64 * h:1024 + 64 * h + 64, 0:1])
                p.dma(bvrow[0:1, 0:64], bqkvr[0:1, 2048 + 64 * h:2048 + 64 * h + 64])
                p.bar()
                evac(c, bvrow_bf[0:1, 0:64], bvrow[0:1, 0:64])
                p.bar()
                qkv_head(c, p, xnTb, wq8, bcolq[0:64, 0:1], 0.125, QT)
                qkv_head(c, p, xnTb, wk8, bcolk[0:64, 0:1], 1.0, KT)
                vproj_head(c, p, xnTb, wv8, bvrow_bf, Vb)

                def bias_fn(qc, kt, ps, att, probe=False):
                    cls = 4 * qc - kt
                    if probe:
                        return cls >= -3
                    if cls >= 1:
                        act(c, att[:], ps, AF.Exp)
                    else:
                        vec_tt(c, scs[:], ps, nmt[cls + 3][:], OP.add)
                        p.bar()
                        act(c, att[:], scs[:], AF.Exp)
                    return True
                attn_core(c, xnTb, ctxT, h, QT, KT, Vb, bias_fn, p)
            wob = alloc_tall(ph, nc, "wob", D, D, BF)
            borow = ph.enter_context(_sb(nc, "borow", [1, D], F32))
            borow_bf = ph.enter_context(_sb(nc, "borowb", [1, D], BF))
            cast_load(c, c.xtile, wob, wproj, D, D)
            p.dma(borow[0:1, :], bproj)
            p.bar()
            evac(c, borow_bf[0:1, :], borow[0:1, :])
            p.bar()
            out_proj(c, p, ctxT, wob, borow_bf, scr_s)

        with contextlib.ExitStack() as ph:
            Cb = alloc_tall(ph, nc, "Cb", D, D, BF)
            Sb_ = alloc_tall(ph, nc, "Sb_", D, D, BF)
            ATb = alloc_tall(ph, nc, "ATb", D, D, BF)
            BTb = alloc_tall(ph, nc, "BTb", D, D, BF)
            for i in range(8):
                p.dma(Cb.tile(i)[:], C1024[i * 128:(i + 1) * 128, :])
                p.dma(Sb_.tile(i)[:], S1024[i * 128:(i + 1) * 128, :])
            p.bar()
            with contextlib.ExitStack() as ph2:
                Wrb = alloc_tall(ph2, nc, "Wrb", D, D, BF)
                Wib = alloc_tall(ph2, nc, "Wib", D, D, BF)
                cast_load(c, c.xtile, Wrb, wfr, D, D)
                cast_load(c, c.xtile, Wib, wfi, D, D)
                for po in range(8):
                    for ch in range(2):
                        cs = slice(ch * 512, (ch + 1) * 512)
                        items = [(Wrb.tile(pi)[:, po * 128:(po + 1) * 128], Cb.tile(pi)[:, cs]) for pi in range(8)]
                        items += [(Wib.tile(pi)[:, po * 128:(po + 1) * 128], Sb_.tile(pi)[:, cs]) for pi in range(8)]
                        mm(c, c.PM[0][0:128, 0:512], items)
                        p.bar()
                        evac(c, ATb.tile(po)[:, cs], c.PM[0][0:128, 0:512])
                        p.bar()
                for i in range(8):
                    evac(c, Wib.tile(i)[:], Wib.tile(i)[:], scale=-1.0)
                p.bar()
                for po in range(8):
                    for ch in range(2):
                        cs = slice(ch * 512, (ch + 1) * 512)
                        items = [(Wrb.tile(pi)[:, po * 128:(po + 1) * 128], Sb_.tile(pi)[:, cs]) for pi in range(8)]
                        items += [(Wib.tile(pi)[:, po * 128:(po + 1) * 128], Cb.tile(pi)[:, cs]) for pi in range(8)]
                        mm(c, c.PM[0][0:128, 0:512], items)
                        p.bar()
                        evac(c, BTb.tile(po)[:, cs], c.PM[0][0:128, 0:512])
                        p.bar()
            Mb = alloc_tall(ph, nc, "Mb", D, D, BF)
            for po in range(8):
                for ch in range(2):
                    cs = slice(ch * 512, (ch + 1) * 512)
                    items = [(ATb.tile(pi)[:, po * 128:(po + 1) * 128], Cb.tile(pi)[:, cs]) for pi in range(8)]
                    items += [(BTb.tile(pi)[:, po * 128:(po + 1) * 128], Sb_.tile(pi)[:, cs]) for pi in range(8)]
                    mm(c, c.PM[0][0:128, 0:512], items)
                    p.bar()
                    evac(c, Mb.tile(po)[:, cs], c.PM[0][0:128, 0:512], scale=1.0 / 1024.0)
                    p.bar()
            fbb = ph.enter_context(_sb(nc, "fbb", [1, D], BF))
            rstage = ph.enter_context(_sb(nc, "rstg", [1, D], F32))
            p.dma(rstage[0:1, :], fbias)
            p.bar()
            evac(c, fbb[0:1, :], rstage[0:1, :])
            p.bar()
            for tt in range(NT):
                for ch in range(2):
                    cs = slice(ch * 512, (ch + 1) * 512)
                    items = [(xnTb.tile(pi)[:, tt * 128:(tt + 1) * 128], Mb.tile(pi)[:, cs]) for pi in range(8)]
                    items.append((c.ONES1B[0:1, 0:128], fbb[0:1, cs]))
                    mm(c, c.PM[ch][0:128, 0:512], items)
                p.bar()
                for ch in range(2):
                    evac(c, g.tile(tt)[:, ch * 512:(ch + 1) * 512], c.PM[ch][0:128, 0:512], func=AF.Silu)
                p.bar()

        update_phase(c, x_in, x_out, scr_s, g)
        if fx is None:
            p.finalize(sems)
    return nc


def prep_l3(inputs):
    i_ = np.arange(1024)
    ang = 2 * np.pi * np.outer(i_, i_) / 1024.0
    cls_ = np.arange(-3, 1)
    p_ = np.arange(128)
    f_ = np.arange(512)
    dmat = 128 * cls_[:, None, None] + f_[None, None, :] - p_[None, :, None]
    negm = np.where(dmat >= 0, 0.0, -1e30).astype(np.float32)
    b = np.asarray(inputs["L3_attn_b_qkv"], np.float32)
    d = {
        "w_qkv": np.asarray(inputs["L3_attn_w_qkv"], np.float32),
        "w_bqkv": b.reshape(3072, 1),
        "w_bqkvr": b.reshape(1, 3072),
        "w_proj": np.asarray(inputs["L3_attn_w_proj"], np.float32),
        "w_bproj": np.asarray(inputs["L3_attn_b_proj"], np.float32).reshape(1, D),
        "w_wfr": np.asarray(inputs["L3_ffn_w_freq_real"], np.float32),
        "w_wfi": np.asarray(inputs["L3_ffn_w_freq_imag"], np.float32),
        "w_fbias": np.asarray(inputs["L3_ffn_bias"], np.float32).reshape(1, D),
        "c_C1024": np.cos(ang).astype(BFNP),
        "c_S1024": np.sin(ang).astype(BFNP),
        "c_negm": negm,
    }
    return d


def kernel_unfused(**inputs):
    consts = host_consts()
    x = np.ascontiguousarray(np.asarray(inputs["L0_attn_Z"], np.float32))
    cur = [np.ascontiguousarray(x[b]) for b in range(B)]
    builders = [(build_l0, prep_l0), (build_l1, prep_l1), (build_l2, prep_l2), (build_l3, prep_l3)]
    for li, (bf, pf) in enumerate(builders):
        nc = bf()
        wmap = pf(inputs)
        wmap.update(consts)
        in_maps = [dict(wmap, xin=cur[b]) for b in range(B)]
        res = run_bass_kernel_spmd(nc, in_maps, core_ids=list(range(B)))
        cur = [np.ascontiguousarray(np.asarray(res.results[b]["xo"], np.float32)) for b in range(B)]
    return np.stack(cur, 0)


def build_fused():
    nc = bass.Bass("TRN2", target_bir_lowering=False)
    x0 = nc.dram_tensor("xin", [T, D], F32, kind="ExternalInput").ap()
    x4 = nc.dram_tensor("xo", [T, D], F32, kind="ExternalOutput").ap()
    x1 = nc.dram_tensor("ix1", [T, D], F32, kind="Internal").ap()
    x2 = nc.dram_tensor("ix2", [T, D], F32, kind="Internal").ap()
    x3 = nc.dram_tensor("ix3", [T, D], F32, kind="Internal").ap()
    with contextlib.ExitStack() as stk:
        p = Prog(nc)
        c = mk_ctx(stk, nc, p)
        sems = [stk.enter_context(nc.semaphore(f"sem{i}")) for i in range(4)]
        build_l0(fx=(nc, c, p, x0, x1))
        build_l1(fx=(nc, c, p, x1, x2))
        build_l2(fx=(nc, c, p, x2, x3))
        build_l3(fx=(nc, c, p, x3, x4))
        p.finalize(sems)
    return nc


def kernel(**inputs):
    consts = host_consts()
    x = np.ascontiguousarray(np.asarray(inputs["L0_attn_Z"], np.float32))
    nc = build_fused()
    wmap = dict(consts)
    for pf in (prep_l0, prep_l1, prep_l2, prep_l3):
        wmap.update(pf(inputs))
    in_maps = [dict(wmap, xin=np.ascontiguousarray(x[b])) for b in range(B)]
    res = run_bass_kernel_spmd(nc, in_maps, core_ids=list(range(B)))
    return np.stack([np.ascontiguousarray(np.asarray(res.results[b]["xo"], np.float32)) for b in range(B)], 0)


# version 03
# speedup vs baseline: 1.2818x; 1.1111x over previous
import math
import numpy as np
import ml_dtypes
import concourse.bass as bass
import concourse.mybir as mybir
from concourse.bass_utils import run_bass_kernel_spmd
from concourse.alu_op_type import AluOpType as OP

AF = mybir.ActivationFunctionType
AX = mybir.AxisListType
F32 = mybir.dt.float32
BF = mybir.dt.bfloat16
I32 = mybir.dt.int32
BFNP = ml_dtypes.bfloat16

B, T, D = 8, 2048, 1024
NT = T // 128


def _ap(x):
    return x if isinstance(x, bass.AP) else x[:]


_UNIQ = [0]


def _sb(nc, name, shape, dtype):
    _UNIQ[0] += 1
    return nc.sbuf_tensor(f"{name}_u{_UNIQ[0]}", shape, dtype)


class Prog:
    def __init__(self, nc, nsem=4):
        self.nc = nc
        self.nsem = nsem
        self.stages = []
        self.cur = []

    def op(self, eng, fn, inc=1):
        self.cur.append((eng, fn, inc))

    def dma(self, out, in_, eng="sync"):
        out = _ap(out); in_ = _ap(in_)
        self.op(eng, lambda e, o=out, i=in_: e.dma_start(out=o, in_=i), 16)

    def bar(self):
        if self.cur:
            self.stages.append(self.cur)
            self.cur = []

    def finalize(self, sems):
        self.bar()
        R = len(sems)
        cum = [0] * R
        tgt = []
        for s, st in enumerate(self.stages):
            k = s % R
            cum[k] += sum(i for (_, _, i) in st)
            tgt.append((k, cum[k]))

        def emit(eng_obj, name):
            for s, st in enumerate(self.stages):
                ops = [(f, i) for (e, f, i) in st if e == name]
                if not ops:
                    continue
                if s > 0:
                    k, v = tgt[s - 1]
                    eng_obj.wait_ge(sems[k], v)
                for f, i in ops:
                    inst = f(eng_obj)
                    inst.then_inc(sems[s % R], i)

        nc = self.nc
        with nc.Block() as blk:
            @blk.sync
            def _(e):
                emit(e, "sync")
                k, v = tgt[-1]
                e.wait_ge(sems[k], v)

            @blk.tensor
            def _(e):
                emit(e, "tensor")

            @blk.vector
            def _(e):
                emit(e, "vector")

            @blk.scalar
            def _(e):
                emit(e, "scalar")

            @blk.gpsimd
            def _(e):
                emit(e, "gpsimd")


class Tall:

    def __init__(self, tiles):
        self.t = tiles

    def tile(self, i):
        return self.t[i]

    def __len__(self):
        return len(self.t)


def alloc_tall(stk, nc, name, rows, cols, dtype):
    n = (rows + 127) // 128
    tiles = []
    for i in range(n):
        r = min(128, rows - 128 * i)
        h = stk.enter_context(_sb(nc, f"{name}_{i}", [r, cols], dtype))
        tiles.append(h)
    return Tall(tiles)



class Ctx:
    pass


def mk_ctx(stk, nc, p):
    c = Ctx()
    c.nc = nc
    c.p = p
    c.stk = stk
    c.PT = [stk.enter_context(nc.psum_tensor(f"pt{i}", [128, 512], F32)) for i in range(2)]
    c.PM = [stk.enter_context(nc.psum_tensor(f"pm{i}", [128, 512], F32)) for i in range(4)]
    c.PR = stk.enter_context(nc.psum_tensor("pr", [128, 512], F32))
    c.PS = stk.enter_context(nc.psum_tensor("ps", [128, 512], F32))
    c.IDENT = stk.enter_context(_sb(nc, "ident", [128, 128], F32))
    c.ONES1 = stk.enter_context(_sb(nc, "ones1", [1, 128], F32))
    c.ONES1B = stk.enter_context(_sb(nc, "ones1b", [1, 128], BF))
    c.ONESCB = stk.enter_context(_sb(nc, "onescb", [128, 1], BF))
    ident = nc.dram_tensor("c_ident", [128, 128], F32, kind="ExternalInput").ap()
    ones1 = nc.dram_tensor("c_ones1", [1, 128], F32, kind="ExternalInput").ap()
    ones1b = nc.dram_tensor("c_ones1b", [1, 128], BF, kind="ExternalInput").ap()
    onescb = nc.dram_tensor("c_onescb", [128, 1], BF, kind="ExternalInput").ap()
    p.dma(c.IDENT[:], ident)
    p.dma(c.ONES1[:], ones1)
    p.dma(c.ONES1B[:], ones1b)
    p.dma(c.ONESCB[:], onescb)
    c.CC = stk.enter_context(_sb(nc, "cc", [128, 4], F32))
    p.op("vector", lambda e: e.memset(c.CC[:, 0:1], 1e-5))
    p.bar()
    c.EPS = c.CC[:, 0:1]
    return c


def host_consts():
    return {
        "c_ident": np.eye(128, dtype=np.float32),
        "c_ones1": np.ones((1, 128), np.float32),
        "c_ones1b": np.ones((1, 128), BFNP),
        "c_onescb": np.ones((128, 1), BFNP),
    }


def mm(c, out, items, start=True, stop=True):
    out = _ap(out)
    items = [(_ap(l), _ap(r)) for (l, r) in items]
    def fn(e, out=out, items=list(items)):
        n = len(items)
        inst = None
        for i, (l, r) in enumerate(items):
            inst = e.matmul(out, l, r, start=(start and i == 0), stop=(stop and i == n - 1))
        return inst
    c.p.op("tensor", fn)


def transpose_round(c, items):
    for g0 in range(0, len(items), 8):
        grp = items[g0:g0 + 8]
        for i, it in enumerate(grp):
            src = it[0]
            a, bb = src.shape
            slot = c.PT[i // 4][0:bb, 128 * (i % 4):128 * (i % 4) + a]
            c.p.op("tensor", lambda e, s=src, o=slot, aa=a: e.transpose(o, s, c.IDENT[0:aa, 0:aa]))
        c.p.bar()
        for i, it in enumerate(grp):
            src, dst = it[0], it[1]
            func = it[2] if len(it) > 2 else AF.Copy
            scale = it[3] if len(it) > 3 else 1.0
            a, bb = src.shape
            slot = c.PT[i // 4][0:bb, 128 * (i % 4):128 * (i % 4) + a]
            c.p.op("scalar", lambda e, o=dst, s=slot, f=func, sc=scale: e.activation(o, s, f, scale=sc))
        c.p.bar()


def evac(c, dst, src, func=AF.Copy, scale=1.0, bias=...):
    dst = _ap(dst); src = _ap(src)
    c.p.op("scalar", lambda e, o=dst, s=src, f=func, sc=scale, b=bias:
           e.activation(o, s, f, scale=sc) if b is ... else e.activation(o, s, f, scale=sc, bias=b))


def vec_tt(c, out, in0, in1, op):
    out = _ap(out); in0 = _ap(in0); in1 = _ap(in1)
    c.p.op("vector", lambda e, o=out, a=in0, b=in1, oo=op: e.tensor_tensor(o, a, b, oo))


def vec_ts(c, out, in0, s1, op0, s2=None, op1=None):
    out = _ap(out); in0 = _ap(in0)
    s1 = _ap(s1) if not isinstance(s1, (int, float)) else s1
    if s2 is None:
        c.p.op("vector", lambda e, o=out, a=in0, s=s1, oo=op0: e.tensor_scalar(o, a, s, None, oo))
    else:
        c.p.op("vector", lambda e, o=out, a=in0, sa=s1, sb=s2, o0=op0, o1=op1:
               e.tensor_scalar(o, a, sa, sb, o0, o1))


def vec_stt(c, out, in0, scalar, in1, op0, op1):
    out = _ap(out); in0 = _ap(in0); in1 = _ap(in1)
    scalar = _ap(scalar) if not isinstance(scalar, (int, float)) else scalar
    c.p.op("vector", lambda e, o=out, a=in0, s=scalar, b=in1, o0=op0, o1=op1:
           e.scalar_tensor_tensor(o, a, s, b, o0, o1))


def vec_red(c, out, in_, op):
    out = _ap(out); in_ = _ap(in_)
    c.p.op("vector", lambda e, o=out, i=in_, oo=op: e.tensor_reduce(o, i, AX.X, oo))


def act(c, out, in_, func, scale=1.0, bias=...):
    out = _ap(out); in_ = _ap(in_)
    if bias is not ... and not isinstance(bias, (int, float)):
        bias = _ap(bias)
    c.p.op("scalar", lambda e, o=out, i=in_, f=func, sc=scale, b=bias:
           e.activation(o, i, f, scale=sc) if b is ... else e.activation(o, i, f, scale=sc, bias=b))


def bcast_row(c, dst, row, n=128, lhs=None):
    F = row.shape[1]
    lhsT = (lhs if lhs is not None else c.ONES1)[0:1, 0:n]
    mm(c, c.PR[0:n, 0:F], [(lhsT, row)])
    c.p.bar()
    evac(c, dst, c.PR[0:n, 0:F])
    c.p.bar()


def layernorm_tile(c, xn_out, x_in, tmp, eps=None):
    Dd = x_in.shape[1]
    vec_red(c, tmp["r1"][:, 0:1], x_in, OP.add)
    c.p.bar()
    c.p.op("scalar", lambda e, o=tmp["r2"][:, 0:1], i=tmp["r1"][:, 0:1]: e.mul(o, i, -1.0 / Dd))
    c.p.bar()
    vec_ts(c, tmp["a"], x_in, tmp["r2"][:, 0:1], OP.add)
    c.p.bar()
    act(c, tmp["b"], tmp["a"], AF.Square)
    c.p.bar()
    vec_red(c, tmp["r1"][:, 0:1], tmp["b"], OP.add)
    c.p.bar()
    act(c, tmp["r2"][:, 0:1], tmp["r1"][:, 0:1], AF.Sqrt, scale=1.0 / Dd, bias=c.EPS)
    c.p.bar()
    c.p.op("vector", lambda e, o=tmp["r1"][:, 0:1], i=tmp["r2"][:, 0:1]: e.reciprocal(o, i))
    c.p.bar()
    vec_ts(c, xn_out, tmp["a"], tmp["r1"][:, 0:1], OP.mult)
    c.p.bar()


def fp4_quant(c, q_out_f32, x_in, tmp):
    F = x_in.shape[1]
    nb = F // 16
    absx = tmp["a"]
    act(c, absx[:, 0:F], x_in, AF.Abs)
    c.p.bar()
    a3 = absx[:, 0:F].rearrange("p (b k) -> p b k", k=16)
    vec_red(c, tmp["s"][:, 0:nb], a3, OP.max)
    c.p.bar()
    vec_ts(c, tmp["s2"][:, 0:nb], tmp["s"][:, 0:nb], 1.0 / 6.0, OP.mult, 1e-12, OP.max)
    c.p.bar()
    c.p.op("vector", lambda e, o=tmp["s"][:, 0:nb], i=tmp["s2"][:, 0:nb]: e.reciprocal(o, i))
    c.p.bar()
    recb = tmp["s"][:, 0:nb].rearrange("p (b o) -> p b o", o=1).broadcast_to((128, nb, 16))
    scb = tmp["s2"][:, 0:nb].rearrange("p (b o) -> p b o", o=1).broadcast_to((128, nb, 16))
    tq = tmp["b"]
    vec_tt(c, tq[:, 0:F].rearrange("p (b k) -> p b k", k=16), a3, recb, OP.mult)
    c.p.bar()
    accs = [tmp["c"], tmp["d"]]
    vec_ts(c, accs[0][:, 0:F], tq[:, 0:F], 0.25, OP.is_gt, 0.5, OP.mult)
    c.p.bar()
    cur = 0
    for mid, w in [(0.75, 0.5), (1.25, 0.5), (1.75, 0.5), (2.5, 1.0), (3.5, 1.0), (5.0, 2.0)]:
        vec_ts(c, tmp["e"][:, 0:F], tq[:, 0:F], mid, OP.is_gt, w, OP.mult)
        c.p.bar()
        vec_tt(c, accs[1 - cur][:, 0:F], accs[cur][:, 0:F], tmp["e"][:, 0:F], OP.add)
        c.p.bar()
        cur = 1 - cur
    vec_tt(c, tmp["e"][:, 0:F].rearrange("p (b k) -> p b k", k=16),
           accs[cur][:, 0:F].rearrange("p (b k) -> p b k", k=16), scb, OP.mult)
    c.p.op("scalar", lambda e, o=tmp["b"][:, 0:F], i=x_in: e.sign(o, i))
    c.p.bar()
    vec_tt(c, q_out_f32, tmp["e"][:, 0:F], tmp["b"][:, 0:F], OP.mult)
    c.p.bar()


def mk_tmp(stk, nc, name, F=1024):
    t = {}
    for k in ["a", "b", "c", "d", "e"]:
        t[k] = stk.enter_context(_sb(nc, f"{name}_{k}", [128, F], F32))
    t["s"] = stk.enter_context(_sb(nc, f"{name}_s", [128, 256], F32))
    t["s2"] = stk.enter_context(_sb(nc, f"{name}_s2", [128, 256], F32))
    t["r1"] = stk.enter_context(_sb(nc, f"{name}_r1", [128, 8], F32))
    t["r2"] = stk.enter_context(_sb(nc, f"{name}_r2", [128, 8], F32))
    return t




import contextlib


def phase1(c, x_in, tmp, xnTb=None, l0=None, l2=None):
    nc, p = c.nc, c.p
    for tt in range(NT):
        p.dma(c.xtile, x_in[tt * 128:(tt + 1) * 128, :])
        p.bar()
        layernorm_tile(c, c.xn[:], c.xtile[:], tmp)
        items = []
        if xnTb is not None:
            for k in range(8):
                items.append((c.xn[:, 128 * k:128 * (k + 1)],
                              [(xnTb.tile(k)[:, tt * 128:(tt + 1) * 128], AF.Copy, 1.0)]))
        if l2 is not None:
            for k in range(8):
                items.append((c.xn[:, 128 * k:128 * (k + 1)],
                              [(l2["strip"].tile(k)[:], AF.Copy, 1.0)]))
        if l0 is not None:
            p.dma(l0["scr_xn"][tt * 128:(tt + 1) * 128, :], c.xn[:])
        transpose_round2(c, items)
        if l0 is not None:
            fp4_quant(c, c.xq[:], c.xn[:], tmp)
            qitems = [(c.xq[:, 128 * k:128 * (k + 1)],
                       [(l0["xqT"].tile(k)[:, tt * 128:(tt + 1) * 128], AF.Copy, 1.0)])
                      for k in range(8)]
            transpose_round2(c, qitems)
        if l2 is not None:
            mm(c, c.PS[0:128, 0:8],
               [(l2["strip"].tile(pi)[:], l2["gwT"].tile(pi)[:]) for pi in range(8)])
            p.bar()
            evac(c, l2["lg"][:, 0:8], c.PS[0:128, 0:8])
            p.bar()
            lg = l2["lg"]
            r1, r2 = tmp["r1"], tmp["r2"]
            vec_red(c, r1[:, 0:1], lg[:, 0:8], OP.max)
            p.bar()
            vec_ts(c, l2["msk1"][:, 0:8], lg[:, 0:8], r1[:, 0:1], OP.is_ge)
            p.bar()
            vec_stt(c, l2["l2nd"][:, 0:8], l2["msk1"][:, 0:8], -1e30, lg[:, 0:8], OP.mult, OP.add)
            p.bar()
            vec_red(c, r2[:, 0:1], l2["l2nd"][:, 0:8], OP.max)
            p.bar()
            vec_ts(c, l2["msk2"][:, 0:8], l2["l2nd"][:, 0:8], r2[:, 0:1], OP.is_ge)
            vec_tt(c, r2[:, 1:2], r2[:, 0:1], r1[:, 0:1], OP.subtract)
            p.bar()
            act(c, r2[:, 2:3], r2[:, 1:2], AF.Exp)
            p.bar()
            c.p.op("scalar", lambda e, o=r2[:, 3:4], i=r2[:, 2:3]: e.add(o, i, 1.0))
            p.bar()
            c.p.op("vector", lambda e, o=r1[:, 1:2], i=r2[:, 3:4]: e.reciprocal(o, i))
            p.bar()
            vec_ts(c, l2["msk2"][:, 0:8], l2["msk2"][:, 0:8], r2[:, 2:3], OP.mult)
            p.bar()
            vec_tt(c, l2["msk1"][:, 0:8], l2["msk1"][:, 0:8], l2["msk2"][:, 0:8], OP.add)
            p.bar()
            vec_ts(c, l2["comb"][tt][:, 0:8], l2["msk1"][:, 0:8], r1[:, 1:2], OP.mult)
            p.bar()
        p.bar()


def transpose_round2(c, items):
    for g0 in range(0, len(items), 8):
        grp = items[g0:g0 + 8]
        for i, (src, dsts) in enumerate(grp):
            a, bb = src.shape
            slot = c.PT[i // 4][0:bb, 128 * (i % 4):128 * (i % 4) + a]
            c.p.op("tensor", lambda e, s=src, o=slot, aa=a: e.transpose(o, s, c.IDENT[0:aa, 0:aa]))
        c.p.bar()
        for i, (src, dsts) in enumerate(grp):
            a, bb = src.shape
            slot = c.PT[i // 4][0:bb, 128 * (i % 4):128 * (i % 4) + a]
            for (dst, func, scale) in dsts:
                c.p.op("scalar", lambda e, o=_ap(dst), ss=slot, f=func, sc=scale: e.activation(o, ss, f, scale=sc))
        c.p.bar()


def update_phase(c, x_in, x_out, scr_s, g, dbg=None):
    p = c.p
    for tt in range(NT):
        sl = slice(tt * 128, (tt + 1) * 128)
        p.dma(c.xtile, x_in[sl, :])
        p.dma(c.xn, scr_s[sl, :])
        p.bar()
        if dbg is not None:
            p.dma(dbg["o_s"][sl, :], c.xn[:])
            evac(c, c.xq[:], g.tile(tt)[:])
            p.bar()
            p.dma(dbg["o_g"][sl, :], c.xq[:])
            p.bar()
        vec_tt(c, c.xq[:], c.xn[:], g.tile(tt)[:], OP.mult)
        p.bar()
        vec_tt(c, c.xn[:], c.xtile[:], c.xq[:], OP.add)
        p.bar()
        p.dma(x_out[sl, :], c.xn[:])
        p.bar()


def cast_load(c, stage, dst_tall, src_dram, rows, cols, scale=1.0, col0=0, row0=0):
    n = (rows + 127) // 128
    for i in range(n):
        r = min(128, rows - 128 * i)
        c.p.dma(stage[0:r, 0:cols], src_dram[row0 + 128 * i:row0 + 128 * i + r, col0:col0 + cols])
        c.p.bar()
        evac(c, dst_tall.tile(i)[0:r, 0:cols], stage[0:r, 0:cols], scale=scale)
        c.p.bar()


def build_l0(dbg=False, fx=None):
    if fx is None:
        nc = bass.Bass("TRN2", target_bir_lowering=False)
        x_in = nc.dram_tensor("xin", [T, D], F32, kind="ExternalInput").ap()
        x_out = nc.dram_tensor("xo", [T, D], F32, kind="ExternalOutput").ap()
    else:
        nc, _c, _p, x_in, x_out = fx
    dec = nc.dram_tensor("w_dec", [1, 1], F32, kind="ExternalInput").ap()
    WT = nc.dram_tensor("w_WT", [D, D], F32, kind="ExternalInput").ap()
    bb = nc.dram_tensor("w_b", [1, D], F32, kind="ExternalInput").ap()
    gam = nc.dram_tensor("w_gam", [1, D], F32, kind="ExternalInput").ap()
    bet = nc.dram_tensor("w_bet", [1, D], F32, kind="ExternalInput").ap()
    wg = nc.dram_tensor("w_wg", [2816, D], F32, kind="ExternalInput").ap()
    wu = nc.dram_tensor("w_wu", [2816, D], F32, kind="ExternalInput").ap()
    wd = nc.dram_tensor("w_wd", [D, 2816], F32, kind="ExternalInput").ap()
    scr_xn = nc.dram_tensor("l0scr_xn", [T, D], F32, kind="Internal").ap()
    scr_s = nc.dram_tensor("l0scr_s", [T, D], F32, kind="Internal").ap()
    scr_hq = nc.dram_tensor("l0scr_hq", [2816, T], BF, kind="Internal").ap()
    dbgd = None
    if dbg:
        dbgd = {"o_s": nc.dram_tensor("o_s", [T, D], F32, kind="ExternalOutput").ap(),
                "o_g": nc.dram_tensor("o_g", [T, D], F32, kind="ExternalOutput").ap()}

    with contextlib.ExitStack() as stk:
        if fx is None:
            p = Prog(nc)
            c = mk_ctx(stk, nc, p)
            sems = [stk.enter_context(nc.semaphore(f"sem{i}")) for i in range(4)]
        else:
            c, p = _c, _p
        c.xtile = stk.enter_context(_sb(nc, "xtile", [128, D], F32))
        c.xn = stk.enter_context(_sb(nc, "xnb", [128, D], F32))
        c.xq = stk.enter_context(_sb(nc, "xqb", [128, D], F32))
        c.hqtb = stk.enter_context(_sb(nc, "hqtb", [128, 512], BF))
        tmp = mk_tmp(stk, nc, "tmp", D)
        xqT = alloc_tall(stk, nc, "xqT", D, T, BF)
        g = alloc_tall(stk, nc, "gacc", T, D, BF)

        phase1(c, x_in, tmp, l0={"scr_xn": scr_xn, "xqT": xqT})

        with contextlib.ExitStack() as ph:
            Sb = alloc_tall(ph, nc, "Sb", D, T, BF)
            afull = ph.enter_context(_sb(nc, "afull", [128, 512], F32))
            states = ph.enter_context(_sb(nc, "states", [128, 8], F32))
            stage = ph.enter_context(_sb(nc, "stg", [128, 512], F32))
            stage2 = ph.enter_context(_sb(nc, "stg2", [128, 512], F32))
            dct = ph.enter_context(_sb(nc, "dct", [1, 4], F32))
            acol = ph.enter_context(_sb(nc, "acol", [128, 1], F32))
            xns = [ph.enter_context(_sb(nc, f"xns{j}", [128, D], F32)) for j in range(4)]
            p.dma(dct[0:1, 0:1], dec)
            p.op("vector", lambda e: e.memset(afull[:], 1.0))
            p.op("vector", lambda e: e.memset(states[:], 0.0))
            p.bar()
            act(c, dct[0:1, 1:2], dct[0:1, 0:1], AF.Sigmoid)
            p.bar()
            mm(c, c.PR[0:128, 0:1], [(c.ONES1[0:1, 0:128], dct[0:1, 1:2])])
            p.bar()
            evac(c, acol[:, 0:1], c.PR[0:128, 0:1])
            p.bar()
            vec_ts(c, afull[:], afull[:], acol[:, 0:1], OP.mult)
            p.bar()
            for tc in range(4):
                for j in range(4):
                    p.dma(xns[j][:], scr_xn[(4 * tc + j) * 128:(4 * tc + j + 1) * 128, :])
                p.bar()
                for ft in range(8):
                    items = [(xns[j][:, 128 * ft:128 * (ft + 1)],
                              [(stage[:, 128 * j:128 * (j + 1)], AF.Copy, 1.0)]) for j in range(4)]
                    transpose_round2(c, items)
                    p.op("vector", lambda e, o=stage2[:], d0=afull[:], d1=stage[:], ini=states[:, ft:ft + 1]:
                         e.tensor_tensor_scan(o, d0, d1, ini, OP.mult, OP.add))
                    p.bar()
                    evac(c, Sb.tile(ft)[:, tc * 512:(tc + 1) * 512], stage2[:])
                    c.p.op("vector", lambda e, o=states[:, ft:ft + 1], i=stage2[:, 511:512]: e.tensor_copy(o, i))
                    p.bar()
            WTb = alloc_tall(ph, nc, "WTb", D, D, BF)
            brow = ph.enter_context(_sb(nc, "brow", [1, D], BF))
            Grep = ph.enter_context(_sb(nc, "grep", [128, D], F32))
            Brep = ph.enter_context(_sb(nc, "brep", [128, D], F32))
            rstage = ph.enter_context(_sb(nc, "rstage", [1, D], F32))
            cast_load(c, c.xtile, WTb, WT, D, D)
            p.dma(rstage[0:1, :], bb)
            p.bar()
            evac(c, brow[0:1, :], rstage[0:1, :])
            p.bar()
            p.dma(rstage[0:1, :], gam)
            p.bar()
            bcast_row(c, Grep[:, 0:512], rstage[0:1, 0:512])
            bcast_row(c, Grep[:, 512:1024], rstage[0:1, 512:1024])
            p.dma(rstage[0:1, :], bet)
            p.bar()
            bcast_row(c, Brep[:, 0:512], rstage[0:1, 0:512])
            bcast_row(c, Brep[:, 512:1024], rstage[0:1, 512:1024])
            for tt in range(NT):
                for ch in range(2):
                    items = [(Sb.tile(pi)[:, tt * 128:(tt + 1) * 128],
                              WTb.tile(pi)[:, ch * 512:(ch + 1) * 512]) for pi in range(8)]
                    items.append((c.ONES1B[0:1, 0:128], brow[0:1, ch * 512:(ch + 1) * 512]))
                    mm(c, c.PM[ch][0:128, 0:512], items)
                p.bar()
                for ch in range(2):
                    evac(c, c.xtile[:, ch * 512:(ch + 1) * 512], c.PM[ch][0:128, 0:512])
                p.bar()
                layernorm_tile(c, c.xn[:], c.xtile[:], tmp)
                vec_tt(c, c.xn[:], c.xn[:], Grep[:], OP.mult)
                p.bar()
                vec_tt(c, c.xn[:], c.xn[:], Brep[:], OP.add)
                p.bar()
                p.dma(scr_s[tt * 128:(tt + 1) * 128, :], c.xn[:])
                p.bar()

        NCH = [(0, 512), (512, 1024), (1024, 1536), (1536, 2048), (2048, 2560), (2560, 2816)]
        with contextlib.ExitStack() as ph:
            wgqT = alloc_tall(ph, nc, "wgqT", D, 2816, BF)
            wuqT = alloc_tall(ph, nc, "wuqT", D, 2816, BF)
            hqf = ph.enter_context(_sb(nc, "hqf", [128, 512], F32))
            tmq = mk_tmp(ph, nc, "tmq", 512)
            for (wdram, wdst) in [(wg, wgqT), (wu, wuqT)]:
                for i in range(22):
                    p.dma(c.xtile, wdram[i * 128:(i + 1) * 128, :])
                    p.bar()
                    fp4_quant(c, c.xq[:], c.xtile[:], tmp)
                    items = [(c.xq[:, 128 * k:128 * (k + 1)],
                              [(wdst.tile(k)[:, 128 * i:128 * (i + 1)], AF.Copy, 1.0)]) for k in range(8)]
                    transpose_round2(c, items)
            for tt in range(NT):
                for (n0, n1) in NCH:
                    w = n1 - n0
                    mm(c, c.PM[0][0:128, 0:w],
                       [(xqT.tile(pi)[:, tt * 128:(tt + 1) * 128], wgqT.tile(pi)[:, n0:n1]) for pi in range(8)])
                    mm(c, c.PM[1][0:128, 0:w],
                       [(xqT.tile(pi)[:, tt * 128:(tt + 1) * 128], wuqT.tile(pi)[:, n0:n1]) for pi in range(8)])
                    p.bar()
                    act(c, tmq["a"][:, 0:w], c.PM[0][0:128, 0:w], AF.Silu)
                    p.bar()
                    vec_tt(c, c.xq[:, 0:w], tmq["a"][:, 0:w], c.PM[1][0:128, 0:w], OP.mult)
                    p.bar()
                    fp4_quant(c, hqf[:, 0:w], c.xq[:, 0:w], tmq)
                    nb = w // 128
                    items = [(hqf[:, 128 * k:128 * (k + 1)],
                              [(c.hqtb[:, 128 * k:128 * (k + 1)], AF.Copy, 1.0)]) for k in range(nb)]
                    transpose_round2(c, items)
                    for k in range(nb):
                        p.dma(scr_hq[n0 + 128 * k:n0 + 128 * (k + 1), tt * 128:(tt + 1) * 128],
                              c.hqtb[:, 128 * k:128 * (k + 1)])
                    p.bar()
        with contextlib.ExitStack() as ph:
            wdqT = alloc_tall(ph, nc, "wdqT", 2816, D, BF)
            hcol = alloc_tall(ph, nc, "hcol", 2816, 128, BF)
            qstage = ph.enter_context(_sb(nc, "qstage", [128, 2816], F32))
            hqf = ph.enter_context(_sb(nc, "hqf2", [128, 512], F32))
            tmq = mk_tmp(ph, nc, "tmq2", 512)
            for i in range(8):
                p.dma(qstage[:], wd[i * 128:(i + 1) * 128, :])
                p.bar()
                for (n0, n1) in NCH:
                    w = n1 - n0
                    fp4_quant(c, hqf[:, 0:w], qstage[:, n0:n1], tmq)
                    nb = w // 128
                    items = [(hqf[:, 128 * k:128 * (k + 1)],
                              [(wdqT.tile((n0 + 128 * k) // 128)[:, 128 * i:128 * (i + 1)], AF.Copy, 1.0)])
                             for k in range(nb)]
                    transpose_round2(c, items)
            for tt in range(NT):
                for pi in range(22):
                    p.dma(hcol.tile(pi)[:], scr_hq[pi * 128:(pi + 1) * 128, tt * 128:(tt + 1) * 128])
                p.bar()
                for ch in range(2):
                    mm(c, c.PM[ch][0:128, 0:512],
                       [(hcol.tile(pi)[:], wdqT.tile(pi)[:, ch * 512:(ch + 1) * 512]) for pi in range(22)])
                p.bar()
                for ch in range(2):
                    evac(c, g.tile(tt)[:, ch * 512:(ch + 1) * 512], c.PM[ch][0:128, 0:512], func=AF.Silu)
                p.bar()

        update_phase(c, x_in, x_out, scr_s, g, dbg=dbgd)
        if fx is None:
            p.finalize(sems)
    return nc


def prep_l0(inputs):
    d = {
        "w_dec": np.asarray(inputs["L0_attn_decay"], np.float32).reshape(1, 1),
        "w_WT": np.ascontiguousarray(np.asarray(inputs["L0_attn_W"], np.float32).T),
        "w_b": np.asarray(inputs["L0_attn_b"], np.float32).reshape(1, D),
        "w_gam": np.asarray(inputs["L0_attn_gamma"], np.float32).reshape(1, D),
        "w_bet": np.asarray(inputs["L0_attn_beta"], np.float32).reshape(1, D),
        "w_wg": np.asarray(inputs["L0_ffn_w_gate"], np.float32),
        "w_wu": np.asarray(inputs["L0_ffn_w_up"], np.float32),
        "w_wd": np.asarray(inputs["L0_ffn_w_down"], np.float32),
    }
    return d


def build_l1(parts=("z", "conv", "proj", "ffn", "upd"), fx=None):
    if fx is None:
        nc = bass.Bass("TRN2", target_bir_lowering=False)
        x_in = nc.dram_tensor("xin", [T, D], F32, kind="ExternalInput").ap()
        x_out = nc.dram_tensor("xo", [T, D], F32, kind="ExternalOutput").ap()
    else:
        nc, _c, _p, x_in, x_out = fx
    w_inT = nc.dram_tensor("w_inT", [D, 2048], F32, kind="ExternalInput").ap()
    b_in = nc.dram_tensor("w_binr", [1, 2048], F32, kind="ExternalInput").ap()
    waT = nc.dram_tensor("w_waT", [256, D], F32, kind="ExternalInput").ap()
    wbT = nc.dram_tensor("w_wbT", [256, D], F32, kind="ExternalInput").ap()
    bab = nc.dram_tensor("w_bab", [1, 2048], F32, kind="ExternalInput").ap()
    w_outT = nc.dram_tensor("w_outT", [2048, D], F32, kind="ExternalInput").ap()
    b_out = nc.dram_tensor("w_bout", [1, D], F32, kind="ExternalInput").ap()
    wgT = nc.dram_tensor("w_wgT", [D, 3072], I32, kind="ExternalInput").ap()
    wuT = nc.dram_tensor("w_wuT", [D, 3072], I32, kind="ExternalInput").ap()
    wdT = nc.dram_tensor("w_wdT", [3072, D], I32, kind="ExternalInput").ap()
    C512 = nc.dram_tensor("c_C512", [512, 512], BF, kind="ExternalInput").ap()
    S512 = nc.dram_tensor("c_S512", [512, 512], BF, kind="ExternalInput").ap()
    CRD = nc.dram_tensor("c_CR", [256, 512], BF, kind="ExternalInput").ap()
    SRD = nc.dram_tensor("c_SR", [256, 512], BF, kind="ExternalInput").ap()
    ICD = nc.dram_tensor("c_IC", [512, 256], BF, kind="ExternalInput").ap()
    ISD = nc.dram_tensor("c_IS", [512, 256], BF, kind="ExternalInput").ap()
    scr_z = nc.dram_tensor("l1scr_z", [T, 2048], BF, kind="Internal").ap()
    scr_cat = nc.dram_tensor("l1scr_cat", [2048, T], BF, kind="Internal").ap()
    scr_s = nc.dram_tensor("l1scr_s", [T, D], F32, kind="Internal").ap()

    with contextlib.ExitStack() as stk:
        if fx is None:
            p = Prog(nc)
            c = mk_ctx(stk, nc, p)
            sems = [stk.enter_context(nc.semaphore(f"sem{i}")) for i in range(4)]
        else:
            c, p = _c, _p
        c.xtile = stk.enter_context(_sb(nc, "xtile", [128, D], F32))
        c.xn = stk.enter_context(_sb(nc, "xnb", [128, D], F32))
        c.xq = stk.enter_context(_sb(nc, "xqb", [128, D], F32))
        tmp = mk_tmp(stk, nc, "tmp", D)
        xnTb = alloc_tall(stk, nc, "xnTb", D, T, BF)

        phase1(c, x_in, tmp, xnTb=xnTb)

        if "z" in parts:
         with contextlib.ExitStack() as ph:
            w_inTb = alloc_tall(ph, nc, "winTb", D, 2048, BF)
            binb = ph.enter_context(_sb(nc, "binb", [1, 2048], BF))
            rstage = ph.enter_context(_sb(nc, "rstg", [1, 2048], F32))
            zbuf = ph.enter_context(_sb(nc, "zbuf", [128, 512], BF))
            for i in range(8):
                p.dma(c.xtile[:, 0:D], w_inT[i * 128:(i + 1) * 128, 0:1024])
                p.bar()
                evac(c, w_inTb.tile(i)[:, 0:1024], c.xtile[:, 0:D])
                p.bar()
                p.dma(c.xtile[:, 0:D], w_inT[i * 128:(i + 1) * 128, 1024:2048])
                p.bar()
                evac(c, w_inTb.tile(i)[:, 1024:2048], c.xtile[:, 0:D])
                p.bar()
            p.dma(rstage[0:1, :], b_in)
            p.bar()
            evac(c, binb[0:1, :], rstage[0:1, :])
            p.bar()
            for tt in range(NT):
                for ec in range(4):
                    items = [(xnTb.tile(pi)[:, tt * 128:(tt + 1) * 128],
                              w_inTb.tile(pi)[:, ec * 512:(ec + 1) * 512]) for pi in range(8)]
                    items.append((c.ONES1B[0:1, 0:128], binb[0:1, ec * 512:(ec + 1) * 512]))
                    mm(c, c.PM[0][0:128, 0:512], items)
                    p.bar()
                    evac(c, zbuf[:], c.PM[0][0:128, 0:512])
                    p.bar()
                    p.dma(scr_z[tt * 128:(tt + 1) * 128, ec * 512:(ec + 1) * 512], zbuf[:])
                    p.bar()

        if "conv" in parts:
         with contextlib.ExitStack() as ph:
            C512b = alloc_tall(ph, nc, "C512b", 512, 512, BF)
            S512b = alloc_tall(ph, nc, "S512b", 512, 512, BF)
            CRb = alloc_tall(ph, nc, "CRb", 256, 512, BF)
            SRb = alloc_tall(ph, nc, "SRb", 256, 512, BF)
            ICb = alloc_tall(ph, nc, "ICb", 512, 256, BF)
            ISb = alloc_tall(ph, nc, "ISb", 512, 256, BF)
            for tl, dr in [(C512b, C512), (S512b, S512)]:
                for i in range(4):
                    p.dma(tl.tile(i)[:], dr[i * 128:(i + 1) * 128, :])
            for tl, dr in [(CRb, CRD), (SRb, SRD)]:
                for i in range(2):
                    p.dma(tl.tile(i)[:], dr[i * 128:(i + 1) * 128, :])
            for tl, dr in [(ICb, ICD), (ISb, ISD)]:
                for i in range(4):
                    p.dma(tl.tile(i)[:], dr[i * 128:(i + 1) * 128, :])
            p.bar()
            Wc = alloc_tall(ph, nc, "Wcb", 512, 2048, BF)
            Ws = alloc_tall(ph, nc, "Wsb", 512, 2048, BF)
            watb = alloc_tall(ph, nc, "watb", 256, D, BF)
            wbtb = alloc_tall(ph, nc, "wbtb", 256, D, BF)
            for tl, dr in [(watb, waT), (wbtb, wbT)]:
                for i in range(2):
                    p.dma(c.xtile, dr[i * 128:(i + 1) * 128, :])
                    p.bar()
                    evac(c, tl.tile(i)[:], c.xtile[:])
                    p.bar()
            for (dst, lt) in [(Wc, CRb), (Ws, SRb)]:
                for po in range(4):
                    for m, wsb in [(0, watb), (1, wbtb)]:
                        for cch in range(2):
                            items = [(lt.tile(pi)[:, po * 128:(po + 1) * 128],
                                      wsb.tile(pi)[:, cch * 512:(cch + 1) * 512]) for pi in range(2)]
                            mm(c, c.PM[0][0:128, 0:512], items)
                            p.bar()
                            evac(c, dst.tile(po)[:, m * 1024 + cch * 512:m * 1024 + (cch + 1) * 512],
                                 c.PM[0][0:128, 0:512])
                            p.bar()
            babb = ph.enter_context(_sb(nc, "babb", [1, 2048], BF))
            rstage = ph.enter_context(_sb(nc, "rstg2", [1, 2048], F32))
            p.dma(rstage[0:1, :], bab)
            p.bar()
            evac(c, babb[0:1, :], rstage[0:1, :])
            p.bar()
            win = [ph.enter_context(_sb(nc, f"win{j}", [128, 2048], BF)) for j in range(4)]
            zpad = ph.enter_context(_sb(nc, "zpad", [128, 2048], BF))
            p.op("vector", lambda e: e.memset(zpad[:], 0.0))
            p.bar()
            a_bf = alloc_tall(ph, nc, "abf", 512, 512, BF)
            b_bf = alloc_tall(ph, nc, "bbf", 512, 512, BF)
            conv_all = alloc_tall(ph, nc, "conv", 256, 2048, F32)
            gab = ph.enter_context(_sb(nc, "gab", [128, 1024], F32))
            gbb = ph.enter_context(_sb(nc, "gbb", [128, 1024], F32))
            sgb = ph.enter_context(_sb(nc, "sgb", [128, 1024], F32))
            ctst = ph.enter_context(_sb(nc, "ctst", [128, 1024], BF))
            VSP = [c.PT[0], c.PT[1], c.PR, c.PS]
            for tau in range(8):
                for si in range(4):
                    zt = 2 * tau - 2 + si
                    if zt < 0:
                        pass
                    else:
                        p.dma(win[si][:], scr_z[zt * 128:(zt + 1) * 128, :])
                p.bar()
                wins = [zpad if (2 * tau - 2 + si) < 0 else win[si] for si in range(4)]
                for cc in range(4):
                    cs = slice(cc * 512, (cc + 1) * 512)
                    for po in range(4):
                        mm(c, c.PM[po][0:128, 0:512],
                           [(C512b.tile(pi)[:, po * 128:(po + 1) * 128], wins[pi][:, cs]) for pi in range(4)])
                        mm(c, VSP[po][0:128, 0:512],
                           [(S512b.tile(pi)[:, po * 128:(po + 1) * 128], wins[pi][:, cs]) for pi in range(4)])
                    p.bar()
                    for po in range(4):
                        vec_stt(c, tmp["a"][:, 0:512], c.PM[po][0:128, 0:512], 1.0, Wc.tile(po)[:, cs], OP.mult, OP.mult)
                        p.bar()
                        vec_stt(c, tmp["b"][:, 0:512], VSP[po][0:128, 0:512], 1.0, Ws.tile(po)[:, cs], OP.mult, OP.mult)
                        p.bar()
                        vec_tt(c, a_bf.tile(po)[:], tmp["a"][:, 0:512], tmp["b"][:, 0:512], OP.subtract)
                        vec_stt(c, tmp["c"][:, 0:512], c.PM[po][0:128, 0:512], 1.0, Ws.tile(po)[:, cs], OP.mult, OP.mult)
                        p.bar()
                        vec_stt(c, tmp["d"][:, 0:512], VSP[po][0:128, 0:512], 1.0, Wc.tile(po)[:, cs], OP.mult, OP.mult)
                        p.bar()
                        vec_tt(c, b_bf.tile(po)[:], tmp["c"][:, 0:512], tmp["d"][:, 0:512], OP.add)
                        p.bar()
                    for pp in range(2):
                        items = [(ICb.tile(pi)[:, pp * 128:(pp + 1) * 128], a_bf.tile(pi)[:]) for pi in range(4)]
                        items += [(ISb.tile(pi)[:, pp * 128:(pp + 1) * 128], b_bf.tile(pi)[:]) for pi in range(4)]
                        items.append((c.ONES1B[0:1, 0:128], babb[0:1, cs]))
                        mm(c, c.PM[pp][0:128, 0:512], items)
                    p.bar()
                    for pp in range(2):
                        evac(c, conv_all.tile(pp)[:, cs], c.PM[pp][0:128, 0:512])
                    p.bar()
                for pt in range(2):
                    act(c, sgb[:], conv_all.tile(pt)[:, 1024:2048], AF.Sigmoid)
                    p.bar()
                    vec_tt(c, gab[:], conv_all.tile(pt)[:, 0:1024], sgb[:], OP.mult)
                    act(c, tmp["a"][:, 0:1024], conv_all.tile(pt)[:, 0:1024], AF.Sigmoid)
                    p.bar()
                    vec_tt(c, gbb[:], conv_all.tile(pt)[:, 1024:2048], tmp["a"][:, 0:1024], OP.mult)
                    p.bar()
                    tcol = tau * 256 + pt * 128
                    items = [(gab[:, cb * 128:(cb + 1) * 128],
                              [(ctst[:, cb * 128:(cb + 1) * 128], AF.Copy, 1.0)]) for cb in range(8)]
                    transpose_round2(c, items)
                    for cb in range(8):
                        p.dma(scr_cat[cb * 128:(cb + 1) * 128, tcol:tcol + 128], ctst[:, cb * 128:(cb + 1) * 128])
                    p.bar()
                    items = [(gbb[:, cb * 128:(cb + 1) * 128],
                              [(ctst[:, cb * 128:(cb + 1) * 128], AF.Copy, 1.0)]) for cb in range(8)]
                    transpose_round2(c, items)
                    for cb in range(8):
                        p.dma(scr_cat[1024 + cb * 128:1024 + (cb + 1) * 128, tcol:tcol + 128],
                              ctst[:, cb * 128:(cb + 1) * 128])
                    p.bar()

        if "proj" in parts:
         with contextlib.ExitStack() as ph:
            w_outTb = alloc_tall(ph, nc, "woutTb", 2048, D, BF)
            boutb = ph.enter_context(_sb(nc, "boutb", [1, D], BF))
            rstage = ph.enter_context(_sb(nc, "rstg3", [1, D], F32))
            catcol = alloc_tall(ph, nc, "catcol", 2048, 128, BF)
            cast_load(c, c.xtile, w_outTb, w_outT, 2048, D)
            p.dma(rstage[0:1, :], b_out)
            p.bar()
            evac(c, boutb[0:1, :], rstage[0:1, :])
            p.bar()
            for tt in range(NT):
                for pi in range(16):
                    p.dma(catcol.tile(pi)[:], scr_cat[pi * 128:(pi + 1) * 128, tt * 128:(tt + 1) * 128])
                p.bar()
                for ch in range(2):
                    items = [(catcol.tile(pi)[:], w_outTb.tile(pi)[:, ch * 512:(ch + 1) * 512])
                             for pi in range(16)]
                    items.append((c.ONES1B[0:1, 0:128], boutb[0:1, ch * 512:(ch + 1) * 512]))
                    mm(c, c.PM[ch][0:128, 0:512], items)
                p.bar()
                for ch in range(2):
                    evac(c, c.xn[:, ch * 512:(ch + 1) * 512], c.PM[ch][0:128, 0:512])
                p.bar()
                p.dma(scr_s[tt * 128:(tt + 1) * 128, :], c.xn[:])
                p.bar()

        g = alloc_tall(stk, nc, "gacc", T, D, BF)
        if "ffn" in parts:
         with contextlib.ExitStack() as ph:
            istage = ph.enter_context(_sb(nc, "istage", [128, 1024], I32))
            yTh = alloc_tall(ph, nc, "yTh", D, 1024, F32)
            wgb = alloc_tall(ph, nc, "wgb", D, 512, BF)
            wub = alloc_tall(ph, nc, "wub", D, 512, BF)
            wdb = alloc_tall(ph, nc, "wdb", 512, D, BF)
            hTb = alloc_tall(ph, nc, "hTb", 512, 1024, BF)

            def i32load(dst_ap, src_ap, wcols):
                p.dma(istage[0:128, 0:wcols], src_ap)
                p.bar()
                c.p.op("vector", lambda e, o=c.xtile[:, 0:wcols], i=istage[0:128, 0:wcols]: e.tensor_copy(o, i))
                p.bar()
                evac(c, dst_ap, c.xtile[:, 0:wcols], scale=1.0 / 127.0)
                p.bar()

            for half in range(2):
                for ho in range(8):
                    p.op("vector", lambda e, o=yTh.tile(ho)[:]: e.memset(o, 0.0))
                p.bar()
                for chh in range(6):
                    for i in range(8):
                        i32load(wgb.tile(i)[:], wgT[i * 128:(i + 1) * 128, chh * 512:(chh + 1) * 512], 512)
                        i32load(wub.tile(i)[:], wuT[i * 128:(i + 1) * 128, chh * 512:(chh + 1) * 512], 512)
                    for i in range(4):
                        i32load(wdb.tile(i)[:], wdT[chh * 512 + i * 128:chh * 512 + (i + 1) * 128, :], 1024)
                    for io in range(4):
                        for tch in range(2):
                            t0 = half * 1024 + tch * 512
                            mm(c, c.PM[0][0:128, 0:512],
                               [(wgb.tile(pi)[:, io * 128:(io + 1) * 128], xnTb.tile(pi)[:, t0:t0 + 512])
                                for pi in range(8)])
                            mm(c, c.PM[1][0:128, 0:512],
                               [(wub.tile(pi)[:, io * 128:(io + 1) * 128], xnTb.tile(pi)[:, t0:t0 + 512])
                                for pi in range(8)])
                            p.bar()
                            act(c, tmp["a"][:, 0:512], c.PM[0][0:128, 0:512], AF.Silu)
                            p.bar()
                            vec_tt(c, hTb.tile(io)[:, tch * 512:(tch + 1) * 512],
                                   tmp["a"][:, 0:512], c.PM[1][0:128, 0:512], OP.mult)
                            p.bar()
                    for ho in range(8):
                        for tch in range(2):
                            mm(c, c.PM[2][0:128, 0:512],
                               [(wdb.tile(pi)[:, ho * 128:(ho + 1) * 128],
                                 hTb.tile(pi)[:, tch * 512:(tch + 1) * 512]) for pi in range(4)])
                            p.bar()
                            vec_tt(c, yTh.tile(ho)[:, tch * 512:(tch + 1) * 512],
                                   yTh.tile(ho)[:, tch * 512:(tch + 1) * 512], c.PM[2][0:128, 0:512], OP.add)
                            p.bar()
                for tl in range(8):
                    items = [(yTh.tile(pi)[:, tl * 128:(tl + 1) * 128],
                              [(g.tile(half * 8 + tl)[:, pi * 128:(pi + 1) * 128], AF.Silu, 1.0)])
                             for pi in range(8)]
                    transpose_round2(c, items)

        if "upd" in parts:
            update_phase(c, x_in, x_out, scr_s, g)
        else:
            for tt in range(NT):
                p.dma(c.xtile, x_in[tt * 128:(tt + 1) * 128, :])
                p.bar()
                p.dma(x_out[tt * 128:(tt + 1) * 128, :], c.xtile[:])
                p.bar()
        if fx is None:
            p.finalize(sems)
    return nc


def prep_l1(inputs):
    s_ = np.arange(512)
    f_ = np.arange(512)
    ang = 2 * np.pi * np.outer(s_, f_) / 512.0
    k_ = np.arange(256)
    angr = 2 * np.pi * np.outer(255 - k_, f_).T.astype(np.float64) / 512.0
    pp_ = np.arange(256)
    angi = 2 * np.pi * np.outer(f_, pp_ + 256) / 512.0
    d = {
        "w_inT": np.ascontiguousarray(np.asarray(inputs["L1_attn_w_in"], np.float32).T),
        "w_binr": np.asarray(inputs["L1_attn_b_in"], np.float32).reshape(1, 2048),
        "w_waT": np.ascontiguousarray(np.asarray(inputs["L1_attn_w_a"], np.float32).reshape(1024, 256).T),
        "w_wbT": np.ascontiguousarray(np.asarray(inputs["L1_attn_w_b"], np.float32).reshape(1024, 256).T),
        "w_bab": np.concatenate([np.asarray(inputs["L1_attn_b_a"], np.float32),
                                 np.asarray(inputs["L1_attn_b_b"], np.float32)]).reshape(1, 2048),
        "w_outT": np.ascontiguousarray(np.asarray(inputs["L1_attn_w_out"], np.float32).T),
        "w_bout": np.asarray(inputs["L1_attn_b_out"], np.float32).reshape(1, D),
        "w_wgT": np.ascontiguousarray(np.asarray(inputs["L1_ffn_w_gate"], np.int32).T),
        "w_wuT": np.ascontiguousarray(np.asarray(inputs["L1_ffn_w_up"], np.int32).T),
        "w_wdT": np.ascontiguousarray(np.asarray(inputs["L1_ffn_w_down"], np.int32).T),
        "c_C512": np.cos(ang).astype(BFNP),
        "c_S512": np.sin(ang).astype(BFNP),
        "c_CR": np.cos(angr.T).astype(BFNP).reshape(256, 512) if False else np.cos(2 * np.pi * np.outer(255 - k_, f_) / 512.0).astype(BFNP),
        "c_SR": np.sin(2 * np.pi * np.outer(255 - k_, f_) / 512.0).astype(BFNP),
        "c_IC": (np.cos(angi) / 512.0).astype(BFNP),
        "c_IS": (np.sin(angi) / 512.0).astype(BFNP),
    }
    return d


def attn_core(c, xnTb, ctxT, h, qt_src, kt_src, v_src, bias_fn, p):
    SCP = [c.PM[2], c.PT[0]]
    ATTs = [c.ATT, c.ATT2]
    for qc in range(4):
        qs = slice(qc * 512, (qc + 1) * 512)
        valid = [kt for kt in range(16) if bias_fn(qc, kt, None, None, probe=True)]
        n = len(valid)
        for step in range(n + 2):
            if step < n:
                kt = valid[step]
                mm(c, SCP[step % 2][0:128, 0:512],
                   [(kt_src[:, kt * 128:(kt + 1) * 128], qt_src[:, qs])])
            if 1 <= step <= n:
                j = step - 1
                bias_fn(qc, valid[j], SCP[j % 2][0:128, 0:512], ATTs[j % 2])
            if step >= 2:
                j = step - 2
                kt = valid[j]
                mm(c, c.PS[0:1, 0:512], [(c.ONESCB[0:128, 0:1], ATTs[j % 2][:])],
                   start=(j == 0), stop=(j == n - 1))
                mm(c, c.PM[3][0:64, 0:512], [(v_src.tile(kt)[:], ATTs[j % 2][:])],
                   start=(j == 0), stop=(j == n - 1))
            p.bar()
        evac(c, c.csr[0:1, 0:512], c.PS[0:1, 0:512])
        p.bar()
        c.p.op("vector", lambda e, o=c.csi[0:1, 0:512], i=c.csr[0:1, 0:512]: e.reciprocal(o, i))
        p.bar()
        mm(c, c.PR[0:64, 0:512], [(c.ONES1[0:1, 0:64], c.csi[0:1, 0:512])])
        p.bar()
        evac(c, c.rrep[0:64, 0:512], c.PR[0:64, 0:512])
        p.bar()
        vec_tt(c, ctxT.tile(h // 2)[64 * (h % 2):64 * (h % 2) + 64, qs],
               c.PM[3][0:64, 0:512], c.rrep[0:64, 0:512], OP.mult)
        p.bar()


def load_head_w(c, p, dst8, wdram, col0, stage):
    for i in range(8):
        p.dma(stage[0:128, 0:64], wdram[i * 128:(i + 1) * 128, col0:col0 + 64])
        p.bar()
        evac(c, dst8[i][0:128, 0:64], stage[0:128, 0:64])
        p.bar()


def qkv_head(c, p, xnTb, wb8, bcol, scale, QT):
    for qc in range(4):
        mm(c, c.PM[0][0:64, 0:512],
           [(wb8[pi][0:128, 0:64], xnTb.tile(pi)[:, qc * 512:(qc + 1) * 512]) for pi in range(8)])
        p.bar()
        vec_ts(c, QT[0:64, qc * 512:(qc + 1) * 512], c.PM[0][0:64, 0:512], bcol, OP.add, scale, OP.mult)
        p.bar()


def vproj_head(c, p, xnTb, wb8, bvrow_bf, Vb):
    for tt in range(NT):
        items = [(xnTb.tile(pi)[:, tt * 128:(tt + 1) * 128], wb8[pi][0:128, 0:64]) for pi in range(8)]
        items.append((c.ONES1B[0:1, 0:128], bvrow_bf[0:1, 0:64]))
        mm(c, c.PM[0][0:128, 0:64], items)
        p.bar()
        evac(c, Vb.tile(tt)[:, 0:64], c.PM[0][0:128, 0:64])
        p.bar()


def out_proj(c, p, ctxT, wob, borow_bf, scr_s):
    for tt in range(NT):
        for ch in range(2):
            items = [(ctxT.tile(pi)[:, tt * 128:(tt + 1) * 128], wob.tile(pi)[:, ch * 512:(ch + 1) * 512])
                     for pi in range(8)]
            items.append((c.ONES1B[0:1, 0:128], borow_bf[0:1, ch * 512:(ch + 1) * 512]))
            mm(c, c.PM[ch][0:128, 0:512], items)
        p.bar()
        for ch in range(2):
            evac(c, c.xn[:, ch * 512:(ch + 1) * 512], c.PM[ch][0:128, 0:512])
        p.bar()
        p.dma(scr_s[tt * 128:(tt + 1) * 128, :], c.xn[:])
        p.bar()


def build_l2(fx=None):
    if fx is None:
        nc = bass.Bass("TRN2", target_bir_lowering=False)
        x_in = nc.dram_tensor("xin", [T, D], F32, kind="ExternalInput").ap()
        x_out = nc.dram_tensor("xo", [T, D], F32, kind="ExternalOutput").ap()
    else:
        nc, _c, _p, x_in, x_out = fx
    Wq = nc.dram_tensor("w_Wq", [D, D], F32, kind="ExternalInput").ap()
    Wk = nc.dram_tensor("w_Wk", [D, D], F32, kind="ExternalInput").ap()
    Wv = nc.dram_tensor("w_Wv", [D, D], F32, kind="ExternalInput").ap()
    Wo = nc.dram_tensor("w_Wo", [D, D], F32, kind="ExternalInput").ap()
    bq = nc.dram_tensor("w_bq", [D, 1], F32, kind="ExternalInput").ap()
    bk = nc.dram_tensor("w_bk", [D, 1], F32, kind="ExternalInput").ap()
    bv = nc.dram_tensor("w_bv", [1, D], F32, kind="ExternalInput").ap()
    bo = nc.dram_tensor("w_bo", [1, D], F32, kind="ExternalInput").ap()
    relb = nc.dram_tensor("w_relb", [32, 16], F32, kind="ExternalInput").ap()
    crawk = nc.dram_tensor("w_crawk", [8, 16, 128, 512], F32, kind="ExternalInput").ap()
    gwT = nc.dram_tensor("w_gwT", [D, 8], F32, kind="ExternalInput").ap()
    wegT = nc.dram_tensor("w_wegT", [8, D, D], F32, kind="ExternalInput").ap()
    weuT = nc.dram_tensor("w_weuT", [8, D, D], F32, kind="ExternalInput").ap()
    wedT = nc.dram_tensor("w_wedT", [8, D, D], F32, kind="ExternalInput").ap()
    wsgT = nc.dram_tensor("w_wsgT", [D, 2048], F32, kind="ExternalInput").ap()
    wsuT = nc.dram_tensor("w_wsuT", [D, 2048], F32, kind="ExternalInput").ap()
    wsdT = nc.dram_tensor("w_wsdT", [2048, D], F32, kind="ExternalInput").ap()
    scr_s = nc.dram_tensor("l2scr_s", [T, D], F32, kind="Internal").ap()

    with contextlib.ExitStack() as stk:
        if fx is None:
            p = Prog(nc)
            c = mk_ctx(stk, nc, p)
            sems = [stk.enter_context(nc.semaphore(f"sem{i}")) for i in range(4)]
        else:
            c, p = _c, _p
        c.xtile = stk.enter_context(_sb(nc, "xtile", [128, D], F32))
        c.xn = stk.enter_context(_sb(nc, "xnb", [128, D], F32))
        c.xq = stk.enter_context(_sb(nc, "xqb", [128, D], F32))
        tmp = mk_tmp(stk, nc, "tmp", D)
        xnTb = alloc_tall(stk, nc, "xnTb", D, T, BF)
        g = alloc_tall(stk, nc, "gacc", T, D, BF)
        combs = [stk.enter_context(_sb(nc, f"comb{i}", [128, 8], F32)) for i in range(NT)]

        with contextlib.ExitStack() as ph:
            strip = alloc_tall(ph, nc, "strip", D, 128, F32)
            gwTb = alloc_tall(ph, nc, "gwTb", D, 8, F32)
            l2d = {"strip": strip, "gwT": gwTb,
                   "lg": ph.enter_context(_sb(nc, "lg", [128, 8], F32)),
                   "msk1": ph.enter_context(_sb(nc, "msk1", [128, 8], F32)),
                   "msk2": ph.enter_context(_sb(nc, "msk2", [128, 8], F32)),
                   "l2nd": ph.enter_context(_sb(nc, "l2nd", [128, 8], F32)),
                   "comb": combs}
            for i in range(8):
                p.dma(gwTb.tile(i)[:], gwT[i * 128:(i + 1) * 128, :])
            p.bar()
            phase1(c, x_in, tmp, xnTb=xnTb, l2=l2d)

        with contextlib.ExitStack() as ph:
            ctxT = alloc_tall(ph, nc, "ctxT", D, T, BF)
            QT = ph.enter_context(_sb(nc, "QT", [64, T], BF))
            KT = ph.enter_context(_sb(nc, "KT", [64, T], BF))
            Vb = alloc_tall(ph, nc, "Vb", T, 64, BF)
            wq8 = [ph.enter_context(_sb(nc, f"wq8_{i}", [128, 64], BF)) for i in range(8)]
            wk8 = [ph.enter_context(_sb(nc, f"wk8_{i}", [128, 64], BF)) for i in range(8)]
            wv8 = [ph.enter_context(_sb(nc, f"wv8_{i}", [128, 64], BF)) for i in range(8)]
            wst = ph.enter_context(_sb(nc, "wst", [128, 64], F32))
            bcolq = ph.enter_context(_sb(nc, "bcolq", [64, 2], F32))
            bcolk = ph.enter_context(_sb(nc, "bcolk", [64, 2], F32))
            bvrow = ph.enter_context(_sb(nc, "bvrow", [1, 64], F32))
            bvrow_bf = ph.enter_context(_sb(nc, "bvrowb", [1, 64], BF))
            rb0 = ph.enter_context(_sb(nc, "rb0", [128, 16], F32))
            rb31 = ph.enter_context(_sb(nc, "rb31", [128, 16], F32))
            rstage = ph.enter_context(_sb(nc, "rstg", [1, 16], F32))
            crb8 = alloc_tall(ph, nc, "crb8", 1024, 512, F32)
            c.ATT = ph.enter_context(_sb(nc, "ATT", [128, 512], BF))
            c.ATT2 = ph.enter_context(_sb(nc, "ATT2", [128, 512], BF))
            c.csr = ph.enter_context(_sb(nc, "csr", [2, 512], F32))
            c.csi = ph.enter_context(_sb(nc, "csi", [2, 512], F32))
            c.rrep = ph.enter_context(_sb(nc, "rrep", [64, 512], F32))
            scs = ph.enter_context(_sb(nc, "scs", [128, 512], F32))
            p.dma(rstage[0:1, :], relb[0:1, :])
            p.bar()
            mm(c, c.PR[0:128, 0:16], [(c.ONES1[0:1, 0:128], rstage[0:1, 0:16])])
            p.bar()
            evac(c, rb0[:, 0:16], c.PR[0:128, 0:16], scale=8.0)
            p.bar()
            p.dma(rstage[0:1, :], relb[31:32, :])
            p.bar()
            mm(c, c.PR[0:128, 0:16], [(c.ONES1[0:1, 0:128], rstage[0:1, 0:16])])
            p.bar()
            evac(c, rb31[:, 0:16], c.PR[0:128, 0:16], scale=8.0)
            p.bar()
            for h in range(16):
                load_head_w(c, p, wq8, Wq, 64 * h, wst)
                load_head_w(c, p, wk8, Wk, 64 * h, wst)
                load_head_w(c, p, wv8, Wv, 64 * h, wst)
                p.dma(bcolq[0:64, 0:1], bq[64 * h:64 * h + 64, 0:1])
                p.dma(bcolk[0:64, 0:1], bk[64 * h:64 * h + 64, 0:1])
                p.dma(bvrow[0:1, 0:64], bv[0:1, 64 * h:64 * h + 64])
                p.bar()
                evac(c, bvrow_bf[0:1, 0:64], bvrow[0:1, 0:64])
                p.bar()
                qkv_head(c, p, xnTb, wq8, bcolq[0:64, 0:1], 0.125, QT)
                qkv_head(c, p, xnTb, wk8, bcolk[0:64, 0:1], 1.0, KT)
                vproj_head(c, p, xnTb, wv8, bvrow_bf, Vb)

                for cl8 in range(8):
                    p.dma(crb8.tile(cl8)[:], crawk[cl8, h, :, :])
                p.bar()

                def bias_fn(qc, kt, ps, att, probe=False, h=h):
                    if probe:
                        return True
                    cls = 4 * qc - kt
                    if cls <= -4:
                        act(c, att[:], ps, AF.Exp, bias=rb0[:, h:h + 1])
                    elif cls >= 5:
                        act(c, att[:], ps, AF.Exp, bias=rb31[:, h:h + 1])
                    else:
                        vec_stt(c, scs[:], crb8.tile(cls + 3)[:], 8.0, ps, OP.mult, OP.add)
                        p.bar()
                        act(c, att[:], scs[:], AF.Exp)
                    return True
                attn_core(c, xnTb, ctxT, h, QT, KT, Vb, bias_fn, p)
            wob = alloc_tall(ph, nc, "wob", D, D, BF)
            borow = ph.enter_context(_sb(nc, "borow", [1, D], F32))
            borow_bf = ph.enter_context(_sb(nc, "borowb", [1, D], BF))
            cast_load(c, c.xtile, wob, Wo, D, D)
            p.dma(borow[0:1, :], bo)
            p.bar()
            evac(c, borow_bf[0:1, :], borow[0:1, :])
            p.bar()
            out_proj(c, p, ctxT, wob, borow_bf, scr_s)

        with contextlib.ExitStack() as ph:
            yTh = alloc_tall(ph, nc, "yTh", D, 1024, F32)
            wegb = alloc_tall(ph, nc, "wegb", D, D, BF)
            weub = alloc_tall(ph, nc, "weub", D, D, BF)
            wedb = alloc_tall(ph, nc, "wedb", D, D, BF)
            hsT = alloc_tall(ph, nc, "hsT", D, 1024, BF)
            combrow = ph.enter_context(_sb(nc, "combrow", [1, T], F32))
            crep = ph.enter_context(_sb(nc, "crep", [128, 512], F32))
            for half in range(2):
                for ho in range(8):
                    p.op("vector", lambda e, o=yTh.tile(ho)[:]: e.memset(o, 0.0))
                p.bar()
                for e_ in range(8):
                    cast_load(c, c.xtile, wegb, wegT[e_], D, D)
                    cast_load(c, c.xtile, weub, weuT[e_], D, D)
                    items = [(combs[tt][:, e_:e_ + 1],
                              [(combrow[0:1, tt * 128:(tt + 1) * 128], AF.Copy, 1.0)]) for tt in range(NT)]
                    transpose_round2(c, items)
                    for tch in range(2):
                        t0 = half * 1024 + tch * 512
                        mm(c, c.PR[0:128, 0:512], [(c.ONES1[0:1, 0:128], combrow[0:1, t0:t0 + 512])])
                        p.bar()
                        evac(c, crep[:], c.PR[0:128, 0:512])
                        p.bar()
                        for io in range(8):
                            mm(c, c.PM[0][0:128, 0:512],
                               [(wegb.tile(pi)[:, io * 128:(io + 1) * 128], xnTb.tile(pi)[:, t0:t0 + 512])
                                for pi in range(8)])
                            mm(c, c.PM[1][0:128, 0:512],
                               [(weub.tile(pi)[:, io * 128:(io + 1) * 128], xnTb.tile(pi)[:, t0:t0 + 512])
                                for pi in range(8)])
                            p.bar()
                            act(c, tmp["a"][:, 0:512], c.PM[0][0:128, 0:512], AF.Silu)
                            p.bar()
                            vec_tt(c, tmp["b"][:, 0:512], tmp["a"][:, 0:512], c.PM[1][0:128, 0:512], OP.mult)
                            p.bar()
                            vec_tt(c, hsT.tile(io)[:, tch * 512:(tch + 1) * 512],
                                   tmp["b"][:, 0:512], crep[:], OP.mult)
                            p.bar()
                    cast_load(c, c.xtile, wedb, wedT[e_], D, D)
                    for ho in range(8):
                        for tch in range(2):
                            mm(c, c.PM[2][0:128, 0:512],
                               [(wedb.tile(pi)[:, ho * 128:(ho + 1) * 128],
                                 hsT.tile(pi)[:, tch * 512:(tch + 1) * 512]) for pi in range(8)])
                            p.bar()
                            vec_tt(c, yTh.tile(ho)[:, tch * 512:(tch + 1) * 512],
                                   yTh.tile(ho)[:, tch * 512:(tch + 1) * 512], c.PM[2][0:128, 0:512], OP.add)
                            p.bar()
                for chh in range(4):
                    i0 = chh * 512
                    cast_load(c, c.xtile, wegb, wsgT, D, 512, col0=i0)
                    cast_load(c, c.xtile, weub, wsuT, D, 512, col0=i0)
                    for i in range(4):
                        p.dma(c.xtile, wsdT[i0 + i * 128:i0 + (i + 1) * 128, :])
                        p.bar()
                        evac(c, wedb.tile(i)[:], c.xtile[:])
                        p.bar()
                    for io in range(4):
                        for tch in range(2):
                            t0 = half * 1024 + tch * 512
                            mm(c, c.PM[0][0:128, 0:512],
                               [(wegb.tile(pi)[:, io * 128:(io + 1) * 128], xnTb.tile(pi)[:, t0:t0 + 512])
                                for pi in range(8)])
                            mm(c, c.PM[1][0:128, 0:512],
                               [(weub.tile(pi)[:, io * 128:(io + 1) * 128], xnTb.tile(pi)[:, t0:t0 + 512])
                                for pi in range(8)])
                            p.bar()
                            act(c, tmp["a"][:, 0:512], c.PM[0][0:128, 0:512], AF.Silu)
                            p.bar()
                            vec_tt(c, hsT.tile(io)[:, tch * 512:(tch + 1) * 512],
                                   tmp["a"][:, 0:512], c.PM[1][0:128, 0:512], OP.mult)
                            p.bar()
                    for ho in range(8):
                        for tch in range(2):
                            mm(c, c.PM[2][0:128, 0:512],
                               [(wedb.tile(pi)[:, ho * 128:(ho + 1) * 128],
                                 hsT.tile(pi)[:, tch * 512:(tch + 1) * 512]) for pi in range(4)])
                            p.bar()
                            vec_tt(c, yTh.tile(ho)[:, tch * 512:(tch + 1) * 512],
                                   yTh.tile(ho)[:, tch * 512:(tch + 1) * 512], c.PM[2][0:128, 0:512], OP.add)
                            p.bar()
                for tl in range(8):
                    items = [(yTh.tile(pi)[:, tl * 128:(tl + 1) * 128],
                              [(g.tile(half * 8 + tl)[:, pi * 128:(pi + 1) * 128], AF.Silu, 1.0)])
                             for pi in range(8)]
                    transpose_round2(c, items)

        update_phase(c, x_in, x_out, scr_s, g)
        if fx is None:
            p.finalize(sems)
    return nc


def _bucket_np(d):
    rp = np.maximum(d, 0)
    is_small = rp < 16
    rp_f = np.maximum(rp, 1).astype(np.float32)
    large = 16 + (np.log(rp_f / np.float32(16.0)) / np.log(64.0 / 16.0) * 16).astype(np.int32)
    large = np.minimum(large, 31)
    return np.where(is_small, rp, large)


def prep_l2(inputs):
    relb = np.asarray(inputs["L2_attn_rel_bias"], np.float32)
    cls_ = np.arange(-3, 5)
    p_ = np.arange(128)
    f_ = np.arange(512)
    dmat = 128 * cls_[:, None, None] + f_[None, None, :] - p_[None, :, None]
    buck = _bucket_np(dmat)
    crawk = relb[buck]
    crawk = np.ascontiguousarray(np.transpose(crawk, (0, 3, 1, 2)))
    d = {
        "w_Wq": np.asarray(inputs["L2_attn_Wq"], np.float32),
        "w_Wk": np.asarray(inputs["L2_attn_Wk"], np.float32),
        "w_Wv": np.asarray(inputs["L2_attn_Wv"], np.float32),
        "w_Wo": np.asarray(inputs["L2_attn_Wo"], np.float32),
        "w_bq": np.asarray(inputs["L2_attn_bq"], np.float32).reshape(D, 1),
        "w_bk": np.asarray(inputs["L2_attn_bk"], np.float32).reshape(D, 1),
        "w_bv": np.asarray(inputs["L2_attn_bv"], np.float32).reshape(1, D),
        "w_bo": np.asarray(inputs["L2_attn_bo"], np.float32).reshape(1, D),
        "w_relb": relb,
        "w_crawk": crawk,
        "w_gwT": np.ascontiguousarray(np.asarray(inputs["L2_ffn_gate_w"], np.float32).T),
        "w_wegT": np.ascontiguousarray(np.transpose(np.asarray(inputs["L2_ffn_we_gate"], np.float32), (0, 2, 1))),
        "w_weuT": np.ascontiguousarray(np.transpose(np.asarray(inputs["L2_ffn_we_up"], np.float32), (0, 2, 1))),
        "w_wedT": np.ascontiguousarray(np.transpose(np.asarray(inputs["L2_ffn_we_down"], np.float32), (0, 2, 1))),
        "w_wsgT": np.ascontiguousarray(np.asarray(inputs["L2_ffn_ws_gate"], np.float32).T),
        "w_wsuT": np.ascontiguousarray(np.asarray(inputs["L2_ffn_ws_up"], np.float32).T),
        "w_wsdT": np.ascontiguousarray(np.asarray(inputs["L2_ffn_ws_down"], np.float32).T),
    }
    return d


def build_l3(fx=None):
    if fx is None:
        nc = bass.Bass("TRN2", target_bir_lowering=False)
        x_in = nc.dram_tensor("xin", [T, D], F32, kind="ExternalInput").ap()
        x_out = nc.dram_tensor("xo", [T, D], F32, kind="ExternalOutput").ap()
    else:
        nc, _c, _p, x_in, x_out = fx
    wqkv = nc.dram_tensor("w_qkv", [D, 3072], F32, kind="ExternalInput").ap()
    bqkv = nc.dram_tensor("w_bqkv", [3072, 1], F32, kind="ExternalInput").ap()
    bqkvr = nc.dram_tensor("w_bqkvr", [1, 3072], F32, kind="ExternalInput").ap()
    wproj = nc.dram_tensor("w_proj", [D, D], F32, kind="ExternalInput").ap()
    bproj = nc.dram_tensor("w_bproj", [1, D], F32, kind="ExternalInput").ap()
    wfr = nc.dram_tensor("w_wfr", [D, D], F32, kind="ExternalInput").ap()
    wfi = nc.dram_tensor("w_wfi", [D, D], F32, kind="ExternalInput").ap()
    fbias = nc.dram_tensor("w_fbias", [1, D], F32, kind="ExternalInput").ap()
    C1024 = nc.dram_tensor("c_C1024", [D, D], BF, kind="ExternalInput").ap()
    S1024 = nc.dram_tensor("c_S1024", [D, D], BF, kind="ExternalInput").ap()
    negm = nc.dram_tensor("c_negm", [4, 128, 512], F32, kind="ExternalInput").ap()
    scr_s = nc.dram_tensor("l3scr_s", [T, D], F32, kind="Internal").ap()

    with contextlib.ExitStack() as stk:
        if fx is None:
            p = Prog(nc)
            c = mk_ctx(stk, nc, p)
            sems = [stk.enter_context(nc.semaphore(f"sem{i}")) for i in range(4)]
        else:
            c, p = _c, _p
        c.xtile = stk.enter_context(_sb(nc, "xtile", [128, D], F32))
        c.xn = stk.enter_context(_sb(nc, "xnb", [128, D], F32))
        c.xq = stk.enter_context(_sb(nc, "xqb", [128, D], F32))
        tmp = mk_tmp(stk, nc, "tmp", D)
        xnTb = alloc_tall(stk, nc, "xnTb", D, T, BF)
        g = alloc_tall(stk, nc, "gacc", T, D, BF)

        phase1(c, x_in, tmp, xnTb=xnTb)

        with contextlib.ExitStack() as ph:
            ctxT = alloc_tall(ph, nc, "ctxT", D, T, BF)
            QT = ph.enter_context(_sb(nc, "QT", [64, T], BF))
            KT = ph.enter_context(_sb(nc, "KT", [64, T], BF))
            Vb = alloc_tall(ph, nc, "Vb", T, 64, BF)
            wq8 = [ph.enter_context(_sb(nc, f"wq8_{i}", [128, 64], BF)) for i in range(8)]
            wk8 = [ph.enter_context(_sb(nc, f"wk8_{i}", [128, 64], BF)) for i in range(8)]
            wv8 = [ph.enter_context(_sb(nc, f"wv8_{i}", [128, 64], BF)) for i in range(8)]
            wst = ph.enter_context(_sb(nc, "wst", [128, 64], F32))
            bcolq = ph.enter_context(_sb(nc, "bcolq", [64, 2], F32))
            bcolk = ph.enter_context(_sb(nc, "bcolk", [64, 2], F32))
            bvrow = ph.enter_context(_sb(nc, "bvrow", [1, 64], F32))
            bvrow_bf = ph.enter_context(_sb(nc, "bvrowb", [1, 64], BF))
            nmt = [ph.enter_context(_sb(nc, f"nmt{i}", [128, 512], F32)) for i in range(4)]
            c.ATT = ph.enter_context(_sb(nc, "ATT", [128, 512], BF))
            c.ATT2 = ph.enter_context(_sb(nc, "ATT2", [128, 512], BF))
            c.csr = ph.enter_context(_sb(nc, "csr", [2, 512], F32))
            c.csi = ph.enter_context(_sb(nc, "csi", [2, 512], F32))
            c.rrep = ph.enter_context(_sb(nc, "rrep", [64, 512], F32))
            scs = ph.enter_context(_sb(nc, "scs", [128, 512], F32))
            for i in range(4):
                p.dma(nmt[i][:], negm[i, :, :])
            p.bar()
            for h in range(16):
                load_head_w(c, p, wq8, wqkv, 64 * h, wst)
                load_head_w(c, p, wk8, wqkv, 1024 + 64 * h, wst)
                load_head_w(c, p, wv8, wqkv, 2048 + 64 * h, wst)
                p.dma(bcolq[0:64, 0:1], bqkv[64 * h:64 * h + 64, 0:1])
                p.dma(bcolk[0:64, 0:1], bqkv[1024 + 64 * h:1024 + 64 * h + 64, 0:1])
                p.dma(bvrow[0:1, 0:64], bqkvr[0:1, 2048 + 64 * h:2048 + 64 * h + 64])
                p.bar()
                evac(c, bvrow_bf[0:1, 0:64], bvrow[0:1, 0:64])
                p.bar()
                qkv_head(c, p, xnTb, wq8, bcolq[0:64, 0:1], 0.125, QT)
                qkv_head(c, p, xnTb, wk8, bcolk[0:64, 0:1], 1.0, KT)
                vproj_head(c, p, xnTb, wv8, bvrow_bf, Vb)

                def bias_fn(qc, kt, ps, att, probe=False):
                    cls = 4 * qc - kt
                    if probe:
                        return cls >= -3
                    if cls >= 1:
                        act(c, att[:], ps, AF.Exp)
                    else:
                        vec_tt(c, scs[:], ps, nmt[cls + 3][:], OP.add)
                        p.bar()
                        act(c, att[:], scs[:], AF.Exp)
                    return True
                attn_core(c, xnTb, ctxT, h, QT, KT, Vb, bias_fn, p)
            wob = alloc_tall(ph, nc, "wob", D, D, BF)
            borow = ph.enter_context(_sb(nc, "borow", [1, D], F32))
            borow_bf = ph.enter_context(_sb(nc, "borowb", [1, D], BF))
            cast_load(c, c.xtile, wob, wproj, D, D)
            p.dma(borow[0:1, :], bproj)
            p.bar()
            evac(c, borow_bf[0:1, :], borow[0:1, :])
            p.bar()
            out_proj(c, p, ctxT, wob, borow_bf, scr_s)

        with contextlib.ExitStack() as ph:
            Cb = alloc_tall(ph, nc, "Cb", D, D, BF)
            Sb_ = alloc_tall(ph, nc, "Sb_", D, D, BF)
            ATb = alloc_tall(ph, nc, "ATb", D, D, BF)
            BTb = alloc_tall(ph, nc, "BTb", D, D, BF)
            for i in range(8):
                p.dma(Cb.tile(i)[:], C1024[i * 128:(i + 1) * 128, :])
                p.dma(Sb_.tile(i)[:], S1024[i * 128:(i + 1) * 128, :])
            p.bar()
            with contextlib.ExitStack() as ph2:
                Wrb = alloc_tall(ph2, nc, "Wrb", D, D, BF)
                Wib = alloc_tall(ph2, nc, "Wib", D, D, BF)
                cast_load(c, c.xtile, Wrb, wfr, D, D)
                cast_load(c, c.xtile, Wib, wfi, D, D)
                for po in range(8):
                    for ch in range(2):
                        cs = slice(ch * 512, (ch + 1) * 512)
                        items = [(Wrb.tile(pi)[:, po * 128:(po + 1) * 128], Cb.tile(pi)[:, cs]) for pi in range(8)]
                        items += [(Wib.tile(pi)[:, po * 128:(po + 1) * 128], Sb_.tile(pi)[:, cs]) for pi in range(8)]
                        mm(c, c.PM[0][0:128, 0:512], items)
                        p.bar()
                        evac(c, ATb.tile(po)[:, cs], c.PM[0][0:128, 0:512])
                        p.bar()
                for i in range(8):
                    evac(c, Wib.tile(i)[:], Wib.tile(i)[:], scale=-1.0)
                p.bar()
                for po in range(8):
                    for ch in range(2):
                        cs = slice(ch * 512, (ch + 1) * 512)
                        items = [(Wrb.tile(pi)[:, po * 128:(po + 1) * 128], Sb_.tile(pi)[:, cs]) for pi in range(8)]
                        items += [(Wib.tile(pi)[:, po * 128:(po + 1) * 128], Cb.tile(pi)[:, cs]) for pi in range(8)]
                        mm(c, c.PM[0][0:128, 0:512], items)
                        p.bar()
                        evac(c, BTb.tile(po)[:, cs], c.PM[0][0:128, 0:512])
                        p.bar()
            Mb = alloc_tall(ph, nc, "Mb", D, D, BF)
            for po in range(8):
                for ch in range(2):
                    cs = slice(ch * 512, (ch + 1) * 512)
                    items = [(ATb.tile(pi)[:, po * 128:(po + 1) * 128], Cb.tile(pi)[:, cs]) for pi in range(8)]
                    items += [(BTb.tile(pi)[:, po * 128:(po + 1) * 128], Sb_.tile(pi)[:, cs]) for pi in range(8)]
                    mm(c, c.PM[0][0:128, 0:512], items)
                    p.bar()
                    evac(c, Mb.tile(po)[:, cs], c.PM[0][0:128, 0:512], scale=1.0 / 1024.0)
                    p.bar()
            fbb = ph.enter_context(_sb(nc, "fbb", [1, D], BF))
            rstage = ph.enter_context(_sb(nc, "rstg", [1, D], F32))
            p.dma(rstage[0:1, :], fbias)
            p.bar()
            evac(c, fbb[0:1, :], rstage[0:1, :])
            p.bar()
            for tt in range(NT):
                for ch in range(2):
                    cs = slice(ch * 512, (ch + 1) * 512)
                    items = [(xnTb.tile(pi)[:, tt * 128:(tt + 1) * 128], Mb.tile(pi)[:, cs]) for pi in range(8)]
                    items.append((c.ONES1B[0:1, 0:128], fbb[0:1, cs]))
                    mm(c, c.PM[ch][0:128, 0:512], items)
                p.bar()
                for ch in range(2):
                    evac(c, g.tile(tt)[:, ch * 512:(ch + 1) * 512], c.PM[ch][0:128, 0:512], func=AF.Silu)
                p.bar()

        update_phase(c, x_in, x_out, scr_s, g)
        if fx is None:
            p.finalize(sems)
    return nc


def prep_l3(inputs):
    i_ = np.arange(1024)
    ang = 2 * np.pi * np.outer(i_, i_) / 1024.0
    cls_ = np.arange(-3, 1)
    p_ = np.arange(128)
    f_ = np.arange(512)
    dmat = 128 * cls_[:, None, None] + f_[None, None, :] - p_[None, :, None]
    negm = np.where(dmat >= 0, 0.0, -1e30).astype(np.float32)
    b = np.asarray(inputs["L3_attn_b_qkv"], np.float32)
    d = {
        "w_qkv": np.asarray(inputs["L3_attn_w_qkv"], np.float32),
        "w_bqkv": b.reshape(3072, 1),
        "w_bqkvr": b.reshape(1, 3072),
        "w_proj": np.asarray(inputs["L3_attn_w_proj"], np.float32),
        "w_bproj": np.asarray(inputs["L3_attn_b_proj"], np.float32).reshape(1, D),
        "w_wfr": np.asarray(inputs["L3_ffn_w_freq_real"], np.float32),
        "w_wfi": np.asarray(inputs["L3_ffn_w_freq_imag"], np.float32),
        "w_fbias": np.asarray(inputs["L3_ffn_bias"], np.float32).reshape(1, D),
        "c_C1024": np.cos(ang).astype(BFNP),
        "c_S1024": np.sin(ang).astype(BFNP),
        "c_negm": negm,
    }
    return d


def kernel_unfused(**inputs):
    consts = host_consts()
    x = np.ascontiguousarray(np.asarray(inputs["L0_attn_Z"], np.float32))
    cur = [np.ascontiguousarray(x[b]) for b in range(B)]
    builders = [(build_l0, prep_l0), (build_l1, prep_l1), (build_l2, prep_l2), (build_l3, prep_l3)]
    for li, (bf, pf) in enumerate(builders):
        nc = bf()
        wmap = pf(inputs)
        wmap.update(consts)
        in_maps = [dict(wmap, xin=cur[b]) for b in range(B)]
        res = run_bass_kernel_spmd(nc, in_maps, core_ids=list(range(B)))
        cur = [np.ascontiguousarray(np.asarray(res.results[b]["xo"], np.float32)) for b in range(B)]
    return np.stack(cur, 0)


def build_fused():
    nc = bass.Bass("TRN2", target_bir_lowering=False)
    x0 = nc.dram_tensor("xin", [T, D], F32, kind="ExternalInput").ap()
    x4 = nc.dram_tensor("xo", [T, D], F32, kind="ExternalOutput").ap()
    x1 = nc.dram_tensor("ix1", [T, D], F32, kind="Internal").ap()
    x2 = nc.dram_tensor("ix2", [T, D], F32, kind="Internal").ap()
    x3 = nc.dram_tensor("ix3", [T, D], F32, kind="Internal").ap()
    with contextlib.ExitStack() as stk:
        p = Prog(nc)
        c = mk_ctx(stk, nc, p)
        sems = [stk.enter_context(nc.semaphore(f"sem{i}")) for i in range(4)]
        build_l0(fx=(nc, c, p, x0, x1))
        build_l1(fx=(nc, c, p, x1, x2))
        build_l2(fx=(nc, c, p, x2, x3))
        build_l3(fx=(nc, c, p, x3, x4))
        p.finalize(sems)
    return nc


def kernel(**inputs):
    consts = host_consts()
    x = np.ascontiguousarray(np.asarray(inputs["L0_attn_Z"], np.float32))
    nc = build_fused()
    wmap = dict(consts)
    for pf in (prep_l0, prep_l1, prep_l2, prep_l3):
        wmap.update(pf(inputs))
    in_maps = [dict(wmap, xin=np.ascontiguousarray(x[b])) for b in range(B)]
    res = run_bass_kernel_spmd(nc, in_maps, core_ids=list(range(B)))
    return np.stack([np.ascontiguousarray(np.asarray(res.results[b]["xo"], np.float32)) for b in range(B)], 0)


# version 04 (final)
# speedup vs baseline: 1.3473x; 1.0511x over previous
import math
import numpy as np
import ml_dtypes
import concourse.bass as bass
import concourse.mybir as mybir
from concourse.bass_utils import run_bass_kernel_spmd
from concourse.alu_op_type import AluOpType as OP

AF = mybir.ActivationFunctionType
AX = mybir.AxisListType
F32 = mybir.dt.float32
BF = mybir.dt.bfloat16
I32 = mybir.dt.int32
BFNP = ml_dtypes.bfloat16

B, T, D = 8, 2048, 1024
NT = T // 128


def _ap(x):
    return x if isinstance(x, bass.AP) else x[:]


_UNIQ = [0]


def _sb(nc, name, shape, dtype):
    _UNIQ[0] += 1
    return nc.sbuf_tensor(f"{name}_u{_UNIQ[0]}", shape, dtype)


class Prog:
    def __init__(self, nc, nsem=4):
        self.nc = nc
        self.nsem = nsem
        self.stages = []
        self.cur = []

    def op(self, eng, fn, inc=1):
        self.cur.append((eng, fn, inc))

    def dma(self, out, in_, eng="sync"):
        out = _ap(out); in_ = _ap(in_)
        self.op(eng, lambda e, o=out, i=in_: e.dma_start(out=o, in_=i), 16)

    def bar(self):
        if self.cur:
            self.stages.append(self.cur)
            self.cur = []

    def finalize(self, sems):
        self.bar()
        R = len(sems)
        cum = [0] * R
        tgt = []
        for s, st in enumerate(self.stages):
            k = s % R
            cum[k] += sum(i for (_, _, i) in st)
            tgt.append((k, cum[k]))

        def emit(eng_obj, name):
            for s, st in enumerate(self.stages):
                ops = [(f, i) for (e, f, i) in st if e == name]
                if not ops:
                    continue
                if s > 0:
                    k, v = tgt[s - 1]
                    eng_obj.wait_ge(sems[k], v)
                for f, i in ops:
                    inst = f(eng_obj)
                    inst.then_inc(sems[s % R], i)

        nc = self.nc
        with nc.Block() as blk:
            @blk.sync
            def _(e):
                emit(e, "sync")
                k, v = tgt[-1]
                e.wait_ge(sems[k], v)

            @blk.tensor
            def _(e):
                emit(e, "tensor")

            @blk.vector
            def _(e):
                emit(e, "vector")

            @blk.scalar
            def _(e):
                emit(e, "scalar")

            @blk.gpsimd
            def _(e):
                emit(e, "gpsimd")


class Tall:

    def __init__(self, tiles):
        self.t = tiles

    def tile(self, i):
        return self.t[i]

    def __len__(self):
        return len(self.t)


def alloc_tall(stk, nc, name, rows, cols, dtype):
    n = (rows + 127) // 128
    tiles = []
    for i in range(n):
        r = min(128, rows - 128 * i)
        h = stk.enter_context(_sb(nc, f"{name}_{i}", [r, cols], dtype))
        tiles.append(h)
    return Tall(tiles)



class Ctx:
    pass


def mk_ctx(stk, nc, p):
    c = Ctx()
    c.nc = nc
    c.p = p
    c.stk = stk
    c.PT = [stk.enter_context(nc.psum_tensor(f"pt{i}", [128, 512], F32)) for i in range(2)]
    c.PM = [stk.enter_context(nc.psum_tensor(f"pm{i}", [128, 512], F32)) for i in range(4)]
    c.PR = stk.enter_context(nc.psum_tensor("pr", [128, 512], F32))
    c.PS = stk.enter_context(nc.psum_tensor("ps", [128, 512], F32))
    c.IDENT = stk.enter_context(_sb(nc, "ident", [128, 128], F32))
    c.ONES1 = stk.enter_context(_sb(nc, "ones1", [1, 128], F32))
    c.ONES1B = stk.enter_context(_sb(nc, "ones1b", [1, 128], BF))
    c.ONESCB = stk.enter_context(_sb(nc, "onescb", [128, 1], BF))
    ident = nc.dram_tensor("c_ident", [128, 128], F32, kind="ExternalInput").ap()
    ones1 = nc.dram_tensor("c_ones1", [1, 128], F32, kind="ExternalInput").ap()
    ones1b = nc.dram_tensor("c_ones1b", [1, 128], BF, kind="ExternalInput").ap()
    onescb = nc.dram_tensor("c_onescb", [128, 1], BF, kind="ExternalInput").ap()
    p.dma(c.IDENT[:], ident)
    p.dma(c.ONES1[:], ones1)
    p.dma(c.ONES1B[:], ones1b)
    p.dma(c.ONESCB[:], onescb)
    c.CC = stk.enter_context(_sb(nc, "cc", [128, 4], F32))
    p.op("vector", lambda e: e.memset(c.CC[:, 0:1], 1e-5))
    p.bar()
    c.EPS = c.CC[:, 0:1]
    return c


def host_consts():
    return {
        "c_ident": np.eye(128, dtype=np.float32),
        "c_ones1": np.ones((1, 128), np.float32),
        "c_ones1b": np.ones((1, 128), BFNP),
        "c_onescb": np.ones((128, 1), BFNP),
    }


def mm(c, out, items, start=True, stop=True):
    out = _ap(out)
    items = [(_ap(l), _ap(r)) for (l, r) in items]
    def fn(e, out=out, items=list(items)):
        n = len(items)
        inst = None
        for i, (l, r) in enumerate(items):
            inst = e.matmul(out, l, r, start=(start and i == 0), stop=(stop and i == n - 1))
        return inst
    c.p.op("tensor", fn)


def transpose_round(c, items):
    for g0 in range(0, len(items), 8):
        grp = items[g0:g0 + 8]
        for i, it in enumerate(grp):
            src = it[0]
            a, bb = src.shape
            slot = c.PT[i // 4][0:bb, 128 * (i % 4):128 * (i % 4) + a]
            c.p.op("tensor", lambda e, s=src, o=slot, aa=a: e.transpose(o, s, c.IDENT[0:aa, 0:aa]))
        c.p.bar()
        for i, it in enumerate(grp):
            src, dst = it[0], it[1]
            func = it[2] if len(it) > 2 else AF.Copy
            scale = it[3] if len(it) > 3 else 1.0
            a, bb = src.shape
            slot = c.PT[i // 4][0:bb, 128 * (i % 4):128 * (i % 4) + a]
            c.p.op("scalar", lambda e, o=dst, s=slot, f=func, sc=scale: e.activation(o, s, f, scale=sc))
        c.p.bar()


def evac(c, dst, src, func=AF.Copy, scale=1.0, bias=...):
    dst = _ap(dst); src = _ap(src)
    c.p.op("scalar", lambda e, o=dst, s=src, f=func, sc=scale, b=bias:
           e.activation(o, s, f, scale=sc) if b is ... else e.activation(o, s, f, scale=sc, bias=b))


def vec_tt(c, out, in0, in1, op):
    out = _ap(out); in0 = _ap(in0); in1 = _ap(in1)
    c.p.op("vector", lambda e, o=out, a=in0, b=in1, oo=op: e.tensor_tensor(o, a, b, oo))


def vec_ts(c, out, in0, s1, op0, s2=None, op1=None):
    out = _ap(out); in0 = _ap(in0)
    s1 = _ap(s1) if not isinstance(s1, (int, float)) else s1
    if s2 is None:
        c.p.op("vector", lambda e, o=out, a=in0, s=s1, oo=op0: e.tensor_scalar(o, a, s, None, oo))
    else:
        c.p.op("vector", lambda e, o=out, a=in0, sa=s1, sb=s2, o0=op0, o1=op1:
               e.tensor_scalar(o, a, sa, sb, o0, o1))


def vec_stt(c, out, in0, scalar, in1, op0, op1):
    out = _ap(out); in0 = _ap(in0); in1 = _ap(in1)
    scalar = _ap(scalar) if not isinstance(scalar, (int, float)) else scalar
    c.p.op("vector", lambda e, o=out, a=in0, s=scalar, b=in1, o0=op0, o1=op1:
           e.scalar_tensor_tensor(o, a, s, b, o0, o1))


def vec_red(c, out, in_, op):
    out = _ap(out); in_ = _ap(in_)
    c.p.op("vector", lambda e, o=out, i=in_, oo=op: e.tensor_reduce(o, i, AX.X, oo))


def act(c, out, in_, func, scale=1.0, bias=...):
    out = _ap(out); in_ = _ap(in_)
    if bias is not ... and not isinstance(bias, (int, float)):
        bias = _ap(bias)
    c.p.op("scalar", lambda e, o=out, i=in_, f=func, sc=scale, b=bias:
           e.activation(o, i, f, scale=sc) if b is ... else e.activation(o, i, f, scale=sc, bias=b))


def bcast_row(c, dst, row, n=128, lhs=None):
    F = row.shape[1]
    lhsT = (lhs if lhs is not None else c.ONES1)[0:1, 0:n]
    mm(c, c.PR[0:n, 0:F], [(lhsT, row)])
    c.p.bar()
    evac(c, dst, c.PR[0:n, 0:F])
    c.p.bar()


def layernorm_tile(c, xn_out, x_in, tmp, eps=None):
    Dd = x_in.shape[1]
    vec_red(c, tmp["r1"][:, 0:1], x_in, OP.add)
    c.p.bar()
    c.p.op("scalar", lambda e, o=tmp["r2"][:, 0:1], i=tmp["r1"][:, 0:1]: e.mul(o, i, -1.0 / Dd))
    c.p.bar()
    vec_ts(c, tmp["a"], x_in, tmp["r2"][:, 0:1], OP.add)
    c.p.bar()
    act(c, tmp["b"], tmp["a"], AF.Square)
    c.p.bar()
    vec_red(c, tmp["r1"][:, 0:1], tmp["b"], OP.add)
    c.p.bar()
    act(c, tmp["r2"][:, 0:1], tmp["r1"][:, 0:1], AF.Sqrt, scale=1.0 / Dd, bias=c.EPS)
    c.p.bar()
    c.p.op("vector", lambda e, o=tmp["r1"][:, 0:1], i=tmp["r2"][:, 0:1]: e.reciprocal(o, i))
    c.p.bar()
    vec_ts(c, xn_out, tmp["a"], tmp["r1"][:, 0:1], OP.mult)
    c.p.bar()


def fp4_quant(c, q_out_f32, x_in, tmp):
    F = x_in.shape[1]
    nb = F // 16
    absx = tmp["a"]
    act(c, absx[:, 0:F], x_in, AF.Abs)
    c.p.bar()
    a3 = absx[:, 0:F].rearrange("p (b k) -> p b k", k=16)
    vec_red(c, tmp["s"][:, 0:nb], a3, OP.max)
    c.p.bar()
    vec_ts(c, tmp["s2"][:, 0:nb], tmp["s"][:, 0:nb], 1.0 / 6.0, OP.mult, 1e-12, OP.max)
    c.p.bar()
    c.p.op("vector", lambda e, o=tmp["s"][:, 0:nb], i=tmp["s2"][:, 0:nb]: e.reciprocal(o, i))
    c.p.bar()
    recb = tmp["s"][:, 0:nb].rearrange("p (b o) -> p b o", o=1).broadcast_to((128, nb, 16))
    scb = tmp["s2"][:, 0:nb].rearrange("p (b o) -> p b o", o=1).broadcast_to((128, nb, 16))
    tq = tmp["b"]
    vec_tt(c, tq[:, 0:F].rearrange("p (b k) -> p b k", k=16), a3, recb, OP.mult)
    c.p.bar()
    accs = [tmp["c"], tmp["d"]]
    vec_ts(c, accs[0][:, 0:F], tq[:, 0:F], 0.25, OP.is_gt, 0.5, OP.mult)
    c.p.bar()
    cur = 0
    for mid, w in [(0.75, 0.5), (1.25, 0.5), (1.75, 0.5), (2.5, 1.0), (3.5, 1.0), (5.0, 2.0)]:
        vec_ts(c, tmp["e"][:, 0:F], tq[:, 0:F], mid, OP.is_gt, w, OP.mult)
        c.p.bar()
        vec_tt(c, accs[1 - cur][:, 0:F], accs[cur][:, 0:F], tmp["e"][:, 0:F], OP.add)
        c.p.bar()
        cur = 1 - cur
    vec_tt(c, tmp["e"][:, 0:F].rearrange("p (b k) -> p b k", k=16),
           accs[cur][:, 0:F].rearrange("p (b k) -> p b k", k=16), scb, OP.mult)
    c.p.op("scalar", lambda e, o=tmp["b"][:, 0:F], i=x_in: e.sign(o, i))
    c.p.bar()
    vec_tt(c, q_out_f32, tmp["e"][:, 0:F], tmp["b"][:, 0:F], OP.mult)
    c.p.bar()


def mk_tmp(stk, nc, name, F=1024):
    t = {}
    for k in ["a", "b", "c", "d", "e"]:
        t[k] = stk.enter_context(_sb(nc, f"{name}_{k}", [128, F], F32))
    t["s"] = stk.enter_context(_sb(nc, f"{name}_s", [128, 256], F32))
    t["s2"] = stk.enter_context(_sb(nc, f"{name}_s2", [128, 256], F32))
    t["r1"] = stk.enter_context(_sb(nc, f"{name}_r1", [128, 8], F32))
    t["r2"] = stk.enter_context(_sb(nc, f"{name}_r2", [128, 8], F32))
    return t




import contextlib


def phase1(c, x_in, tmp, xnTb=None, l0=None, l2=None):
    nc, p = c.nc, c.p
    for tt in range(NT):
        p.dma(c.xtile, x_in[tt * 128:(tt + 1) * 128, :])
        p.bar()
        layernorm_tile(c, c.xn[:], c.xtile[:], tmp)
        items = []
        if xnTb is not None:
            for k in range(8):
                items.append((c.xn[:, 128 * k:128 * (k + 1)],
                              [(xnTb.tile(k)[:, tt * 128:(tt + 1) * 128], AF.Copy, 1.0)]))
        if l2 is not None:
            for k in range(8):
                items.append((c.xn[:, 128 * k:128 * (k + 1)],
                              [(l2["strip"].tile(k)[:], AF.Copy, 1.0)]))
        if l0 is not None:
            p.dma(l0["scr_xn"][tt * 128:(tt + 1) * 128, :], c.xn[:])
        transpose_round2(c, items)
        if l0 is not None:
            fp4_quant(c, c.xq[:], c.xn[:], tmp)
            qitems = [(c.xq[:, 128 * k:128 * (k + 1)],
                       [(l0["xqT"].tile(k)[:, tt * 128:(tt + 1) * 128], AF.Copy, 1.0)])
                      for k in range(8)]
            transpose_round2(c, qitems)
        if l2 is not None:
            mm(c, c.PS[0:128, 0:8],
               [(l2["strip"].tile(pi)[:], l2["gwT"].tile(pi)[:]) for pi in range(8)])
            p.bar()
            evac(c, l2["lg"][:, 0:8], c.PS[0:128, 0:8])
            p.bar()
            lg = l2["lg"]
            r1, r2 = tmp["r1"], tmp["r2"]
            vec_red(c, r1[:, 0:1], lg[:, 0:8], OP.max)
            p.bar()
            vec_ts(c, l2["msk1"][:, 0:8], lg[:, 0:8], r1[:, 0:1], OP.is_ge)
            p.bar()
            vec_stt(c, l2["l2nd"][:, 0:8], l2["msk1"][:, 0:8], -1e30, lg[:, 0:8], OP.mult, OP.add)
            p.bar()
            vec_red(c, r2[:, 0:1], l2["l2nd"][:, 0:8], OP.max)
            p.bar()
            vec_ts(c, l2["msk2"][:, 0:8], l2["l2nd"][:, 0:8], r2[:, 0:1], OP.is_ge)
            vec_tt(c, r2[:, 1:2], r2[:, 0:1], r1[:, 0:1], OP.subtract)
            p.bar()
            act(c, r2[:, 2:3], r2[:, 1:2], AF.Exp)
            p.bar()
            c.p.op("scalar", lambda e, o=r2[:, 3:4], i=r2[:, 2:3]: e.add(o, i, 1.0))
            p.bar()
            c.p.op("vector", lambda e, o=r1[:, 1:2], i=r2[:, 3:4]: e.reciprocal(o, i))
            p.bar()
            vec_ts(c, l2["msk2"][:, 0:8], l2["msk2"][:, 0:8], r2[:, 2:3], OP.mult)
            p.bar()
            vec_tt(c, l2["msk1"][:, 0:8], l2["msk1"][:, 0:8], l2["msk2"][:, 0:8], OP.add)
            p.bar()
            vec_ts(c, l2["comb"][tt][:, 0:8], l2["msk1"][:, 0:8], r1[:, 1:2], OP.mult)
            p.bar()
        p.bar()


def transpose_round2(c, items):
    for g0 in range(0, len(items), 8):
        grp = items[g0:g0 + 8]
        for i, (src, dsts) in enumerate(grp):
            a, bb = src.shape
            slot = c.PT[i // 4][0:bb, 128 * (i % 4):128 * (i % 4) + a]
            c.p.op("tensor", lambda e, s=src, o=slot, aa=a: e.transpose(o, s, c.IDENT[0:aa, 0:aa]))
        c.p.bar()
        for i, (src, dsts) in enumerate(grp):
            a, bb = src.shape
            slot = c.PT[i // 4][0:bb, 128 * (i % 4):128 * (i % 4) + a]
            for (dst, func, scale) in dsts:
                c.p.op("scalar", lambda e, o=_ap(dst), ss=slot, f=func, sc=scale: e.activation(o, ss, f, scale=sc))
        c.p.bar()


def update_phase(c, x_in, x_out, scr_s, g, dbg=None):
    p = c.p
    for tt in range(NT):
        sl = slice(tt * 128, (tt + 1) * 128)
        p.dma(c.xtile, x_in[sl, :])
        p.dma(c.xn, scr_s[sl, :])
        p.bar()
        if dbg is not None:
            p.dma(dbg["o_s"][sl, :], c.xn[:])
            evac(c, c.xq[:], g.tile(tt)[:])
            p.bar()
            p.dma(dbg["o_g"][sl, :], c.xq[:])
            p.bar()
        vec_tt(c, c.xq[:], c.xn[:], g.tile(tt)[:], OP.mult)
        p.bar()
        vec_tt(c, c.xn[:], c.xtile[:], c.xq[:], OP.add)
        p.bar()
        p.dma(x_out[sl, :], c.xn[:])
        p.bar()


def cast_load(c, stage, dst_tall, src_dram, rows, cols, scale=1.0, col0=0, row0=0):
    n = (rows + 127) // 128
    for i in range(n):
        r = min(128, rows - 128 * i)
        c.p.dma(stage[0:r, 0:cols], src_dram[row0 + 128 * i:row0 + 128 * i + r, col0:col0 + cols])
        c.p.bar()
        evac(c, dst_tall.tile(i)[0:r, 0:cols], stage[0:r, 0:cols], scale=scale)
        c.p.bar()


def build_l0(dbg=False, fx=None):
    if fx is None:
        nc = bass.Bass("TRN2", target_bir_lowering=False)
        x_in = nc.dram_tensor("xin", [T, D], F32, kind="ExternalInput").ap()
        x_out = nc.dram_tensor("xo", [T, D], F32, kind="ExternalOutput").ap()
    else:
        nc, _c, _p, x_in, x_out = fx
    dec = nc.dram_tensor("w_dec", [1, 1], F32, kind="ExternalInput").ap()
    WT = nc.dram_tensor("w_WT", [D, D], F32, kind="ExternalInput").ap()
    bb = nc.dram_tensor("w_b", [1, D], F32, kind="ExternalInput").ap()
    gam = nc.dram_tensor("w_gam", [1, D], F32, kind="ExternalInput").ap()
    bet = nc.dram_tensor("w_bet", [1, D], F32, kind="ExternalInput").ap()
    wg = nc.dram_tensor("w_wg", [2816, D], F32, kind="ExternalInput").ap()
    wu = nc.dram_tensor("w_wu", [2816, D], F32, kind="ExternalInput").ap()
    wd = nc.dram_tensor("w_wd", [D, 2816], F32, kind="ExternalInput").ap()
    scr_xn = nc.dram_tensor("l0scr_xn", [T, D], F32, kind="Internal").ap()
    scr_s = nc.dram_tensor("l0scr_s", [T, D], F32, kind="Internal").ap()
    scr_hq = nc.dram_tensor("l0scr_hq", [2816, T], BF, kind="Internal").ap()
    dbgd = None
    if dbg:
        dbgd = {"o_s": nc.dram_tensor("o_s", [T, D], F32, kind="ExternalOutput").ap(),
                "o_g": nc.dram_tensor("o_g", [T, D], F32, kind="ExternalOutput").ap()}

    with contextlib.ExitStack() as stk:
        if fx is None:
            p = Prog(nc)
            c = mk_ctx(stk, nc, p)
            sems = [stk.enter_context(nc.semaphore(f"sem{i}")) for i in range(4)]
        else:
            c, p = _c, _p
        c.xtile = stk.enter_context(_sb(nc, "xtile", [128, D], F32))
        c.xn = stk.enter_context(_sb(nc, "xnb", [128, D], F32))
        c.xq = stk.enter_context(_sb(nc, "xqb", [128, D], F32))
        c.hqtb = stk.enter_context(_sb(nc, "hqtb", [128, 512], BF))
        tmp = mk_tmp(stk, nc, "tmp", D)
        xqT = alloc_tall(stk, nc, "xqT", D, T, BF)
        g = alloc_tall(stk, nc, "gacc", T, D, BF)

        phase1(c, x_in, tmp, l0={"scr_xn": scr_xn, "xqT": xqT})

        with contextlib.ExitStack() as ph:
            Sb = alloc_tall(ph, nc, "Sb", D, T, BF)
            afull = ph.enter_context(_sb(nc, "afull", [128, 512], F32))
            states = ph.enter_context(_sb(nc, "states", [128, 8], F32))
            stage = ph.enter_context(_sb(nc, "stg", [128, 512], F32))
            stage2 = ph.enter_context(_sb(nc, "stg2", [128, 512], F32))
            dct = ph.enter_context(_sb(nc, "dct", [1, 4], F32))
            acol = ph.enter_context(_sb(nc, "acol", [128, 1], F32))
            xns = [ph.enter_context(_sb(nc, f"xns{j}", [128, D], F32)) for j in range(4)]
            p.dma(dct[0:1, 0:1], dec)
            p.op("vector", lambda e: e.memset(afull[:], 1.0))
            p.op("vector", lambda e: e.memset(states[:], 0.0))
            p.bar()
            act(c, dct[0:1, 1:2], dct[0:1, 0:1], AF.Sigmoid)
            p.bar()
            mm(c, c.PR[0:128, 0:1], [(c.ONES1[0:1, 0:128], dct[0:1, 1:2])])
            p.bar()
            evac(c, acol[:, 0:1], c.PR[0:128, 0:1])
            p.bar()
            vec_ts(c, afull[:], afull[:], acol[:, 0:1], OP.mult)
            p.bar()
            for tc in range(4):
                for j in range(4):
                    p.dma(xns[j][:], scr_xn[(4 * tc + j) * 128:(4 * tc + j + 1) * 128, :])
                p.bar()
                for ft in range(8):
                    items = [(xns[j][:, 128 * ft:128 * (ft + 1)],
                              [(stage[:, 128 * j:128 * (j + 1)], AF.Copy, 1.0)]) for j in range(4)]
                    transpose_round2(c, items)
                    p.op("vector", lambda e, o=stage2[:], d0=afull[:], d1=stage[:], ini=states[:, ft:ft + 1]:
                         e.tensor_tensor_scan(o, d0, d1, ini, OP.mult, OP.add))
                    p.bar()
                    evac(c, Sb.tile(ft)[:, tc * 512:(tc + 1) * 512], stage2[:])
                    c.p.op("vector", lambda e, o=states[:, ft:ft + 1], i=stage2[:, 511:512]: e.tensor_copy(o, i))
                    p.bar()
            WTb = alloc_tall(ph, nc, "WTb", D, D, BF)
            brow = ph.enter_context(_sb(nc, "brow", [1, D], BF))
            Grep = ph.enter_context(_sb(nc, "grep", [128, D], F32))
            Brep = ph.enter_context(_sb(nc, "brep", [128, D], F32))
            rstage = ph.enter_context(_sb(nc, "rstage", [1, D], F32))
            cast_load(c, c.xtile, WTb, WT, D, D)
            p.dma(rstage[0:1, :], bb)
            p.bar()
            evac(c, brow[0:1, :], rstage[0:1, :])
            p.bar()
            p.dma(rstage[0:1, :], gam)
            p.bar()
            bcast_row(c, Grep[:, 0:512], rstage[0:1, 0:512])
            bcast_row(c, Grep[:, 512:1024], rstage[0:1, 512:1024])
            p.dma(rstage[0:1, :], bet)
            p.bar()
            bcast_row(c, Brep[:, 0:512], rstage[0:1, 0:512])
            bcast_row(c, Brep[:, 512:1024], rstage[0:1, 512:1024])
            for tt in range(NT):
                for ch in range(2):
                    items = [(Sb.tile(pi)[:, tt * 128:(tt + 1) * 128],
                              WTb.tile(pi)[:, ch * 512:(ch + 1) * 512]) for pi in range(8)]
                    items.append((c.ONES1B[0:1, 0:128], brow[0:1, ch * 512:(ch + 1) * 512]))
                    mm(c, c.PM[ch][0:128, 0:512], items)
                p.bar()
                for ch in range(2):
                    evac(c, c.xtile[:, ch * 512:(ch + 1) * 512], c.PM[ch][0:128, 0:512])
                p.bar()
                layernorm_tile(c, c.xn[:], c.xtile[:], tmp)
                vec_tt(c, c.xn[:], c.xn[:], Grep[:], OP.mult)
                p.bar()
                vec_tt(c, c.xn[:], c.xn[:], Brep[:], OP.add)
                p.bar()
                p.dma(scr_s[tt * 128:(tt + 1) * 128, :], c.xn[:])
                p.bar()

        NCH = [(0, 512), (512, 1024), (1024, 1536), (1536, 2048), (2048, 2560), (2560, 2816)]
        with contextlib.ExitStack() as ph:
            wgqT = alloc_tall(ph, nc, "wgqT", D, 2816, BF)
            wuqT = alloc_tall(ph, nc, "wuqT", D, 2816, BF)
            hqf = ph.enter_context(_sb(nc, "hqf", [128, 512], F32))
            tmq = mk_tmp(ph, nc, "tmq", 512)
            for (wdram, wdst) in [(wg, wgqT), (wu, wuqT)]:
                for i in range(22):
                    p.dma(c.xtile, wdram[i * 128:(i + 1) * 128, :])
                    p.bar()
                    fp4_quant(c, c.xq[:], c.xtile[:], tmp)
                    items = [(c.xq[:, 128 * k:128 * (k + 1)],
                              [(wdst.tile(k)[:, 128 * i:128 * (i + 1)], AF.Copy, 1.0)]) for k in range(8)]
                    transpose_round2(c, items)
            for tt in range(NT):
                for (n0, n1) in NCH:
                    w = n1 - n0
                    mm(c, c.PM[0][0:128, 0:w],
                       [(xqT.tile(pi)[:, tt * 128:(tt + 1) * 128], wgqT.tile(pi)[:, n0:n1]) for pi in range(8)])
                    mm(c, c.PM[1][0:128, 0:w],
                       [(xqT.tile(pi)[:, tt * 128:(tt + 1) * 128], wuqT.tile(pi)[:, n0:n1]) for pi in range(8)])
                    p.bar()
                    act(c, tmq["a"][:, 0:w], c.PM[0][0:128, 0:w], AF.Silu)
                    p.bar()
                    vec_tt(c, c.xq[:, 0:w], tmq["a"][:, 0:w], c.PM[1][0:128, 0:w], OP.mult)
                    p.bar()
                    fp4_quant(c, hqf[:, 0:w], c.xq[:, 0:w], tmq)
                    nb = w // 128
                    items = [(hqf[:, 128 * k:128 * (k + 1)],
                              [(c.hqtb[:, 128 * k:128 * (k + 1)], AF.Copy, 1.0)]) for k in range(nb)]
                    transpose_round2(c, items)
                    for k in range(nb):
                        p.dma(scr_hq[n0 + 128 * k:n0 + 128 * (k + 1), tt * 128:(tt + 1) * 128],
                              c.hqtb[:, 128 * k:128 * (k + 1)])
                    p.bar()
        with contextlib.ExitStack() as ph:
            wdqT = alloc_tall(ph, nc, "wdqT", 2816, D, BF)
            hcol = alloc_tall(ph, nc, "hcol", 2816, 128, BF)
            qstage = ph.enter_context(_sb(nc, "qstage", [128, 2816], F32))
            hqf = ph.enter_context(_sb(nc, "hqf2", [128, 512], F32))
            tmq = mk_tmp(ph, nc, "tmq2", 512)
            for i in range(8):
                p.dma(qstage[:], wd[i * 128:(i + 1) * 128, :])
                p.bar()
                for (n0, n1) in NCH:
                    w = n1 - n0
                    fp4_quant(c, hqf[:, 0:w], qstage[:, n0:n1], tmq)
                    nb = w // 128
                    items = [(hqf[:, 128 * k:128 * (k + 1)],
                              [(wdqT.tile((n0 + 128 * k) // 128)[:, 128 * i:128 * (i + 1)], AF.Copy, 1.0)])
                             for k in range(nb)]
                    transpose_round2(c, items)
            for tt in range(NT):
                for pi in range(22):
                    p.dma(hcol.tile(pi)[:], scr_hq[pi * 128:(pi + 1) * 128, tt * 128:(tt + 1) * 128])
                p.bar()
                for ch in range(2):
                    mm(c, c.PM[ch][0:128, 0:512],
                       [(hcol.tile(pi)[:], wdqT.tile(pi)[:, ch * 512:(ch + 1) * 512]) for pi in range(22)])
                p.bar()
                for ch in range(2):
                    evac(c, g.tile(tt)[:, ch * 512:(ch + 1) * 512], c.PM[ch][0:128, 0:512], func=AF.Silu)
                p.bar()

        update_phase(c, x_in, x_out, scr_s, g, dbg=dbgd)
        if fx is None:
            p.finalize(sems)
    return nc


def prep_l0(inputs):
    d = {
        "w_dec": np.asarray(inputs["L0_attn_decay"], np.float32).reshape(1, 1),
        "w_WT": np.ascontiguousarray(np.asarray(inputs["L0_attn_W"], np.float32).T),
        "w_b": np.asarray(inputs["L0_attn_b"], np.float32).reshape(1, D),
        "w_gam": np.asarray(inputs["L0_attn_gamma"], np.float32).reshape(1, D),
        "w_bet": np.asarray(inputs["L0_attn_beta"], np.float32).reshape(1, D),
        "w_wg": np.asarray(inputs["L0_ffn_w_gate"], np.float32),
        "w_wu": np.asarray(inputs["L0_ffn_w_up"], np.float32),
        "w_wd": np.asarray(inputs["L0_ffn_w_down"], np.float32),
    }
    return d


def build_l1(parts=("z", "conv", "proj", "ffn", "upd"), fx=None):
    if fx is None:
        nc = bass.Bass("TRN2", target_bir_lowering=False)
        x_in = nc.dram_tensor("xin", [T, D], F32, kind="ExternalInput").ap()
        x_out = nc.dram_tensor("xo", [T, D], F32, kind="ExternalOutput").ap()
    else:
        nc, _c, _p, x_in, x_out = fx
    w_inT = nc.dram_tensor("w_inT", [D, 2048], F32, kind="ExternalInput").ap()
    b_in = nc.dram_tensor("w_binr", [1, 2048], F32, kind="ExternalInput").ap()
    waT = nc.dram_tensor("w_waT", [256, D], F32, kind="ExternalInput").ap()
    wbT = nc.dram_tensor("w_wbT", [256, D], F32, kind="ExternalInput").ap()
    bab = nc.dram_tensor("w_bab", [1, 2048], F32, kind="ExternalInput").ap()
    w_outT = nc.dram_tensor("w_outT", [2048, D], F32, kind="ExternalInput").ap()
    b_out = nc.dram_tensor("w_bout", [1, D], F32, kind="ExternalInput").ap()
    wgT = nc.dram_tensor("w_wgT", [D, 3072], I32, kind="ExternalInput").ap()
    wuT = nc.dram_tensor("w_wuT", [D, 3072], I32, kind="ExternalInput").ap()
    wdT = nc.dram_tensor("w_wdT", [3072, D], I32, kind="ExternalInput").ap()
    C512 = nc.dram_tensor("c_C512", [512, 512], BF, kind="ExternalInput").ap()
    S512 = nc.dram_tensor("c_S512", [512, 512], BF, kind="ExternalInput").ap()
    CRD = nc.dram_tensor("c_CR", [256, 512], BF, kind="ExternalInput").ap()
    SRD = nc.dram_tensor("c_SR", [256, 512], BF, kind="ExternalInput").ap()
    ICD = nc.dram_tensor("c_IC", [512, 256], BF, kind="ExternalInput").ap()
    ISD = nc.dram_tensor("c_IS", [512, 256], BF, kind="ExternalInput").ap()
    scr_z = nc.dram_tensor("l1scr_z", [T, 2048], BF, kind="Internal").ap()
    scr_cat = nc.dram_tensor("l1scr_cat", [2048, T], BF, kind="Internal").ap()
    scr_s = nc.dram_tensor("l1scr_s", [T, D], F32, kind="Internal").ap()

    with contextlib.ExitStack() as stk:
        if fx is None:
            p = Prog(nc)
            c = mk_ctx(stk, nc, p)
            sems = [stk.enter_context(nc.semaphore(f"sem{i}")) for i in range(4)]
        else:
            c, p = _c, _p
        c.xtile = stk.enter_context(_sb(nc, "xtile", [128, D], F32))
        c.xn = stk.enter_context(_sb(nc, "xnb", [128, D], F32))
        c.xq = stk.enter_context(_sb(nc, "xqb", [128, D], F32))
        tmp = mk_tmp(stk, nc, "tmp", D)
        xnTb = alloc_tall(stk, nc, "xnTb", D, T, BF)

        phase1(c, x_in, tmp, xnTb=xnTb)

        if "z" in parts:
         with contextlib.ExitStack() as ph:
            w_inTb = alloc_tall(ph, nc, "winTb", D, 2048, BF)
            binb = ph.enter_context(_sb(nc, "binb", [1, 2048], BF))
            rstage = ph.enter_context(_sb(nc, "rstg", [1, 2048], F32))
            zbuf = ph.enter_context(_sb(nc, "zbuf", [128, 512], BF))
            for i in range(8):
                p.dma(c.xtile[:, 0:D], w_inT[i * 128:(i + 1) * 128, 0:1024])
                p.bar()
                evac(c, w_inTb.tile(i)[:, 0:1024], c.xtile[:, 0:D])
                p.bar()
                p.dma(c.xtile[:, 0:D], w_inT[i * 128:(i + 1) * 128, 1024:2048])
                p.bar()
                evac(c, w_inTb.tile(i)[:, 1024:2048], c.xtile[:, 0:D])
                p.bar()
            p.dma(rstage[0:1, :], b_in)
            p.bar()
            evac(c, binb[0:1, :], rstage[0:1, :])
            p.bar()
            for tt in range(NT):
                for ec in range(4):
                    items = [(xnTb.tile(pi)[:, tt * 128:(tt + 1) * 128],
                              w_inTb.tile(pi)[:, ec * 512:(ec + 1) * 512]) for pi in range(8)]
                    items.append((c.ONES1B[0:1, 0:128], binb[0:1, ec * 512:(ec + 1) * 512]))
                    mm(c, c.PM[0][0:128, 0:512], items)
                    p.bar()
                    evac(c, zbuf[:], c.PM[0][0:128, 0:512])
                    p.bar()
                    p.dma(scr_z[tt * 128:(tt + 1) * 128, ec * 512:(ec + 1) * 512], zbuf[:])
                    p.bar()

        if "conv" in parts:
         with contextlib.ExitStack() as ph:
            C512b = alloc_tall(ph, nc, "C512b", 512, 512, BF)
            S512b = alloc_tall(ph, nc, "S512b", 512, 512, BF)
            CRb = alloc_tall(ph, nc, "CRb", 256, 512, BF)
            SRb = alloc_tall(ph, nc, "SRb", 256, 512, BF)
            ICb = alloc_tall(ph, nc, "ICb", 512, 256, BF)
            ISb = alloc_tall(ph, nc, "ISb", 512, 256, BF)
            for tl, dr in [(C512b, C512), (S512b, S512)]:
                for i in range(4):
                    p.dma(tl.tile(i)[:], dr[i * 128:(i + 1) * 128, :])
            for tl, dr in [(CRb, CRD), (SRb, SRD)]:
                for i in range(2):
                    p.dma(tl.tile(i)[:], dr[i * 128:(i + 1) * 128, :])
            for tl, dr in [(ICb, ICD), (ISb, ISD)]:
                for i in range(4):
                    p.dma(tl.tile(i)[:], dr[i * 128:(i + 1) * 128, :])
            p.bar()
            Wc = alloc_tall(ph, nc, "Wcb", 512, 2048, BF)
            Ws = alloc_tall(ph, nc, "Wsb", 512, 2048, BF)
            watb = alloc_tall(ph, nc, "watb", 256, D, BF)
            wbtb = alloc_tall(ph, nc, "wbtb", 256, D, BF)
            for tl, dr in [(watb, waT), (wbtb, wbT)]:
                for i in range(2):
                    p.dma(c.xtile, dr[i * 128:(i + 1) * 128, :])
                    p.bar()
                    evac(c, tl.tile(i)[:], c.xtile[:])
                    p.bar()
            for (dst, lt) in [(Wc, CRb), (Ws, SRb)]:
                for po in range(4):
                    for m, wsb in [(0, watb), (1, wbtb)]:
                        for cch in range(2):
                            items = [(lt.tile(pi)[:, po * 128:(po + 1) * 128],
                                      wsb.tile(pi)[:, cch * 512:(cch + 1) * 512]) for pi in range(2)]
                            mm(c, c.PM[0][0:128, 0:512], items)
                            p.bar()
                            evac(c, dst.tile(po)[:, m * 1024 + cch * 512:m * 1024 + (cch + 1) * 512],
                                 c.PM[0][0:128, 0:512])
                            p.bar()
            babb = ph.enter_context(_sb(nc, "babb", [1, 2048], BF))
            rstage = ph.enter_context(_sb(nc, "rstg2", [1, 2048], F32))
            p.dma(rstage[0:1, :], bab)
            p.bar()
            evac(c, babb[0:1, :], rstage[0:1, :])
            p.bar()
            win = [ph.enter_context(_sb(nc, f"win{j}", [128, 2048], BF)) for j in range(4)]
            zpad = ph.enter_context(_sb(nc, "zpad", [128, 2048], BF))
            p.op("vector", lambda e: e.memset(zpad[:], 0.0))
            p.bar()
            a_bf = alloc_tall(ph, nc, "abf", 512, 512, BF)
            b_bf = alloc_tall(ph, nc, "bbf", 512, 512, BF)
            conv_all = alloc_tall(ph, nc, "conv", 256, 2048, F32)
            gab = ph.enter_context(_sb(nc, "gab", [128, 1024], F32))
            gbb = ph.enter_context(_sb(nc, "gbb", [128, 1024], F32))
            sgb = ph.enter_context(_sb(nc, "sgb", [128, 1024], F32))
            ctst = ph.enter_context(_sb(nc, "ctst", [128, 1024], BF))
            VSP = [c.PT[0], c.PT[1], c.PR, c.PS]
            for tau in range(8):
                for si in range(4):
                    zt = 2 * tau - 2 + si
                    if zt < 0:
                        pass
                    else:
                        p.dma(win[si][:], scr_z[zt * 128:(zt + 1) * 128, :])
                p.bar()
                wins = [zpad if (2 * tau - 2 + si) < 0 else win[si] for si in range(4)]
                for cc in range(4):
                    cs = slice(cc * 512, (cc + 1) * 512)
                    for po in range(4):
                        mm(c, c.PM[po][0:128, 0:512],
                           [(C512b.tile(pi)[:, po * 128:(po + 1) * 128], wins[pi][:, cs]) for pi in range(4)])
                        mm(c, VSP[po][0:128, 0:512],
                           [(S512b.tile(pi)[:, po * 128:(po + 1) * 128], wins[pi][:, cs]) for pi in range(4)])
                    p.bar()
                    for po in range(4):
                        vec_stt(c, tmp["a"][:, 0:512], c.PM[po][0:128, 0:512], 1.0, Wc.tile(po)[:, cs], OP.mult, OP.mult)
                        p.bar()
                        vec_stt(c, tmp["b"][:, 0:512], VSP[po][0:128, 0:512], 1.0, Ws.tile(po)[:, cs], OP.mult, OP.mult)
                        p.bar()
                        vec_tt(c, a_bf.tile(po)[:], tmp["a"][:, 0:512], tmp["b"][:, 0:512], OP.subtract)
                        vec_stt(c, tmp["c"][:, 0:512], c.PM[po][0:128, 0:512], 1.0, Ws.tile(po)[:, cs], OP.mult, OP.mult)
                        p.bar()
                        vec_stt(c, tmp["d"][:, 0:512], VSP[po][0:128, 0:512], 1.0, Wc.tile(po)[:, cs], OP.mult, OP.mult)
                        p.bar()
                        vec_tt(c, b_bf.tile(po)[:], tmp["c"][:, 0:512], tmp["d"][:, 0:512], OP.add)
                        p.bar()
                    for pp in range(2):
                        items = [(ICb.tile(pi)[:, pp * 128:(pp + 1) * 128], a_bf.tile(pi)[:]) for pi in range(4)]
                        items += [(ISb.tile(pi)[:, pp * 128:(pp + 1) * 128], b_bf.tile(pi)[:]) for pi in range(4)]
                        items.append((c.ONES1B[0:1, 0:128], babb[0:1, cs]))
                        mm(c, c.PM[pp][0:128, 0:512], items)
                    p.bar()
                    for pp in range(2):
                        evac(c, conv_all.tile(pp)[:, cs], c.PM[pp][0:128, 0:512])
                    p.bar()
                for pt in range(2):
                    act(c, sgb[:], conv_all.tile(pt)[:, 1024:2048], AF.Sigmoid)
                    p.bar()
                    vec_tt(c, gab[:], conv_all.tile(pt)[:, 0:1024], sgb[:], OP.mult)
                    act(c, tmp["a"][:, 0:1024], conv_all.tile(pt)[:, 0:1024], AF.Sigmoid)
                    p.bar()
                    vec_tt(c, gbb[:], conv_all.tile(pt)[:, 1024:2048], tmp["a"][:, 0:1024], OP.mult)
                    p.bar()
                    tcol = tau * 256 + pt * 128
                    items = [(gab[:, cb * 128:(cb + 1) * 128],
                              [(ctst[:, cb * 128:(cb + 1) * 128], AF.Copy, 1.0)]) for cb in range(8)]
                    transpose_round2(c, items)
                    for cb in range(8):
                        p.dma(scr_cat[cb * 128:(cb + 1) * 128, tcol:tcol + 128], ctst[:, cb * 128:(cb + 1) * 128])
                    p.bar()
                    items = [(gbb[:, cb * 128:(cb + 1) * 128],
                              [(ctst[:, cb * 128:(cb + 1) * 128], AF.Copy, 1.0)]) for cb in range(8)]
                    transpose_round2(c, items)
                    for cb in range(8):
                        p.dma(scr_cat[1024 + cb * 128:1024 + (cb + 1) * 128, tcol:tcol + 128],
                              ctst[:, cb * 128:(cb + 1) * 128])
                    p.bar()

        if "proj" in parts:
         with contextlib.ExitStack() as ph:
            w_outTb = alloc_tall(ph, nc, "woutTb", 2048, D, BF)
            boutb = ph.enter_context(_sb(nc, "boutb", [1, D], BF))
            rstage = ph.enter_context(_sb(nc, "rstg3", [1, D], F32))
            catcol = alloc_tall(ph, nc, "catcol", 2048, 128, BF)
            cast_load(c, c.xtile, w_outTb, w_outT, 2048, D)
            p.dma(rstage[0:1, :], b_out)
            p.bar()
            evac(c, boutb[0:1, :], rstage[0:1, :])
            p.bar()
            for tt in range(NT):
                for pi in range(16):
                    p.dma(catcol.tile(pi)[:], scr_cat[pi * 128:(pi + 1) * 128, tt * 128:(tt + 1) * 128])
                p.bar()
                for ch in range(2):
                    items = [(catcol.tile(pi)[:], w_outTb.tile(pi)[:, ch * 512:(ch + 1) * 512])
                             for pi in range(16)]
                    items.append((c.ONES1B[0:1, 0:128], boutb[0:1, ch * 512:(ch + 1) * 512]))
                    mm(c, c.PM[ch][0:128, 0:512], items)
                p.bar()
                for ch in range(2):
                    evac(c, c.xn[:, ch * 512:(ch + 1) * 512], c.PM[ch][0:128, 0:512])
                p.bar()
                p.dma(scr_s[tt * 128:(tt + 1) * 128, :], c.xn[:])
                p.bar()

        g = alloc_tall(stk, nc, "gacc", T, D, BF)
        if "ffn" in parts:
         with contextlib.ExitStack() as ph:
            istage = ph.enter_context(_sb(nc, "istage", [128, 1024], I32))
            yTh = alloc_tall(ph, nc, "yTh", D, 1024, F32)
            wgb = alloc_tall(ph, nc, "wgb", D, 512, BF)
            wub = alloc_tall(ph, nc, "wub", D, 512, BF)
            wdb = alloc_tall(ph, nc, "wdb", 512, D, BF)
            hTb = alloc_tall(ph, nc, "hTb", 512, 1024, BF)

            def i32load(dst_ap, src_ap, wcols):
                p.dma(istage[0:128, 0:wcols], src_ap)
                p.bar()
                c.p.op("vector", lambda e, o=c.xtile[:, 0:wcols], i=istage[0:128, 0:wcols]: e.tensor_copy(o, i))
                p.bar()
                evac(c, dst_ap, c.xtile[:, 0:wcols], scale=1.0 / 127.0)
                p.bar()

            for half in range(2):
                for ho in range(8):
                    p.op("vector", lambda e, o=yTh.tile(ho)[:]: e.memset(o, 0.0))
                p.bar()
                for chh in range(6):
                    for i in range(8):
                        i32load(wgb.tile(i)[:], wgT[i * 128:(i + 1) * 128, chh * 512:(chh + 1) * 512], 512)
                        i32load(wub.tile(i)[:], wuT[i * 128:(i + 1) * 128, chh * 512:(chh + 1) * 512], 512)
                    for i in range(4):
                        i32load(wdb.tile(i)[:], wdT[chh * 512 + i * 128:chh * 512 + (i + 1) * 128, :], 1024)
                    for io in range(4):
                        for tch in range(2):
                            t0 = half * 1024 + tch * 512
                            mm(c, c.PM[0][0:128, 0:512],
                               [(wgb.tile(pi)[:, io * 128:(io + 1) * 128], xnTb.tile(pi)[:, t0:t0 + 512])
                                for pi in range(8)])
                            mm(c, c.PM[1][0:128, 0:512],
                               [(wub.tile(pi)[:, io * 128:(io + 1) * 128], xnTb.tile(pi)[:, t0:t0 + 512])
                                for pi in range(8)])
                            p.bar()
                            act(c, tmp["a"][:, 0:512], c.PM[0][0:128, 0:512], AF.Silu)
                            p.bar()
                            vec_tt(c, hTb.tile(io)[:, tch * 512:(tch + 1) * 512],
                                   tmp["a"][:, 0:512], c.PM[1][0:128, 0:512], OP.mult)
                            p.bar()
                    pairs = [(ho, tch) for ho in range(8) for tch in range(2)]
                    for step in range(len(pairs) + 1):
                        if step < len(pairs):
                            ho, tch = pairs[step]
                            mm(c, c.PM[2 + step % 2][0:128, 0:512],
                               [(wdb.tile(pi)[:, ho * 128:(ho + 1) * 128],
                                 hTb.tile(pi)[:, tch * 512:(tch + 1) * 512]) for pi in range(4)])
                        if step >= 1:
                            ho, tch = pairs[step - 1]
                            sl = yTh.tile(ho)[:, tch * 512:(tch + 1) * 512]
                            vec_tt(c, sl, sl, c.PM[2 + (step - 1) % 2][0:128, 0:512], OP.add)
                        p.bar()
                for tl in range(8):
                    items = [(yTh.tile(pi)[:, tl * 128:(tl + 1) * 128],
                              [(g.tile(half * 8 + tl)[:, pi * 128:(pi + 1) * 128], AF.Silu, 1.0)])
                             for pi in range(8)]
                    transpose_round2(c, items)

        if "upd" in parts:
            update_phase(c, x_in, x_out, scr_s, g)
        else:
            for tt in range(NT):
                p.dma(c.xtile, x_in[tt * 128:(tt + 1) * 128, :])
                p.bar()
                p.dma(x_out[tt * 128:(tt + 1) * 128, :], c.xtile[:])
                p.bar()
        if fx is None:
            p.finalize(sems)
    return nc


def prep_l1(inputs):
    s_ = np.arange(512)
    f_ = np.arange(512)
    ang = 2 * np.pi * np.outer(s_, f_) / 512.0
    k_ = np.arange(256)
    angr = 2 * np.pi * np.outer(255 - k_, f_).T.astype(np.float64) / 512.0
    pp_ = np.arange(256)
    angi = 2 * np.pi * np.outer(f_, pp_ + 256) / 512.0
    d = {
        "w_inT": np.ascontiguousarray(np.asarray(inputs["L1_attn_w_in"], np.float32).T),
        "w_binr": np.asarray(inputs["L1_attn_b_in"], np.float32).reshape(1, 2048),
        "w_waT": np.ascontiguousarray(np.asarray(inputs["L1_attn_w_a"], np.float32).reshape(1024, 256).T),
        "w_wbT": np.ascontiguousarray(np.asarray(inputs["L1_attn_w_b"], np.float32).reshape(1024, 256).T),
        "w_bab": np.concatenate([np.asarray(inputs["L1_attn_b_a"], np.float32),
                                 np.asarray(inputs["L1_attn_b_b"], np.float32)]).reshape(1, 2048),
        "w_outT": np.ascontiguousarray(np.asarray(inputs["L1_attn_w_out"], np.float32).T),
        "w_bout": np.asarray(inputs["L1_attn_b_out"], np.float32).reshape(1, D),
        "w_wgT": np.ascontiguousarray(np.asarray(inputs["L1_ffn_w_gate"], np.int32).T),
        "w_wuT": np.ascontiguousarray(np.asarray(inputs["L1_ffn_w_up"], np.int32).T),
        "w_wdT": np.ascontiguousarray(np.asarray(inputs["L1_ffn_w_down"], np.int32).T),
        "c_C512": np.cos(ang).astype(BFNP),
        "c_S512": np.sin(ang).astype(BFNP),
        "c_CR": np.cos(angr.T).astype(BFNP).reshape(256, 512) if False else np.cos(2 * np.pi * np.outer(255 - k_, f_) / 512.0).astype(BFNP),
        "c_SR": np.sin(2 * np.pi * np.outer(255 - k_, f_) / 512.0).astype(BFNP),
        "c_IC": (np.cos(angi) / 512.0).astype(BFNP),
        "c_IS": (np.sin(angi) / 512.0).astype(BFNP),
    }
    return d


def attn_core(c, xnTb, ctxT, h, qt_src, kt_src, v_src, bias_fn, p):
    SCP = [c.PM[2], c.PT[0]]
    ATTs = [c.ATT, c.ATT2]
    for qc in range(4):
        qs = slice(qc * 512, (qc + 1) * 512)
        valid = [kt for kt in range(16) if bias_fn(qc, kt, None, None, probe=True)]
        n = len(valid)
        for step in range(n + 2):
            if step < n:
                kt = valid[step]
                mm(c, SCP[step % 2][0:128, 0:512],
                   [(kt_src[:, kt * 128:(kt + 1) * 128], qt_src[:, qs])])
            if 1 <= step <= n:
                j = step - 1
                bias_fn(qc, valid[j], SCP[j % 2][0:128, 0:512], ATTs[j % 2])
            if step >= 2:
                j = step - 2
                kt = valid[j]
                mm(c, c.PS[0:1, 0:512], [(c.ONESCB[0:128, 0:1], ATTs[j % 2][:])],
                   start=(j == 0), stop=(j == n - 1))
                mm(c, c.PM[3][0:64, 0:512], [(v_src.tile(kt)[:], ATTs[j % 2][:])],
                   start=(j == 0), stop=(j == n - 1))
            p.bar()
        evac(c, c.csr[0:1, 0:512], c.PS[0:1, 0:512])
        p.bar()
        c.p.op("vector", lambda e, o=c.csi[0:1, 0:512], i=c.csr[0:1, 0:512]: e.reciprocal(o, i))
        p.bar()
        mm(c, c.PR[0:64, 0:512], [(c.ONES1[0:1, 0:64], c.csi[0:1, 0:512])])
        p.bar()
        evac(c, c.rrep[0:64, 0:512], c.PR[0:64, 0:512])
        p.bar()
        vec_tt(c, ctxT.tile(h // 2)[64 * (h % 2):64 * (h % 2) + 64, qs],
               c.PM[3][0:64, 0:512], c.rrep[0:64, 0:512], OP.mult)
        p.bar()


def load_head_w(c, p, dst8, wdram, col0, stage):
    for i in range(8):
        p.dma(stage[0:128, 0:64], wdram[i * 128:(i + 1) * 128, col0:col0 + 64])
        p.bar()
        evac(c, dst8[i][0:128, 0:64], stage[0:128, 0:64])
        p.bar()


def qkv_head(c, p, xnTb, wb8, bcol, scale, QT):
    for qc in range(4):
        mm(c, c.PM[0][0:64, 0:512],
           [(wb8[pi][0:128, 0:64], xnTb.tile(pi)[:, qc * 512:(qc + 1) * 512]) for pi in range(8)])
        p.bar()
        vec_ts(c, QT[0:64, qc * 512:(qc + 1) * 512], c.PM[0][0:64, 0:512], bcol, OP.add, scale, OP.mult)
        p.bar()


def vproj_head(c, p, xnTb, wb8, bvrow_bf, Vb):
    for tt in range(NT):
        items = [(xnTb.tile(pi)[:, tt * 128:(tt + 1) * 128], wb8[pi][0:128, 0:64]) for pi in range(8)]
        items.append((c.ONES1B[0:1, 0:128], bvrow_bf[0:1, 0:64]))
        mm(c, c.PM[0][0:128, 0:64], items)
        p.bar()
        evac(c, Vb.tile(tt)[:, 0:64], c.PM[0][0:128, 0:64])
        p.bar()


def out_proj(c, p, ctxT, wob, borow_bf, scr_s):
    for tt in range(NT):
        for ch in range(2):
            items = [(ctxT.tile(pi)[:, tt * 128:(tt + 1) * 128], wob.tile(pi)[:, ch * 512:(ch + 1) * 512])
                     for pi in range(8)]
            items.append((c.ONES1B[0:1, 0:128], borow_bf[0:1, ch * 512:(ch + 1) * 512]))
            mm(c, c.PM[ch][0:128, 0:512], items)
        p.bar()
        for ch in range(2):
            evac(c, c.xn[:, ch * 512:(ch + 1) * 512], c.PM[ch][0:128, 0:512])
        p.bar()
        p.dma(scr_s[tt * 128:(tt + 1) * 128, :], c.xn[:])
        p.bar()


def build_l2(fx=None):
    if fx is None:
        nc = bass.Bass("TRN2", target_bir_lowering=False)
        x_in = nc.dram_tensor("xin", [T, D], F32, kind="ExternalInput").ap()
        x_out = nc.dram_tensor("xo", [T, D], F32, kind="ExternalOutput").ap()
    else:
        nc, _c, _p, x_in, x_out = fx
    Wq = nc.dram_tensor("w_Wq", [D, D], F32, kind="ExternalInput").ap()
    Wk = nc.dram_tensor("w_Wk", [D, D], F32, kind="ExternalInput").ap()
    Wv = nc.dram_tensor("w_Wv", [D, D], F32, kind="ExternalInput").ap()
    Wo = nc.dram_tensor("w_Wo", [D, D], F32, kind="ExternalInput").ap()
    bq = nc.dram_tensor("w_bq", [D, 1], F32, kind="ExternalInput").ap()
    bk = nc.dram_tensor("w_bk", [D, 1], F32, kind="ExternalInput").ap()
    bv = nc.dram_tensor("w_bv", [1, D], F32, kind="ExternalInput").ap()
    bo = nc.dram_tensor("w_bo", [1, D], F32, kind="ExternalInput").ap()
    relb = nc.dram_tensor("w_relb", [32, 16], F32, kind="ExternalInput").ap()
    crawk = nc.dram_tensor("w_crawk", [8, 16, 128, 512], F32, kind="ExternalInput").ap()
    gwT = nc.dram_tensor("w_gwT", [D, 8], F32, kind="ExternalInput").ap()
    wegT = nc.dram_tensor("w_wegT", [8, D, D], F32, kind="ExternalInput").ap()
    weuT = nc.dram_tensor("w_weuT", [8, D, D], F32, kind="ExternalInput").ap()
    wedT = nc.dram_tensor("w_wedT", [8, D, D], F32, kind="ExternalInput").ap()
    wsgT = nc.dram_tensor("w_wsgT", [D, 2048], F32, kind="ExternalInput").ap()
    wsuT = nc.dram_tensor("w_wsuT", [D, 2048], F32, kind="ExternalInput").ap()
    wsdT = nc.dram_tensor("w_wsdT", [2048, D], F32, kind="ExternalInput").ap()
    scr_s = nc.dram_tensor("l2scr_s", [T, D], F32, kind="Internal").ap()

    with contextlib.ExitStack() as stk:
        if fx is None:
            p = Prog(nc)
            c = mk_ctx(stk, nc, p)
            sems = [stk.enter_context(nc.semaphore(f"sem{i}")) for i in range(4)]
        else:
            c, p = _c, _p
        c.xtile = stk.enter_context(_sb(nc, "xtile", [128, D], F32))
        c.xn = stk.enter_context(_sb(nc, "xnb", [128, D], F32))
        c.xq = stk.enter_context(_sb(nc, "xqb", [128, D], F32))
        tmp = mk_tmp(stk, nc, "tmp", D)
        xnTb = alloc_tall(stk, nc, "xnTb", D, T, BF)
        g = alloc_tall(stk, nc, "gacc", T, D, BF)
        combs = [stk.enter_context(_sb(nc, f"comb{i}", [128, 8], F32)) for i in range(NT)]

        with contextlib.ExitStack() as ph:
            strip = alloc_tall(ph, nc, "strip", D, 128, F32)
            gwTb = alloc_tall(ph, nc, "gwTb", D, 8, F32)
            l2d = {"strip": strip, "gwT": gwTb,
                   "lg": ph.enter_context(_sb(nc, "lg", [128, 8], F32)),
                   "msk1": ph.enter_context(_sb(nc, "msk1", [128, 8], F32)),
                   "msk2": ph.enter_context(_sb(nc, "msk2", [128, 8], F32)),
                   "l2nd": ph.enter_context(_sb(nc, "l2nd", [128, 8], F32)),
                   "comb": combs}
            for i in range(8):
                p.dma(gwTb.tile(i)[:], gwT[i * 128:(i + 1) * 128, :])
            p.bar()
            phase1(c, x_in, tmp, xnTb=xnTb, l2=l2d)

        with contextlib.ExitStack() as ph:
            ctxT = alloc_tall(ph, nc, "ctxT", D, T, BF)
            QT = ph.enter_context(_sb(nc, "QT", [64, T], BF))
            KT = ph.enter_context(_sb(nc, "KT", [64, T], BF))
            Vb = alloc_tall(ph, nc, "Vb", T, 64, BF)
            wq8 = [ph.enter_context(_sb(nc, f"wq8_{i}", [128, 64], BF)) for i in range(8)]
            wk8 = [ph.enter_context(_sb(nc, f"wk8_{i}", [128, 64], BF)) for i in range(8)]
            wv8 = [ph.enter_context(_sb(nc, f"wv8_{i}", [128, 64], BF)) for i in range(8)]
            wst = ph.enter_context(_sb(nc, "wst", [128, 64], F32))
            bcolq = ph.enter_context(_sb(nc, "bcolq", [64, 2], F32))
            bcolk = ph.enter_context(_sb(nc, "bcolk", [64, 2], F32))
            bvrow = ph.enter_context(_sb(nc, "bvrow", [1, 64], F32))
            bvrow_bf = ph.enter_context(_sb(nc, "bvrowb", [1, 64], BF))
            rb0 = ph.enter_context(_sb(nc, "rb0", [128, 16], F32))
            rb31 = ph.enter_context(_sb(nc, "rb31", [128, 16], F32))
            rstage = ph.enter_context(_sb(nc, "rstg", [1, 16], F32))
            crb8 = alloc_tall(ph, nc, "crb8", 1024, 512, F32)
            c.ATT = ph.enter_context(_sb(nc, "ATT", [128, 512], BF))
            c.ATT2 = ph.enter_context(_sb(nc, "ATT2", [128, 512], BF))
            c.csr = ph.enter_context(_sb(nc, "csr", [2, 512], F32))
            c.csi = ph.enter_context(_sb(nc, "csi", [2, 512], F32))
            c.rrep = ph.enter_context(_sb(nc, "rrep", [64, 512], F32))
            scs = ph.enter_context(_sb(nc, "scs", [128, 512], F32))
            p.dma(rstage[0:1, :], relb[0:1, :])
            p.bar()
            mm(c, c.PR[0:128, 0:16], [(c.ONES1[0:1, 0:128], rstage[0:1, 0:16])])
            p.bar()
            evac(c, rb0[:, 0:16], c.PR[0:128, 0:16], scale=8.0)
            p.bar()
            p.dma(rstage[0:1, :], relb[31:32, :])
            p.bar()
            mm(c, c.PR[0:128, 0:16], [(c.ONES1[0:1, 0:128], rstage[0:1, 0:16])])
            p.bar()
            evac(c, rb31[:, 0:16], c.PR[0:128, 0:16], scale=8.0)
            p.bar()
            for h in range(16):
                load_head_w(c, p, wq8, Wq, 64 * h, wst)
                load_head_w(c, p, wk8, Wk, 64 * h, wst)
                load_head_w(c, p, wv8, Wv, 64 * h, wst)
                p.dma(bcolq[0:64, 0:1], bq[64 * h:64 * h + 64, 0:1])
                p.dma(bcolk[0:64, 0:1], bk[64 * h:64 * h + 64, 0:1])
                p.dma(bvrow[0:1, 0:64], bv[0:1, 64 * h:64 * h + 64])
                p.bar()
                evac(c, bvrow_bf[0:1, 0:64], bvrow[0:1, 0:64])
                p.bar()
                qkv_head(c, p, xnTb, wq8, bcolq[0:64, 0:1], 0.125, QT)
                qkv_head(c, p, xnTb, wk8, bcolk[0:64, 0:1], 1.0, KT)
                vproj_head(c, p, xnTb, wv8, bvrow_bf, Vb)

                for cl8 in range(8):
                    p.dma(crb8.tile(cl8)[:], crawk[cl8, h, :, :])
                p.bar()

                def bias_fn(qc, kt, ps, att, probe=False, h=h):
                    if probe:
                        return True
                    cls = 4 * qc - kt
                    if cls <= -4:
                        act(c, att[:], ps, AF.Exp, bias=rb0[:, h:h + 1])
                    elif cls >= 5:
                        act(c, att[:], ps, AF.Exp, bias=rb31[:, h:h + 1])
                    else:
                        vec_stt(c, scs[:], crb8.tile(cls + 3)[:], 8.0, ps, OP.mult, OP.add)
                        p.bar()
                        act(c, att[:], scs[:], AF.Exp)
                    return True
                attn_core(c, xnTb, ctxT, h, QT, KT, Vb, bias_fn, p)
            wob = alloc_tall(ph, nc, "wob", D, D, BF)
            borow = ph.enter_context(_sb(nc, "borow", [1, D], F32))
            borow_bf = ph.enter_context(_sb(nc, "borowb", [1, D], BF))
            cast_load(c, c.xtile, wob, Wo, D, D)
            p.dma(borow[0:1, :], bo)
            p.bar()
            evac(c, borow_bf[0:1, :], borow[0:1, :])
            p.bar()
            out_proj(c, p, ctxT, wob, borow_bf, scr_s)

        with contextlib.ExitStack() as ph:
            yTh = alloc_tall(ph, nc, "yTh", D, 1024, F32)
            wegb = alloc_tall(ph, nc, "wegb", D, D, BF)
            weub = alloc_tall(ph, nc, "weub", D, D, BF)
            wedb = alloc_tall(ph, nc, "wedb", D, D, BF)
            hsT = alloc_tall(ph, nc, "hsT", D, 1024, BF)
            combrow = ph.enter_context(_sb(nc, "combrow", [1, T], F32))
            crep = ph.enter_context(_sb(nc, "crep", [128, 512], F32))
            for half in range(2):
                for ho in range(8):
                    p.op("vector", lambda e, o=yTh.tile(ho)[:]: e.memset(o, 0.0))
                p.bar()
                for e_ in range(8):
                    cast_load(c, c.xtile, wegb, wegT[e_], D, D)
                    cast_load(c, c.xtile, weub, weuT[e_], D, D)
                    items = [(combs[tt][:, e_:e_ + 1],
                              [(combrow[0:1, tt * 128:(tt + 1) * 128], AF.Copy, 1.0)]) for tt in range(NT)]
                    transpose_round2(c, items)
                    for tch in range(2):
                        t0 = half * 1024 + tch * 512
                        mm(c, c.PR[0:128, 0:512], [(c.ONES1[0:1, 0:128], combrow[0:1, t0:t0 + 512])])
                        p.bar()
                        evac(c, crep[:], c.PR[0:128, 0:512])
                        p.bar()
                        PG = [c.PM[0], c.PM[2]]
                        PU = [c.PM[1], c.PM[3]]
                        sg2 = [tmp["a"], tmp["c"]]
                        uc2 = [tmp["b"], tmp["d"]]
                        for step in range(10):
                            if step < 8:
                                mm(c, PG[step % 2][0:128, 0:512],
                                   [(wegb.tile(pi)[:, step * 128:(step + 1) * 128], xnTb.tile(pi)[:, t0:t0 + 512])
                                    for pi in range(8)])
                                mm(c, PU[step % 2][0:128, 0:512],
                                   [(weub.tile(pi)[:, step * 128:(step + 1) * 128], xnTb.tile(pi)[:, t0:t0 + 512])
                                    for pi in range(8)])
                            if 1 <= step <= 8:
                                j = step - 1
                                act(c, sg2[j % 2][:, 0:512], PG[j % 2][0:128, 0:512], AF.Silu)
                                vec_stt(c, uc2[j % 2][:, 0:512], PU[j % 2][0:128, 0:512], 1.0, crep[:],
                                        OP.mult, OP.mult)
                            if step >= 2:
                                j = step - 2
                                vec_tt(c, hsT.tile(j)[:, tch * 512:(tch + 1) * 512],
                                       sg2[j % 2][:, 0:512], uc2[j % 2][:, 0:512], OP.mult)
                            p.bar()
                    cast_load(c, c.xtile, wedb, wedT[e_], D, D)
                    pairs = [(ho, tch) for ho in range(8) for tch in range(2)]
                    for step in range(len(pairs) + 1):
                        if step < len(pairs):
                            ho, tch = pairs[step]
                            mm(c, c.PM[2 + step % 2][0:128, 0:512],
                               [(wedb.tile(pi)[:, ho * 128:(ho + 1) * 128],
                                 hsT.tile(pi)[:, tch * 512:(tch + 1) * 512]) for pi in range(8)])
                        if step >= 1:
                            ho, tch = pairs[step - 1]
                            sl = yTh.tile(ho)[:, tch * 512:(tch + 1) * 512]
                            vec_tt(c, sl, sl, c.PM[2 + (step - 1) % 2][0:128, 0:512], OP.add)
                        p.bar()
                for chh in range(4):
                    i0 = chh * 512
                    cast_load(c, c.xtile, wegb, wsgT, D, 512, col0=i0)
                    cast_load(c, c.xtile, weub, wsuT, D, 512, col0=i0)
                    for i in range(4):
                        p.dma(c.xtile, wsdT[i0 + i * 128:i0 + (i + 1) * 128, :])
                        p.bar()
                        evac(c, wedb.tile(i)[:], c.xtile[:])
                        p.bar()
                    for io in range(4):
                        for tch in range(2):
                            t0 = half * 1024 + tch * 512
                            mm(c, c.PM[0][0:128, 0:512],
                               [(wegb.tile(pi)[:, io * 128:(io + 1) * 128], xnTb.tile(pi)[:, t0:t0 + 512])
                                for pi in range(8)])
                            mm(c, c.PM[1][0:128, 0:512],
                               [(weub.tile(pi)[:, io * 128:(io + 1) * 128], xnTb.tile(pi)[:, t0:t0 + 512])
                                for pi in range(8)])
                            p.bar()
                            act(c, tmp["a"][:, 0:512], c.PM[0][0:128, 0:512], AF.Silu)
                            p.bar()
                            vec_tt(c, hsT.tile(io)[:, tch * 512:(tch + 1) * 512],
                                   tmp["a"][:, 0:512], c.PM[1][0:128, 0:512], OP.mult)
                            p.bar()
                    pairs = [(ho, tch) for ho in range(8) for tch in range(2)]
                    for step in range(len(pairs) + 1):
                        if step < len(pairs):
                            ho, tch = pairs[step]
                            mm(c, c.PM[2 + step % 2][0:128, 0:512],
                               [(wedb.tile(pi)[:, ho * 128:(ho + 1) * 128],
                                 hsT.tile(pi)[:, tch * 512:(tch + 1) * 512]) for pi in range(4)])
                        if step >= 1:
                            ho, tch = pairs[step - 1]
                            sl = yTh.tile(ho)[:, tch * 512:(tch + 1) * 512]
                            vec_tt(c, sl, sl, c.PM[2 + (step - 1) % 2][0:128, 0:512], OP.add)
                        p.bar()
                for tl in range(8):
                    items = [(yTh.tile(pi)[:, tl * 128:(tl + 1) * 128],
                              [(g.tile(half * 8 + tl)[:, pi * 128:(pi + 1) * 128], AF.Silu, 1.0)])
                             for pi in range(8)]
                    transpose_round2(c, items)

        update_phase(c, x_in, x_out, scr_s, g)
        if fx is None:
            p.finalize(sems)
    return nc


def _bucket_np(d):
    rp = np.maximum(d, 0)
    is_small = rp < 16
    rp_f = np.maximum(rp, 1).astype(np.float32)
    large = 16 + (np.log(rp_f / np.float32(16.0)) / np.log(64.0 / 16.0) * 16).astype(np.int32)
    large = np.minimum(large, 31)
    return np.where(is_small, rp, large)


def prep_l2(inputs):
    relb = np.asarray(inputs["L2_attn_rel_bias"], np.float32)
    cls_ = np.arange(-3, 5)
    p_ = np.arange(128)
    f_ = np.arange(512)
    dmat = 128 * cls_[:, None, None] + f_[None, None, :] - p_[None, :, None]
    buck = _bucket_np(dmat)
    crawk = relb[buck]
    crawk = np.ascontiguousarray(np.transpose(crawk, (0, 3, 1, 2)))
    d = {
        "w_Wq": np.asarray(inputs["L2_attn_Wq"], np.float32),
        "w_Wk": np.asarray(inputs["L2_attn_Wk"], np.float32),
        "w_Wv": np.asarray(inputs["L2_attn_Wv"], np.float32),
        "w_Wo": np.asarray(inputs["L2_attn_Wo"], np.float32),
        "w_bq": np.asarray(inputs["L2_attn_bq"], np.float32).reshape(D, 1),
        "w_bk": np.asarray(inputs["L2_attn_bk"], np.float32).reshape(D, 1),
        "w_bv": np.asarray(inputs["L2_attn_bv"], np.float32).reshape(1, D),
        "w_bo": np.asarray(inputs["L2_attn_bo"], np.float32).reshape(1, D),
        "w_relb": relb,
        "w_crawk": crawk,
        "w_gwT": np.ascontiguousarray(np.asarray(inputs["L2_ffn_gate_w"], np.float32).T),
        "w_wegT": np.ascontiguousarray(np.transpose(np.asarray(inputs["L2_ffn_we_gate"], np.float32), (0, 2, 1))),
        "w_weuT": np.ascontiguousarray(np.transpose(np.asarray(inputs["L2_ffn_we_up"], np.float32), (0, 2, 1))),
        "w_wedT": np.ascontiguousarray(np.transpose(np.asarray(inputs["L2_ffn_we_down"], np.float32), (0, 2, 1))),
        "w_wsgT": np.ascontiguousarray(np.asarray(inputs["L2_ffn_ws_gate"], np.float32).T),
        "w_wsuT": np.ascontiguousarray(np.asarray(inputs["L2_ffn_ws_up"], np.float32).T),
        "w_wsdT": np.ascontiguousarray(np.asarray(inputs["L2_ffn_ws_down"], np.float32).T),
    }
    return d


def build_l3(fx=None):
    if fx is None:
        nc = bass.Bass("TRN2", target_bir_lowering=False)
        x_in = nc.dram_tensor("xin", [T, D], F32, kind="ExternalInput").ap()
        x_out = nc.dram_tensor("xo", [T, D], F32, kind="ExternalOutput").ap()
    else:
        nc, _c, _p, x_in, x_out = fx
    wqkv = nc.dram_tensor("w_qkv", [D, 3072], F32, kind="ExternalInput").ap()
    bqkv = nc.dram_tensor("w_bqkv", [3072, 1], F32, kind="ExternalInput").ap()
    bqkvr = nc.dram_tensor("w_bqkvr", [1, 3072], F32, kind="ExternalInput").ap()
    wproj = nc.dram_tensor("w_proj", [D, D], F32, kind="ExternalInput").ap()
    bproj = nc.dram_tensor("w_bproj", [1, D], F32, kind="ExternalInput").ap()
    wfr = nc.dram_tensor("w_wfr", [D, D], F32, kind="ExternalInput").ap()
    wfi = nc.dram_tensor("w_wfi", [D, D], F32, kind="ExternalInput").ap()
    fbias = nc.dram_tensor("w_fbias", [1, D], F32, kind="ExternalInput").ap()
    C1024 = nc.dram_tensor("c_C1024", [D, D], BF, kind="ExternalInput").ap()
    S1024 = nc.dram_tensor("c_S1024", [D, D], BF, kind="ExternalInput").ap()
    negm = nc.dram_tensor("c_negm", [4, 128, 512], F32, kind="ExternalInput").ap()
    scr_s = nc.dram_tensor("l3scr_s", [T, D], F32, kind="Internal").ap()

    with contextlib.ExitStack() as stk:
        if fx is None:
            p = Prog(nc)
            c = mk_ctx(stk, nc, p)
            sems = [stk.enter_context(nc.semaphore(f"sem{i}")) for i in range(4)]
        else:
            c, p = _c, _p
        c.xtile = stk.enter_context(_sb(nc, "xtile", [128, D], F32))
        c.xn = stk.enter_context(_sb(nc, "xnb", [128, D], F32))
        c.xq = stk.enter_context(_sb(nc, "xqb", [128, D], F32))
        tmp = mk_tmp(stk, nc, "tmp", D)
        xnTb = alloc_tall(stk, nc, "xnTb", D, T, BF)
        g = alloc_tall(stk, nc, "gacc", T, D, BF)

        phase1(c, x_in, tmp, xnTb=xnTb)

        with contextlib.ExitStack() as ph:
            ctxT = alloc_tall(ph, nc, "ctxT", D, T, BF)
            QT = ph.enter_context(_sb(nc, "QT", [64, T], BF))
            KT = ph.enter_context(_sb(nc, "KT", [64, T], BF))
            Vb = alloc_tall(ph, nc, "Vb", T, 64, BF)
            wq8 = [ph.enter_context(_sb(nc, f"wq8_{i}", [128, 64], BF)) for i in range(8)]
            wk8 = [ph.enter_context(_sb(nc, f"wk8_{i}", [128, 64], BF)) for i in range(8)]
            wv8 = [ph.enter_context(_sb(nc, f"wv8_{i}", [128, 64], BF)) for i in range(8)]
            wst = ph.enter_context(_sb(nc, "wst", [128, 64], F32))
            bcolq = ph.enter_context(_sb(nc, "bcolq", [64, 2], F32))
            bcolk = ph.enter_context(_sb(nc, "bcolk", [64, 2], F32))
            bvrow = ph.enter_context(_sb(nc, "bvrow", [1, 64], F32))
            bvrow_bf = ph.enter_context(_sb(nc, "bvrowb", [1, 64], BF))
            nmt = [ph.enter_context(_sb(nc, f"nmt{i}", [128, 512], F32)) for i in range(4)]
            c.ATT = ph.enter_context(_sb(nc, "ATT", [128, 512], BF))
            c.ATT2 = ph.enter_context(_sb(nc, "ATT2", [128, 512], BF))
            c.csr = ph.enter_context(_sb(nc, "csr", [2, 512], F32))
            c.csi = ph.enter_context(_sb(nc, "csi", [2, 512], F32))
            c.rrep = ph.enter_context(_sb(nc, "rrep", [64, 512], F32))
            scs = ph.enter_context(_sb(nc, "scs", [128, 512], F32))
            for i in range(4):
                p.dma(nmt[i][:], negm[i, :, :])
            p.bar()
            for h in range(16):
                load_head_w(c, p, wq8, wqkv, 64 * h, wst)
                load_head_w(c, p, wk8, wqkv, 1024 + 64 * h, wst)
                load_head_w(c, p, wv8, wqkv, 2048 + 64 * h, wst)
                p.dma(bcolq[0:64, 0:1], bqkv[64 * h:64 * h + 64, 0:1])
                p.dma(bcolk[0:64, 0:1], bqkv[1024 + 64 * h:1024 + 64 * h + 64, 0:1])
                p.dma(bvrow[0:1, 0:64], bqkvr[0:1, 2048 + 64 * h:2048 + 64 * h + 64])
                p.bar()
                evac(c, bvrow_bf[0:1, 0:64], bvrow[0:1, 0:64])
                p.bar()
                qkv_head(c, p, xnTb, wq8, bcolq[0:64, 0:1], 0.125, QT)
                qkv_head(c, p, xnTb, wk8, bcolk[0:64, 0:1], 1.0, KT)
                vproj_head(c, p, xnTb, wv8, bvrow_bf, Vb)

                def bias_fn(qc, kt, ps, att, probe=False):
                    cls = 4 * qc - kt
                    if probe:
                        return cls >= -3
                    if cls >= 1:
                        act(c, att[:], ps, AF.Exp)
                    else:
                        vec_tt(c, scs[:], ps, nmt[cls + 3][:], OP.add)
                        p.bar()
                        act(c, att[:], scs[:], AF.Exp)
                    return True
                attn_core(c, xnTb, ctxT, h, QT, KT, Vb, bias_fn, p)
            wob = alloc_tall(ph, nc, "wob", D, D, BF)
            borow = ph.enter_context(_sb(nc, "borow", [1, D], F32))
            borow_bf = ph.enter_context(_sb(nc, "borowb", [1, D], BF))
            cast_load(c, c.xtile, wob, wproj, D, D)
            p.dma(borow[0:1, :], bproj)
            p.bar()
            evac(c, borow_bf[0:1, :], borow[0:1, :])
            p.bar()
            out_proj(c, p, ctxT, wob, borow_bf, scr_s)

        with contextlib.ExitStack() as ph:
            Cb = alloc_tall(ph, nc, "Cb", D, D, BF)
            Sb_ = alloc_tall(ph, nc, "Sb_", D, D, BF)
            ATb = alloc_tall(ph, nc, "ATb", D, D, BF)
            BTb = alloc_tall(ph, nc, "BTb", D, D, BF)
            for i in range(8):
                p.dma(Cb.tile(i)[:], C1024[i * 128:(i + 1) * 128, :])
                p.dma(Sb_.tile(i)[:], S1024[i * 128:(i + 1) * 128, :])
            p.bar()
            with contextlib.ExitStack() as ph2:
                Wrb = alloc_tall(ph2, nc, "Wrb", D, D, BF)
                Wib = alloc_tall(ph2, nc, "Wib", D, D, BF)
                cast_load(c, c.xtile, Wrb, wfr, D, D)
                cast_load(c, c.xtile, Wib, wfi, D, D)
                for po in range(8):
                    for ch in range(2):
                        cs = slice(ch * 512, (ch + 1) * 512)
                        items = [(Wrb.tile(pi)[:, po * 128:(po + 1) * 128], Cb.tile(pi)[:, cs]) for pi in range(8)]
                        items += [(Wib.tile(pi)[:, po * 128:(po + 1) * 128], Sb_.tile(pi)[:, cs]) for pi in range(8)]
                        mm(c, c.PM[0][0:128, 0:512], items)
                        p.bar()
                        evac(c, ATb.tile(po)[:, cs], c.PM[0][0:128, 0:512])
                        p.bar()
                for i in range(8):
                    evac(c, Wib.tile(i)[:], Wib.tile(i)[:], scale=-1.0)
                p.bar()
                for po in range(8):
                    for ch in range(2):
                        cs = slice(ch * 512, (ch + 1) * 512)
                        items = [(Wrb.tile(pi)[:, po * 128:(po + 1) * 128], Sb_.tile(pi)[:, cs]) for pi in range(8)]
                        items += [(Wib.tile(pi)[:, po * 128:(po + 1) * 128], Cb.tile(pi)[:, cs]) for pi in range(8)]
                        mm(c, c.PM[0][0:128, 0:512], items)
                        p.bar()
                        evac(c, BTb.tile(po)[:, cs], c.PM[0][0:128, 0:512])
                        p.bar()
            Mb = alloc_tall(ph, nc, "Mb", D, D, BF)
            for po in range(8):
                for ch in range(2):
                    cs = slice(ch * 512, (ch + 1) * 512)
                    items = [(ATb.tile(pi)[:, po * 128:(po + 1) * 128], Cb.tile(pi)[:, cs]) for pi in range(8)]
                    items += [(BTb.tile(pi)[:, po * 128:(po + 1) * 128], Sb_.tile(pi)[:, cs]) for pi in range(8)]
                    mm(c, c.PM[0][0:128, 0:512], items)
                    p.bar()
                    evac(c, Mb.tile(po)[:, cs], c.PM[0][0:128, 0:512], scale=1.0 / 1024.0)
                    p.bar()
            fbb = ph.enter_context(_sb(nc, "fbb", [1, D], BF))
            rstage = ph.enter_context(_sb(nc, "rstg", [1, D], F32))
            p.dma(rstage[0:1, :], fbias)
            p.bar()
            evac(c, fbb[0:1, :], rstage[0:1, :])
            p.bar()
            for tt in range(NT):
                for ch in range(2):
                    cs = slice(ch * 512, (ch + 1) * 512)
                    items = [(xnTb.tile(pi)[:, tt * 128:(tt + 1) * 128], Mb.tile(pi)[:, cs]) for pi in range(8)]
                    items.append((c.ONES1B[0:1, 0:128], fbb[0:1, cs]))
                    mm(c, c.PM[ch][0:128, 0:512], items)
                p.bar()
                for ch in range(2):
                    evac(c, g.tile(tt)[:, ch * 512:(ch + 1) * 512], c.PM[ch][0:128, 0:512], func=AF.Silu)
                p.bar()

        update_phase(c, x_in, x_out, scr_s, g)
        if fx is None:
            p.finalize(sems)
    return nc


def prep_l3(inputs):
    i_ = np.arange(1024)
    ang = 2 * np.pi * np.outer(i_, i_) / 1024.0
    cls_ = np.arange(-3, 1)
    p_ = np.arange(128)
    f_ = np.arange(512)
    dmat = 128 * cls_[:, None, None] + f_[None, None, :] - p_[None, :, None]
    negm = np.where(dmat >= 0, 0.0, -1e30).astype(np.float32)
    b = np.asarray(inputs["L3_attn_b_qkv"], np.float32)
    d = {
        "w_qkv": np.asarray(inputs["L3_attn_w_qkv"], np.float32),
        "w_bqkv": b.reshape(3072, 1),
        "w_bqkvr": b.reshape(1, 3072),
        "w_proj": np.asarray(inputs["L3_attn_w_proj"], np.float32),
        "w_bproj": np.asarray(inputs["L3_attn_b_proj"], np.float32).reshape(1, D),
        "w_wfr": np.asarray(inputs["L3_ffn_w_freq_real"], np.float32),
        "w_wfi": np.asarray(inputs["L3_ffn_w_freq_imag"], np.float32),
        "w_fbias": np.asarray(inputs["L3_ffn_bias"], np.float32).reshape(1, D),
        "c_C1024": np.cos(ang).astype(BFNP),
        "c_S1024": np.sin(ang).astype(BFNP),
        "c_negm": negm,
    }
    return d


def kernel_unfused(**inputs):
    consts = host_consts()
    x = np.ascontiguousarray(np.asarray(inputs["L0_attn_Z"], np.float32))
    cur = [np.ascontiguousarray(x[b]) for b in range(B)]
    builders = [(build_l0, prep_l0), (build_l1, prep_l1), (build_l2, prep_l2), (build_l3, prep_l3)]
    for li, (bf, pf) in enumerate(builders):
        nc = bf()
        wmap = pf(inputs)
        wmap.update(consts)
        in_maps = [dict(wmap, xin=cur[b]) for b in range(B)]
        res = run_bass_kernel_spmd(nc, in_maps, core_ids=list(range(B)))
        cur = [np.ascontiguousarray(np.asarray(res.results[b]["xo"], np.float32)) for b in range(B)]
    return np.stack(cur, 0)


def build_fused():
    nc = bass.Bass("TRN2", target_bir_lowering=False)
    x0 = nc.dram_tensor("xin", [T, D], F32, kind="ExternalInput").ap()
    x4 = nc.dram_tensor("xo", [T, D], F32, kind="ExternalOutput").ap()
    x1 = nc.dram_tensor("ix1", [T, D], F32, kind="Internal").ap()
    x2 = nc.dram_tensor("ix2", [T, D], F32, kind="Internal").ap()
    x3 = nc.dram_tensor("ix3", [T, D], F32, kind="Internal").ap()
    with contextlib.ExitStack() as stk:
        p = Prog(nc)
        c = mk_ctx(stk, nc, p)
        sems = [stk.enter_context(nc.semaphore(f"sem{i}")) for i in range(4)]
        build_l0(fx=(nc, c, p, x0, x1))
        build_l1(fx=(nc, c, p, x1, x2))
        build_l2(fx=(nc, c, p, x2, x3))
        build_l3(fx=(nc, c, p, x3, x4))
        p.finalize(sems)
    return nc


def kernel(**inputs):
    consts = host_consts()
    x = np.ascontiguousarray(np.asarray(inputs["L0_attn_Z"], np.float32))
    nc = build_fused()
    wmap = dict(consts)
    for pf in (prep_l0, prep_l1, prep_l2, prep_l3):
        wmap.update(pf(inputs))
    in_maps = [dict(wmap, xin=np.ascontiguousarray(x[b])) for b in range(B)]
    res = run_bass_kernel_spmd(nc, in_maps, core_ids=list(range(B)))
    return np.stack([np.ascontiguousarray(np.asarray(res.results[b]["xo"], np.float32)) for b in range(B)], 0)
